# version 20
# baseline (speedup 1.0000x reference)
import os
import numpy as np
from contextlib import ExitStack
import concourse.bass as bass
import concourse.mybir as mybir
from concourse.bass_utils import run_bass_kernel_spmd

F32 = mybir.dt.float32
BF16 = mybir.dt.bfloat16
I32 = mybir.dt.int32
AF = mybir.ActivationFunctionType
ALU = mybir.AluOpType
AX = mybir.AxisListType
IOA = bass.IndirectOffsetOnAxis

T = 2048
D = 1024
NCH = D // 128
NTT = T // 512
NTC = T // 128
INNER = 2048
NQ, NKV = 512, 1024
EPS_LN = 1e-5

DEBUG = False


class KB:

    ENGS = ("tensor", "vector", "scalar", "gpsimd", "sync")

    def __init__(self):
        self.nc = bass.Bass("TRN2", target_bir_lowering=False)
        self.ctx = ExitStack()
        self.plan = {e: [] for e in self.ENGS}
        self.sems = {}
        self.semv = {}
        self.dbg = {}

    def sem(self, name):
        if name not in self.sems:
            self.sems[name] = self.ctx.enter_context(self.nc.semaphore(name))
            self.semv[name] = 0
        return self.sems[name]

    def sb(self, name, shape, dt=F32):
        return self.ctx.enter_context(self.nc.sbuf_tensor(name, list(shape), dt))

    def psum(self, name, shape, dt=F32):
        return self.ctx.enter_context(self.nc.psum_tensor(name, list(shape), dt))

    def op(self, eng, fn, wait=(), inc=None, incn=1):
        if inc is not None:
            self.sem(inc)
            self.semv[inc] += incn
            target = self.semv[inc]
        waits = tuple(wait)
        cell = {"extra": None}

        def closure(e, fn=fn, waits=waits, inc=inc, incn=incn, cell=cell):
            for s, v in waits:
                e.wait_ge(self.sems[s], v)
            ins = fn(e)
            if inc is not None:
                assert ins is not None
                ins.then_inc(self.sems[inc], incn)
            elif cell["extra"] is not None:
                assert ins is not None, "bar attached to non-instruction op"
                ins.then_inc(self.sems[cell["extra"][0]], cell["extra"][1])
        self.plan[eng].append(closure)
        if not hasattr(self, "_last"):
            self._last = {}
        self._last[eng] = (cell, inc is not None)
        if inc is not None:
            return (inc, target)
        return None

    def dma(self, eng, group, out, in_, wait=(), **kw):
        return self.op(eng, lambda e: e.dma_start(out=out, in_=in_, **kw),
                       wait=wait, inc="g_" + group, incn=16)

    def dmat(self, group, out, in_, wait=()):
        return self.op("sync", lambda e: e.dma_start_transpose(out, in_),
                       wait=wait, inc="g_" + group, incn=16)

    def gv(self, group):
        return ("g_" + group, self.semv.get("g_" + group, 0))

    def bar(self, producers, waiters=None):
        targets = []
        for eng in producers:
            bn = "bar_" + eng
            self.sem(bn)
            cell, has_inc = self._last.get(eng, (None, True))
            if cell is None or has_inc or cell["extra"] is not None:
                if eng == "vector":
                    def _fv(e):
                        e.drain()
                        return e.memset(self.dummy["vector"][:], 0.0)
                    self.op("vector", _fv, inc=bn)
                elif eng in ("scalar", "gpsimd"):
                    def _fs(e, eng=eng):
                        e.drain()
                        return e.memzero(self.dummy[eng][:])
                    self.op(eng, _fs, inc=bn)
                elif eng == "tensor":
                    self.op("tensor", self.tdummy, inc=bn)
                else:
                    raise AssertionError(f"bar: no op to carry inc on {eng}")
            else:
                self.semv[bn] += 1
                cell["extra"] = (bn, 1)
            targets.append((bn, self.semv[bn]))
        for eng in (waiters if waiters is not None else self.ENGS):
            for bn, v in targets:
                self.plan[eng].append(
                    lambda e, bn=bn, v=v: e.wait_ge(self.sems[bn], v))
        return targets

    def wv(self, semname):
        return (semname, self.semv.get(semname, 0))

    def mk(self, eng):
        return self.bar((eng,), waiters=())[0]

    def wt(self, eng, *ws):
        for w in ws:
            if w is None:
                continue
            self.plan[eng].append(
                lambda e, w=w: e.wait_ge(self.sem(w[0]), w[1]))

    def debug_out(self, name, shape, dt=F32):
        h = self.nc.dram_tensor("dbg_" + name, list(shape), dt, kind="ExternalOutput")
        self.dbg[name] = h.ap()
        return self.dbg[name]

    def finish(self):
        plan = self.plan
        with self.ctx:
            with self.nc.Block() as block:
                @block.tensor
                def _(e):
                    for fn in plan["tensor"]:
                        fn(e)

                @block.vector
                def _(e):
                    for fn in plan["vector"]:
                        fn(e)

                @block.scalar
                def _(e):
                    for fn in plan["scalar"]:
                        fn(e)

                @block.gpsimd
                def _(e):
                    for fn in plan["gpsimd"]:
                        fn(e)

                @block.sync
                def _(e):
                    for fn in plan["sync"]:
                        fn(e)
        return self.nc


def build():
    kb = KB()
    nc = kb.nc
    kb.dummy = {e: kb.sb("dummy_" + e, (1, 8), F32)
                for e in ("vector", "scalar", "gpsimd")}
    kb.dummyb = kb.sb("dummyb_sb", (128, 8), BF16)
    kb.tdummy = lambda e: e.ldweights(kb.dummyb[:, 0:8])
    dram = lambda n, shp, dt=F32, kind="ExternalInput": nc.dram_tensor(
        n, list(shp), dt, kind=kind).ap()

    x0_d = dram("x0", (D, T))
    out_d = dram("out", (D, T), kind="ExternalOutput")
    wqkv_d = dram("wqkv", (D, 1536))
    woutl_d = dram("woutl", (512, D))
    wq_d = dram("wq", (D, 512))
    wkv_d = dram("wkv", (D, 1024))
    wouth_d = dram("wouth", (512, D))
    win_d = dram("win", (D, 4352))
    wout_d = dram("wout", (INNER, D))
    lnw_d = dram("lnw_c", (128, 8))
    lnb_d = dram("lnb_c", (128, 8))
    mgam_d = dram("mgam_c", (128, 8))
    mbet_d = dram("mbet_c", (128, 8))
    hgam_d = dram("hgam_c", (128, 8))
    bin_d = dram("bin_c", (128, 34))
    bout_d = dram("bout_c", (128, 8))
    alog_d = dram("alog", (1, 8))
    dskip_d = dram("dskip", (1, 8))
    dtb_d = dram("dtbias", (1, 8))
    rt2_d = dram("rt2", (D, 2))
    nullcol_d = dram("nullcol_c", (128, 8))
    nullvrow_d = dram("nullvrow_in", (1, 512))
    nullqrep_d = dram("nullqrep_in", (128, D))
    ident_d = dram("ident", (128, 128))
    ones_d = dram("onesrow", (1, 128))
    e4_d = dram("e4", (128, 16))
    ind2_d = dram("ind2", (128, 2))
    ind2t_d = dram("ind2t", (2, 128))
    iota_t_d = dram("iota_t_in", (128, 16))
    maskA_l_d = dram("maskA_l_in", (2, 128))
    maskA_r_d = dram("maskA_r_in", (2, 256))
    maskT_r_d = dram("maskT_r_in", (2, 256))
    logk_d = dram("logk_in", (1, 2))
    padcolT_d = dram("padcolT_in", (128, 2))
    fconst_d = dram("fconst", (128, 4))
    iota2_d = dram("iota2_in", (128, 16))
    skinit_d = dram("skinit", (1, 2))
    jmat_d = dram("jmat", (128, 16))
    neg1e9_d = dram("neg1e9", (1, 256))
    edgeLR_d = dram("edgeLR", (2, 256))
    y_dram = dram("y_scr", (T, D), BF16, kind="Internal")
    rows_dram = dram("rows_scr", (16, T), F32, kind="Internal")
    idx_dram = dram("idx_scr", (1664, 4), F32, kind="Internal")
    hs_dram = dram("hs_scr", (T, D), BF16, kind="Internal")

    x_sb = kb.sb("x_sb", (128, NCH * T), F32)
    xc_sb = kb.sb("xc_sb", (128, NCH * T), BF16)
    repf = kb.sb("repf", (128, T), F32)
    repb = kb.sb("repb", (128, T), BF16)
    stat = kb.sb("stat", (4, 6 * 512), F32)
    SL_MU, SL_VAR, SL_INV, SL_S2, SL_T0, SL_T1 = range(6)
    cols = kb.sb("cols", (128, 96), F32)
    colsb = kb.sb("colsb", (128, 96), BF16)
    onescol_b = kb.sb("onescol_b", (128, 1), BF16)
    onescol_f = kb.sb("onescol_f", (128, 1), F32)
    e4_b = kb.sb("e4_b", (128, 16), BF16)
    identb = kb.sb("identb", (128, 128), BF16)
    identf = kb.sb("identf", (128, 128), F32)
    onesrow_b = kb.sb("onesrow_b", (1, 128), BF16)
    ind2_b = kb.sb("ind2_b", (128, 2), BF16)
    ind2t_b = kb.sb("ind2t_b", (2, 128), BF16)
    iota_t = kb.sb("iota_t", (128, 16), F32)
    maskA_l = kb.sb("maskA_l", (2, 128), BF16)
    maskA_r = kb.sb("maskA_r", (2, 256), BF16)
    maskT_r = kb.sb("maskT_r", (2, 256), BF16)
    logk_row = kb.sb("logk_row", (1, 2), F32)
    padcolT = kb.sb("padcolT", (128, 2), F32)
    fconst = kb.sb("fconst_sb", (128, 4), F32)
    nullq_bh = kb.sb("nullq_bh", (128, D), BF16)
    nullvrow_b = kb.sb("nullvrow_b", (1, 512), BF16)
    nullcol_b = kb.sb("nullcol_b", (128, 8), BF16)
    smallrows = kb.sb("smallrows", (1, 32), F32)
    rt2_sb = kb.sb("rt2_sb", (128, 16), F32)

    w_arena = kb.sb("w_arena", (128, 18432), BF16)
    A_bf = kb.sb("A_bf", (128, 18432), BF16)
    ab = lambda off, n: A_bf[:, off:off + n]
    xs_scr = lambda off, n: xc_sb[:, off:off + n]

    pA = kb.psum("pA", (128, 1536), F32)
    pb0 = kb.psum("pb0", (128, 512), F32)
    pb1 = kb.psum("pb1", (128, 512), F32)
    pb2 = kb.psum("pb2", (128, 512), F32)
    pb3 = kb.psum("pb3", (128, 512), F32)
    pT = kb.psum("pT", (128, 1024), BF16)

    xch = lambda c: x_sb[:, c * T:(c + 1) * T]
    xcch = lambda c: xc_sb[:, c * T:(c + 1) * T]
    statr = lambda s: stat[:, s * 512:(s + 1) * 512]
    stat_tile = lambda s, tt: stat[tt:tt + 1, s * 512:(s + 1) * 512]

    iota2 = kb.sb("iota2", (128, 16), F32)
    ptgt_i = kb.sb("ptgt_i", (128, 16), I32)
    gld_f = kb.sb("gld_f", (128, 48), F32)
    zrow = kb.sb("zrow", (128, 52), F32)
    onesrow_f = kb.sb("onesrow_f", (1, 128), F32)
    skinit_row = kb.sb("skinit_row", (1, 2), F32)
    jmat = kb.sb("jmat_sb", (128, 16), F32)
    zrow_r = zrow[:].rearrange("p (g k) -> p g k", k=4)

    nmlb_dram = dram("nmlb_scr", (2, 4096), BF16, kind="Internal")
    negm_seg = kb.sb("negm_seg", (128, 32), F32)
    s_seg = kb.sb("s_seg", (128, 32), F32)
    nmlT = kb.sb("nmlT", (32, 128), F32)
    nmlT_h = kb.sb("nmlT_h", (32, 128), BF16)
    nmlT_l = kb.sb("nmlT_l", (32, 128), BF16)
    zrow_b = kb.sb("zrow_b", (1, 512), BF16)
    neg1e9_row = kb.sb("neg1e9_row", (1, 256), BF16)
    bv_row = kb.sb("bv_row", (1, 512), BF16)
    u2row = kb.sb("u2row", (1, 128), BF16)
    edgeL_row = kb.sb("edgeL_row", (1, 256), BF16)
    edgeR_row = kb.sb("edgeR_row", (1, 256), BF16)
    bqkv_col = kb.sb("bqkv_col", (128, 12), F32)
    bvT = kb.sb("bvT", (4, 128), F32)
    bq_scaled = kb.sb("bq_scaled", (128, 4), F32)

    x0_r = x0_d.rearrange("(c p) t -> p c t", p=128)
    x_r = x_sb[:].rearrange("p (c t) -> p c t", c=NCH)
    for i in range(4):
        kb.dma("sync", "gx", x_r[:, 2 * i:2 * i + 2, :], x0_r[:, 2 * i:2 * i + 2, :])
    ld_x = kb.gv("gx")

    kb.dma("sync", "gc", identf[:], ident_d)
    kb.dma("sync", "gc", iota_t[:], iota_t_d)
    kb.dma("sync", "gc", logk_row[:], logk_d)
    kb.dma("sync", "gc", padcolT[:], padcolT_d)
    kb.dma("sync", "gc", fconst[:], fconst_d)
    kb.dma("sync", "gc", cols[:, 0:8], lnw_d)
    kb.dma("sync", "gc", cols[:, 8:16], lnb_d)
    kb.dma("sync", "gc", cols[:, 16:24], mgam_d)
    kb.dma("sync", "gc", cols[:, 24:32], mbet_d)
    kb.dma("sync", "gc", cols[:, 32:40], hgam_d)
    kb.dma("sync", "gc", cols[:, 40:74], bin_d)
    kb.dma("sync", "gc", cols[:, 74:82], bout_d)
    kb.dma("sync", "gc", smallrows[0:1, 0:8], alog_d)
    kb.dma("sync", "gc", smallrows[0:1, 8:16], dskip_d)
    kb.dma("sync", "gc", smallrows[0:1, 16:24], dtb_d)
    kb.dma("sync", "gc", onescol_f[:], ones_d.rearrange("r p -> p r"))
    kb.dma("sync", "gc", onesrow_f[:], ones_d)
    kb.dma("sync", "gc", iota2[:], iota2_d)
    kb.dma("sync", "gc", jmat[:], jmat_d)
    kb.dma("sync", "gc", skinit_row[:], skinit_d)
    kb.dma("sync", "gc", rt2_sb[:].rearrange("p (c r) -> p c r", c=8),
           rt2_d.rearrange("(c p) r -> p c r", p=128))
    ld_const = kb.gv("gc")

    for dst, src in ((identb[:], ident_d), (onescol_b[:], ones_d.rearrange("r p -> p r")),
                     (e4_b[:], e4_d), (onesrow_b[:], ones_d), (ind2_b[:], ind2_d),
                     (ind2t_b[:], ind2t_d), (maskA_l[:], maskA_l_d),
                     (maskA_r[:], maskA_r_d), (maskT_r[:], maskT_r_d),
                     (neg1e9_row[:], neg1e9_d), (u2row[:], maskA_l_d[1:2, :]),
                     (edgeL_row[:], edgeLR_d[0:1, :]), (edgeR_row[:], edgeLR_d[1:2, :]),
                     (nullvrow_b[:], nullvrow_d), (nullcol_b[:], nullcol_d)):
        kb.dma("gpsimd", "gcb", dst, src)
    ld_cb = kb.gv("gcb")

    WQKV_O, WOUTL_O = 0, 12288
    WQ_O, WKV_O, WOUTH_O = 0, 4096, 12288
    WINZ_O, WINX_O, WOUT_O = 0, 0, 0

    def load_w(grp, off, src_d, nk, f, wait=()):
        kb.dma("gpsimd", grp,
               w_arena[:, off:off + nk * f].rearrange("p (c f) -> p c f", c=nk),
               src_d.rearrange("(c p) f -> p c f", p=128), wait=wait)
        return kb.gv(grp)

    load_w("gw1", WQKV_O, wqkv_d, 8, 1536)
    ld_w1 = load_w("gw1", WOUTL_O, woutl_d, 4, 1024)
    wslc = lambda off, kc, fdim, ncols=128, col0=0: w_arena[
        :, off + kc * fdim + col0: off + kc * fdim + col0 + ncols]

    kb.dma("gpsimd", "gnq", nullq_bh[:], nullqrep_d)
    ld_cb2 = kb.gv("gnq")
    kb.op("scalar", lambda e: e.activation(out=nullq_bh[:], in_=nullq_bh[:],
                                           func=AF.Identity, scale=0.5),
          wait=[ld_cb2])
    kb.bar(("scalar",))

    kb.op("vector", lambda e: e.memset(zrow[:], 0.0), wait=[ld_const])
    kb.bar(("vector",))

    sqb0 = ab(0, 2048)
    sqb1 = ab(2048, 2048)

    def boundary_stats():
        for c in range(NCH):
            kb.op("vector", lambda e, c=c: e.tensor_copy(out=xcch(c), in_=xch(c)))
        kb.bar(("vector",))
        for tt in range(NTT):
            for c in range(NCH):
                kb.op("tensor", lambda e, tt=tt, c=c: e.matmul(
                    out=pb0[0:4, :], lhsT=e4_b[:, 4 * tt:4 * tt + 4],
                    rhs=xcch(c)[:, tt * 512:(tt + 1) * 512],
                    start=(tt == 0 and c == 0), stop=(tt == 3 and c == 7)))
        kb.bar(("tensor",))
        kb.op("scalar", lambda e: e.activation(out=statr(SL_MU), in_=pb0[0:4, :],
                                               func=AF.Identity, scale=1.0 / D))
        kb.bar(("scalar",))
        kb.op("vector", lambda e: e.tensor_scalar_mul(out=statr(SL_T0),
                                                      in0=statr(SL_MU), scalar1=-1.0))
        kb.bar(("vector",))
        kb.dma("sync", "gsmu", rows_dram[0:1, :].rearrange("r (p j) -> (r p) j", p=4),
               statr(SL_T0))
        w1 = kb.gv("gsmu")
        kb.dma("gpsimd", "gbmu", repf[:], rows_dram[0:1, :].to_broadcast((128, T)),
               wait=[w1])
        w2 = kb.gv("gbmu")
        for c in range(NCH):
            kb.op("vector", lambda e, c=c: e.tensor_tensor(
                out=xcch(c), in0=xch(c), in1=repf[:], op=ALU.add),
                wait=[w2] if c == 0 else ())
        kb.bar(("vector",))
        sq = (sqb0, sqb1)
        for c in range(NCH):
            buf = sq[c % 2]
            w_mm = (("sqmm", kb.semv.get("sqmm", 0)),) if c >= 2 else ()
            wv = kb.op("scalar", lambda e, buf=buf, c=c: e.activation(
                out=buf, in_=xcch(c), func=AF.Square), wait=w_mm, inc="sq")
            for tt in range(NTT):
                kb.op("tensor", lambda e, buf=buf, tt=tt, c=c: e.matmul(
                    out=pb1[0:4, :], lhsT=e4_b[:, 4 * tt:4 * tt + 4],
                    rhs=buf[:, tt * 512:(tt + 1) * 512],
                    start=(c == 0 and tt == 0), stop=(c == 7 and tt == 3)),
                    wait=[wv] if tt == 0 else (),
                    inc="sqmm" if tt == 3 else None)
        kb.bar(("tensor",))
        kb.op("scalar", lambda e: e.activation(out=statr(SL_VAR), in_=pb1[0:4, :],
                                               func=AF.Identity, scale=1.0 / D))
        kb.op("scalar", lambda e: e.drain())
        kb.op("scalar", lambda e: e.activation(out=statr(SL_T1), in_=statr(SL_VAR),
                                               func=AF.Ln, bias=fconst[0:4, 0:1]))
        kb.op("scalar", lambda e: e.drain())
        kb.op("scalar", lambda e: e.activation(out=statr(SL_INV), in_=statr(SL_T1),
                                               func=AF.Exp, scale=-0.5))
        kb.bar(("scalar",))

    def bcast_row(slot, dram_slot, dest, grp, wait_extra=()):
        kb.dma("sync", "gs" + grp, rows_dram[dram_slot:dram_slot + 1, :].rearrange(
            "r (p j) -> (r p) j", p=4), statr(slot), wait=wait_extra)
        w1 = kb.gv("gs" + grp)
        kb.dma("gpsimd", "gb" + grp, dest,
               rows_dram[dram_slot:dram_slot + 1, :].to_broadcast((128, T)), wait=[w1])
        return kb.gv("gb" + grp)

    yf_dram = dram("yf_scr", (D, T), BF16, kind="Internal")
    ytm0 = ab(16384, 1024)
    ytm1 = ab(17408, 1024)
    inv_tm = kb.sb("inv_tm", (128, 16), F32)
    ulch = lambda c: ab(c * T, T)

    def ffn_ustage(call):
        tg = f"f{call}"
        kb.op("vector", lambda e: e.tensor_tensor(out=statr(SL_T0), in0=statr(SL_INV),
                                                  in1=statr(SL_INV), op=ALU.mult))
        kb.op("vector", lambda e: e.drain())
        kb.op("vector", lambda e: e.tensor_tensor(out=statr(SL_T0), in0=statr(SL_T0),
                                                  in1=statr(SL_VAR), op=ALU.mult))
        kb.bar(("vector",))
        kb.op("scalar", lambda e: e.activation(out=statr(SL_T1), in_=statr(SL_T0),
                                               func=AF.Ln, bias=fconst[0:4, 0:1]))
        kb.op("scalar", lambda e: e.drain())
        kb.op("scalar", lambda e: e.activation(out=statr(SL_T1), in_=statr(SL_T1),
                                               func=AF.Exp, scale=-0.5))
        kb.op("scalar", lambda e: e.drain())
        kb.bar(("scalar",))
        kb.op("vector", lambda e: e.tensor_tensor(out=statr(SL_S2), in0=statr(SL_INV),
                                                  in1=statr(SL_T1), op=ALU.mult))
        kb.bar(("vector",))
        w_rep = bcast_row(SL_S2, 2, repb[:], "sl")
        for j in range(4):
            kb.op("tensor", lambda e, j=j: e.transpose(
                out=pb2[0:128, 0:16].rearrange("p (a b) -> p a b", b=4)[:, :, j],
                in_=stat[0:4, SL_INV * 512 + 128 * j:SL_INV * 512 + 128 * (j + 1)],
                identity=identf[0:4, 0:4]))
        kb.bar(("tensor",))
        kb.op("scalar", lambda e: e.copy(out=inv_tm[:], in_=pb2[0:128, 0:16]),
              inc="invtm")
        w_invtm = kb.wv("invtm")
        for c in range(NCH):
            kb.op("vector", lambda e, c=c: e.scalar_tensor_tensor(
                out=ulch(c), in0=xcch(c), scalar=cols[:, c:c + 1],
                in1=repb[:], op0=ALU.mult, op1=ALU.mult),
                wait=[w_rep] if c == 0 else ())
        kb.bar(("vector",))
        xc_r = xc_sb[:].rearrange("p (c t) -> p c t", c=NCH)
        yf_r = yf_dram.rearrange("(c p) t -> p c t", p=128)
        for i in range(2):
            kb.dma("sync", "gyf", yf_r[:, 4 * i:4 * i + 4, :], xc_r[:, 4 * i:4 * i + 4, :])
        w_yf = kb.gv("gyf")
        ybufs = (ytm0, ytm1)
        for tc in range(NTC):
            par = tc % 2
            buf = ybufs[par]
            wt = [w_yf] if tc < 2 else [kb.gv("gyt%d" % par)]
            kb.dmat("gytt%d" % par, buf, yf_dram[:, tc * 128:(tc + 1) * 128],
                    wait=wt)
            w_t = kb.gv("gytt%d" % par)
            kb.op("vector", lambda e, buf=buf, tc=tc: e.tensor_scalar_mul(
                out=buf, in0=buf, scalar1=inv_tm[:, tc:tc + 1]),
                wait=[w_t, w_invtm] if tc < 2 else [w_t], inc="yts")
            mYs = kb.mk("vector")
            kb.plan["sync"].append(lambda e, w=mYs: e.wait_ge(
                kb.sem(w[0]), w[1]))
            kb.dma("sync", "gyt%d" % par, y_dram[tc * 128:(tc + 1) * 128, :], buf)
        return [kb.gv("gyt0"), kb.gv("gyt1")]

    srt_q = kb.sb("srt_q", (128, 16), F32)
    srt_kv = kb.sb("srt_kv", (128, 16), F32)
    sk_sc = kb.sb("sk_sc", (128, 16 * 8), F32)
    skr = lambda i: sk_sc[:, 16 * i:16 * (i + 1)]
    sk_cols = kb.sb("sk_cols", (128, 8), F32)
    sk_small = kb.sb("sk_small", (2, 16), F32)
    sk_row = kb.sb("sk_row", (1, 128), F32)
    sk_row2 = kb.sb("sk_row2", (1, 128), F32)
    msr_col = kb.sb("msr_col", (2, 1), F32)
    scatv = kb.sb("scatv", (128, 64), F32)
    gidx = kb.sb("gidx", (128, 12), I32)
    gmeta = kb.sb("gmeta", (128, 36), F32)

    def route_ms(ridx, s_tm):
        o = 7 * ridx
        c0 = 0 if ridx == 0 else 7
        kb.op("vector", lambda e, c0=c0: e.tensor_reduce(
            out=sk_cols[:, c0:c0 + 1], in_=s_tm[:], axis=AX.X, op=ALU.max))
        mv = kb.mk("vector")
        kb.op("tensor", lambda e, c0=c0: e.transpose(
            out=pb3[0:1, 0:128], in_=sk_cols[:, c0:c0 + 1], identity=identf[:]),
            wait=[mv])
        mt = kb.mk("tensor")
        kb.op("vector", lambda e, o=o: e.tensor_reduce(
            out=sk_small[0:1, o + 6:o + 7], in_=pb3[0:1, 0:128], axis=AX.X,
            op=ALU.max), wait=[mt])
        kb.op("vector", lambda e: e.drain())
        kb.op("vector", lambda e, o=o, ridx=ridx: e.tensor_copy(
            out=sk_small[0:1, o:o + 1],
            in_=skinit_row[0:1, ridx:ridx + 1]))
        kb.op("vector", lambda e: e.drain())

    def route_iters():
        pT2 = None
        pVc = None
        pS1 = None
        pS2 = None
        for it in range(49):
            for o in (0, 7):
                kb.op("vector", lambda e, o=o: e.tensor_scalar_mul(
                    out=sk_small[0:1, o + 1:o + 2], in0=sk_small[0:1, o:o + 1],
                    scalar1=-1.0), wait=[pS2] if pS2 and o == 0 else ())
            kb.op("vector", lambda e: e.drain())
            for o in (0, 7):
                kb.op("vector", lambda e, o=o: e.tensor_tensor(
                    out=sk_small[0:1, o + 3:o + 4],
                    in0=sk_small[0:1, o + 6:o + 7],
                    in1=sk_small[0:1, o + 1:o + 2], op=ALU.min))
            kb.op("vector", lambda e: e.drain())
            for o in (0, 7):
                kb.op("vector", lambda e, o=o: e.tensor_scalar_mul(
                    out=sk_small[0:1, o + 2:o + 3], in0=sk_small[0:1, o + 3:o + 4],
                    scalar1=-1.0))
            mV1 = kb.mk("vector")
            for r, o in ((0, 0), (1, 7)):
                kb.op("tensor", lambda e, r=r, o=o: e.matmul(
                    out=pb3[0:128, 4 * r:4 * r + 2], lhsT=onesrow_f[0:1, :],
                    rhs=sk_small[0:1, o + 1:o + 3], start=True, stop=True),
                    wait=([mV1] if r == 0 else []) +
                    ([pVc] if pVc and r == 0 else []))
            mT1 = kb.mk("tensor")
            for r in (0, 1):
                kb.op("vector", lambda e, r=r: e.tensor_copy(
                    out=sk_cols[:, 1 + 3 * r:3 + 3 * r],
                    in_=pb3[0:128, 4 * r:4 * r + 2]),
                    wait=([mT1] if r == 0 else []) +
                    ([pS1] if pS1 and r == 0 else []))
            kb.op("vector", lambda e: e.drain())
            pVc = kb.mk("vector")
            for r, s_tm in ((0, srt_q), (1, srt_kv)):
                kb.op("vector", lambda e, r=r, s_tm=s_tm: e.tensor_scalar(
                    out=skr(3 + r), in0=s_tm[:],
                    scalar1=sk_cols[:, 1 + 3 * r:2 + 3 * r], scalar2=None,
                    op0=ALU.min))
            mV2 = kb.mk("vector")
            for r in (0, 1):
                kb.op("scalar", lambda e, r=r: e.activation(
                    out=skr(3 + r), in_=skr(3 + r), func=AF.Exp,
                    bias=sk_cols[:, 2 + 3 * r:3 + 3 * r],
                    accum_out=sk_cols[:, 3 + 3 * r:4 + 3 * r]),
                    wait=([mV2] if r == 0 else []) +
                    ([pT2] if pT2 and r == 0 else []))
                kb.op("scalar", lambda e: e.drain())
            mS1 = kb.mk("scalar")
            pS1 = mS1
            for r in (0, 1):
                kb.op("tensor", lambda e, r=r: e.matmul(
                    out=pb3[0:1, 130 + r:131 + r],
                    lhsT=sk_cols[:, 3 + 3 * r:4 + 3 * r], rhs=onescol_f[:],
                    start=True, stop=True),
                    wait=([mS1] if r == 0 else []) +
                    ([pS2] if pS2 and r == 0 else []))
            pT2 = kb.mk("tensor")
            for r, o in ((0, 0), (1, 7)):
                kb.op("scalar", lambda e, r=r, o=o: e.activation(
                    out=sk_small[0:1, o + 4:o + 5],
                    in_=pb3[0:1, 130 + r:131 + r], func=AF.Ln),
                    wait=[pT2] if r == 0 else ())
            kb.op("scalar", lambda e: e.drain())
            pS2 = kb.mk("scalar")
            for r, o in ((0, 0), (1, 7)):
                kb.op("vector", lambda e, r=r, o=o: e.tensor_tensor(
                    out=sk_small[0:1, o + 5:o + 6],
                    in0=logk_row[0:1, r:r + 1],
                    in1=sk_small[0:1, o + 2:o + 3], op=ALU.add),
                    wait=[pS2] if r == 0 else ())
            kb.op("vector", lambda e: e.drain())
            for o in (0, 7):
                kb.op("vector", lambda e, o=o: e.tensor_tensor(
                    out=sk_small[0:1, o:o + 1], in0=sk_small[0:1, o + 5:o + 6],
                    in1=sk_small[0:1, o + 4:o + 5], op=ALU.subtract))
            kb.op("vector", lambda e: e.drain())
        kb.bar(("vector", "scalar", "tensor"))

    def route_one(call, ridx, s_tm, kk, base, trash):
        tg = "r%d_%d" % (call, ridx)
        U = 3; EE = 3; ONES = 4; PFX = 5; CC = 6; TSC = 7; PT = 6; D_ = 0
        kb.op("tensor", lambda e, ridx=ridx: e.matmul(
            out=pb3[0:128, 2:3], lhsT=onesrow_f[0:1, :],
            rhs=sk_small[0:1, 7 * ridx:7 * ridx + 1], start=True, stop=True))
        kb.bar(("tensor",))
        kb.op("vector", lambda e: e.tensor_copy(out=sk_cols[:, 4:5],
                                                in_=pb3[0:128, 2:3]))
        kb.op("vector", lambda e: e.drain())
        kb.op("vector", lambda e: e.tensor_scalar(
            out=skr(U), in0=s_tm[:], scalar1=sk_cols[:, 4:5], scalar2=None,
            op0=ALU.add))
        kb.op("vector", lambda e: e.drain())
        kb.op("vector", lambda e: e.tensor_single_scalar(
            out=skr(ONES), in_=skr(U), scalar=0.0, op=ALU.is_ge))
        kb.op("vector", lambda e: e.tensor_scalar_min(
            out=skr(CC), in0=skr(U), scalar1=0.0))
        kb.bar(("vector",))
        kb.op("scalar", lambda e: e.activation(
            out=skr(CC), in_=skr(CC), func=AF.Exp))
        kb.bar(("scalar",))
        kb.op("vector", lambda e: e.tensor_scalar(
            out=skr(TSC), in0=skr(CC), scalar1=-1.0, scalar2=1.0,
            op0=ALU.mult, op1=ALU.add))
        kb.op("vector", lambda e: e.drain())
        kb.op("vector", lambda e: e.tensor_tensor(
            out=skr(TSC), in0=skr(TSC), in1=skr(CC), op=ALU.add))
        kb.op("vector", lambda e: e.tensor_tensor_scan(
            out=skr(PFX), data0=skr(ONES), data1=skr(ONES), initial=0.0,
            op0=ALU.add, op1=ALU.bypass))
        kb.bar(("vector",))
        kb.op("tensor", lambda e: e.transpose(
            out=pb3[0:1, 128:256], in_=skr(PFX)[:, 15:16], identity=identf[:]))
        kb.bar(("tensor",))
        kb.op("vector", lambda e: e.tensor_copy(out=sk_row2[0:1, :],
                                                in_=pb3[0:1, 128:256]))
        kb.op("vector", lambda e: e.drain())
        kb.op("vector", lambda e: e.tensor_tensor_scan(
            out=sk_row[0:1, :], data0=sk_row2[0:1, :], data1=sk_row2[0:1, :],
            initial=0.0, op0=ALU.add, op1=ALU.bypass))
        kb.bar(("vector",))
        kb.op("tensor", lambda e: e.transpose(
            out=pb3[0:128, 3:4], in_=sk_row[0:1, :], identity=identf[0:1, 0:1]))
        kb.bar(("tensor",))
        kb.op("vector", lambda e: e.tensor_tensor(
            out=sk_cols[:, 5:6], in0=pb3[0:128, 3:4], in1=skr(PFX)[:, 15:16],
            op=ALU.subtract))
        kb.op("vector", lambda e: e.drain())
        kb.op("vector", lambda e: e.tensor_scalar(
            out=skr(PFX), in0=skr(PFX), scalar1=sk_cols[:, 5:6], scalar2=None,
            op0=ALU.add))
        kb.op("vector", lambda e: e.drain())
        kb.op("vector", lambda e: e.tensor_single_scalar(
            out=skr(D_), in_=skr(PFX), scalar=float(kk), op=ALU.is_le))
        kb.op("vector", lambda e: e.drain())
        kb.op("vector", lambda e: e.tensor_tensor(
            out=skr(ONES), in0=skr(ONES), in1=skr(D_), op=ALU.mult))
        kb.op("vector", lambda e: e.drain())
        kb.op("vector", lambda e: e.tensor_scalar(
            out=skr(D_), in0=skr(PFX), scalar1=1.0 - float(base) + float(trash),
            scalar2=None, op0=ALU.subtract))
        kb.op("vector", lambda e: e.drain())
        kb.op("vector", lambda e: e.tensor_tensor(
            out=skr(D_), in0=skr(D_), in1=jmat[:], op=ALU.subtract))
        kb.op("vector", lambda e: e.drain())
        kb.op("vector", lambda e: e.tensor_tensor(
            out=skr(D_), in0=skr(D_), in1=skr(ONES), op=ALU.mult))
        kb.op("vector", lambda e: e.drain())
        kb.op("vector", lambda e: e.tensor_scalar(
            out=skr(PT), in0=skr(D_), scalar1=float(trash), scalar2=None,
            op0=ALU.add))
        kb.op("vector", lambda e: e.drain())
        kb.op("vector", lambda e: e.tensor_tensor(
            out=skr(PT), in0=skr(PT), in1=jmat[:], op=ALU.add))
        kb.op("vector", lambda e: e.drain())
        return TSC, PT

    def route_scatter(call, ridx, TSC, PT, wait_prev=()):
        tg = "sc%d_%d" % (call, ridx)
        kb.op("vector", lambda e: e.tensor_copy(out=ptgt_i[:], in_=skr(PT)),
              wait=wait_prev)
        sv = scatv[:].rearrange("p (j k) -> p j k", k=4)
        kb.op("vector", lambda e: e.tensor_copy(out=sv[:, :, 0], in_=iota2[:]))
        kb.op("vector", lambda e: e.tensor_copy(out=sv[:, :, 1], in_=skr(TSC)))
        kb.op("vector", lambda e: e.tensor_copy(out=sv[:, :, 2], in_=skr(2)))
        kb.op("vector", lambda e: e.tensor_copy(out=sv[:, :, 3], in_=skr(2)))
        kb.bar(("vector",))
        for j in range(16):
            kb.op("gpsimd", lambda e, j=j: e.indirect_dma_start(
                out=idx_dram, out_offset=IOA(ptgt_i[:, j:j + 1], 0),
                in_=scatv[:, 4 * j:4 * j + 4], in_offset=None),
                inc="g_gsc" + tg, incn=16)
        return kb.gv("gsc" + tg)

    def ffn_route(call):
        tg = "rt%d" % call
        kb.plan["tensor"].append(
            lambda e, w=kb.wv("invtm"): e.wait_ge(kb.sem(w[0]), w[1]))
        for tt in range(NTT):
            bp = 0 if tt % 2 == 0 else 32
            for c in range(NCH):
                kb.op("tensor", lambda e, tt=tt, c=c, bp=bp: e.matmul(
                    out=pb2[bp:bp + 2, 0:512],
                    lhsT=rt2_sb[:, 2 * c:2 * c + 2],
                    rhs=xch(c)[:, tt * 512:(tt + 1) * 512],
                    start=(c == 0), stop=(c == 7)))
            kb.bar(("tensor",))
            slot = (SL_T0, SL_T1, SL_S2, SL_VAR)[tt]
            kb.op("scalar", lambda e, slot=slot, bp=bp: e.copy(
                out=stat[0:2, slot * 512:slot * 512 + 512],
                in_=pb2[bp:bp + 2, 0:512]))
            kb.bar(("scalar",))
        kb.bar(("scalar",))
        for tt in range(NTT):
            slot = (SL_T0, SL_T1, SL_S2, SL_VAR)[tt]
            for r in range(2):
                kb.dma("sync", "gsr" + tg,
                       rows_dram[3 + r:4 + r, tt * 512:(tt + 1) * 512],
                       stat[r:r + 1, slot * 512:slot * 512 + 512])
        w_sr = kb.gv("gsr" + tg)
        for c in range(NCH):
            kb.op("tensor", lambda e, c=c: e.matmul(
                out=pb3[0:1, 256:258], lhsT=onescol_f[:],
                rhs=rt2_sb[:, 2 * c:2 * c + 2],
                start=(c == 0), stop=(c == 7)))
        kb.bar(("tensor",))
        kb.op("vector", lambda e: e.tensor_copy(out=sk_small[0:1, 14:16],
                                                in_=pb3[0:1, 256:258]))
        kb.op("vector", lambda e: e.drain())
        kb.op("vector", lambda e: e.tensor_tensor(
            out=statr(SL_T0), in0=statr(SL_MU), in1=statr(SL_INV), op=ALU.mult),
            wait=[w_sr])
        kb.op("vector", lambda e: e.drain())
        kb.op("vector", lambda e: e.tensor_scalar_mul(
            out=statr(SL_T0), in0=statr(SL_T0), scalar1=-1.0))
        kb.bar(("vector",))
        kb.dma("sync", "gsh" + tg,
               rows_dram[9:10, :].rearrange("r (p j) -> (r p) j", p=4),
               statr(SL_T0))
        w_sh = kb.gv("gsh" + tg)
        def ldr(row, dst, w):
            kb.dma("gpsimd", "gld" + tg, dst,
                   rows_dram[row:row + 1, :].rearrange("r (p j) -> p (r j)", p=128),
                   wait=w)
        kb.dma("sync", "gsi" + tg,
               rows_dram[1:2, :].rearrange("r (p j) -> (r p) j", p=4),
               statr(SL_INV))
        w_si = kb.gv("gsi" + tg)
        ldr(3, srt_q[:], [w_sr])
        ldr(4, srt_kv[:], [w_sr])
        ldr(0, skr(0), [])
        ldr(1, skr(1), [w_si])
        ldr(9, skr(2), [w_sh])
        w_ld = kb.gv("gld" + tg)
        kb.plan["vector"].append(
            lambda e, w=w_ld: e.wait_ge(kb.sem(w[0]), w[1]))
        for ridx, s_tm in ((0, srt_q), (1, srt_kv)):
            kb.bar(("vector",))
            kb.op("tensor", lambda e, ridx=ridx: e.matmul(
                out=pb3[0:128, 4:5], lhsT=onesrow_f[0:1, :],
                rhs=sk_small[0:1, 14 + ridx:15 + ridx], start=True, stop=True))
            kb.bar(("tensor",))
            kb.op("vector", lambda e: e.tensor_copy(out=sk_cols[:, 6:7],
                                                    in_=pb3[0:128, 4:5]))
            kb.op("vector", lambda e: e.drain())
            kb.op("vector", lambda e: e.tensor_scalar(
                out=skr(3), in0=skr(0), scalar1=sk_cols[:, 6:7], scalar2=None,
                op0=ALU.mult))
            kb.op("vector", lambda e: e.drain())
            kb.op("vector", lambda e, s_tm=s_tm: e.tensor_tensor(
                out=s_tm[:], in0=s_tm[:], in1=skr(3), op=ALU.add))
            kb.op("vector", lambda e: e.drain())
            kb.op("vector", lambda e, s_tm=s_tm: e.tensor_tensor(
                out=s_tm[:], in0=s_tm[:], in1=skr(1), op=ALU.mult))
            kb.op("vector", lambda e: e.drain())
        kb.dma("sync", "gzi" + tg,
               idx_dram.rearrange("(p a) k -> p (a k)", p=128),
               zrow[:, 0:52])
        w_zi = kb.gv("gzi" + tg)
        kb.plan["gpsimd"].append(
            lambda e, w=w_zi: e.wait_ge(kb.sem(w[0]), w[1]))
        route_ms(0, srt_q)
        route_ms(1, srt_kv)
        route_iters()
        TSCq, PTq = route_one(call, 0, srt_q, 512, 0, 512)
        wsq = route_scatter(call, 0, TSCq, PTq, [w_zi])
        TSCk, PTk = route_one(call, 1, srt_kv, 1024, 528, 1552)
        if DEBUG and call == 1:
            dbg_pt = kb.debug_out("kv_pt", (128, 16))
            dbg_pf = kb.debug_out("kv_pfx", (128, 16))
            dbg_on = kb.debug_out("kv_ones", (128, 16))
            dbg_s = kb.debug_out("kv_s", (128, 16))
            kb.bar(("vector",))
            kb.dma("sync", "gdbgr", dbg_pt, skr(6))
            kb.dma("sync", "gdbgr", dbg_pf, skr(5))
            kb.dma("sync", "gdbgr", dbg_on, skr(4))
            kb.dma("sync", "gdbgr", dbg_s, srt_kv[:])
            dbg_rf = kb.debug_out("kv_rf", (128, 16))
            kb.dma("sync", "gdbgr", dbg_rf, skr(3))
        wsk = route_scatter(call, 1, TSCk, PTk, [wsq])
        def ldidx(row0, ntile, base):
            for g in range(ntile):
                kb.dma("sync", "gli" + tg,
                       gld_f[:, 4 * (base + g):4 * (base + g) + 4],
                       idx_dram[row0 + 128 * g: row0 + 128 * (g + 1), :],
                       wait=[wsq, wsk])
        ldidx(0, 4, 0)
        ldidx(528, 8, 4)
        w_li = kb.gv("gli" + tg)
        kb.plan["vector"].append(
            lambda e, w=w_li: e.wait_ge(kb.sem(w[0]), w[1]))
        for g in range(12):
            kb.op("vector", lambda e, g=g: e.tensor_copy(
                out=gidx[:, g:g + 1], in_=gld_f[:, 4 * g:4 * g + 1]))
            kb.op("vector", lambda e, g=g: e.tensor_copy(
                out=gmeta[:, 2 * g:2 * g + 2], in_=gld_f[:, 4 * g + 1:4 * g + 3]))
        kb.bar(("vector",))


    def setup_bias_qkv():
        kb.op("vector", lambda e: e.tensor_copy(out=colsb[:, 0:82],
                                                in_=cols[:, 0:82]),
              wait=[ld_const])
        kb.bar(("vector",))
        for m in range(12):
            for c in range(NCH):
                kb.op("tensor", lambda e, m=m, c=c: e.matmul(
                    out=pb0[0:128, m:m + 1],
                    lhsT=wslc(WQKV_O, c, 1536, 128, 128 * m),
                    rhs=colsb[:, 8 + c:9 + c],
                    start=(c == 0), stop=(c == 7)), wait=[ld_w1] if m == 0 and c == 0 else ())
        kb.bar(("tensor",))
        kb.op("scalar", lambda e: e.copy(out=bqkv_col[:], in_=pb0[0:128, 0:12]))
        kb.bar(("scalar",))
        kb.op("vector", lambda e: e.tensor_scalar_mul(
            out=bq_scaled[:], in0=bqkv_col[:, 0:4], scalar1=0.125))
        kb.bar(("vector",))
        kb.op("tensor", lambda e: e.transpose(
            out=pb1[0:4, 0:128], in_=bqkv_col[:, 8:12], identity=identf[:]))
        kb.bar(("tensor",))
        kb.op("scalar", lambda e: e.copy(out=bvT[:], in_=pb1[0:4, 0:128]))
        kb.bar(("scalar",))
        kb.dma("sync", "gbv", rows_dram[10:11, 0:512].rearrange(
            "r (m p) -> (r m) p", m=4), bvT[:])
        wbv = kb.gv("gbv")
        kb.dma("gpsimd", "gbv2", bv_row[:], rows_dram[10:11, 0:512], wait=[wbv])
        wbv2 = kb.gv("gbv2")
        kb.plan["tensor"].append(lambda e, w=wbv2: e.wait_ge(kb.sem(w[0]), w[1]))

    def ffn_light(call):
        tg = "lt%d" % call
        QSEG, KSEG, VSEG, OFM = 0, 4096, 9216, 11776
        qseg = xc_sb[0:66, QSEG:QSEG + 4096]
        kseg = xc_sb[0:66, KSEG:KSEG + 5120]
        vseg = xc_sb[:, VSEG:VSEG + 2560]
        vseg_hi = xc_sb[0:64, 13824:16384]
        ofm = xc_sb[:, OFM:OFM + 2048]
        PT_O, ADIS = 16384, 16896
        obank = [pA[:, 0:512], pA[:, 512:1024], pb2[:], pb3[:]]
        for _e in ("vector", "scalar", "tensor"):
            for _g in ("gyt0", "gyt1"):
                kb.plan[_e].append(
                    lambda e, w=kb.gv(_g): e.wait_ge(kb.sem(w[0]), w[1]))
        kb.op("vector", lambda e: e.memset(xc_sb[64:66, KSEG:KSEG + 5120], 1.0))
        kb.op("vector", lambda e: e.memset(zrow_b[:], 0.0))
        kb.bar(("vector",))
        mSeg = None
        mOutV = {}
        for seg in range(4):
            t0 = 512 * seg
            kv0 = t0 - 64
            r0, r1 = max(0, kv0), min(T, t0 + 576)
            if seg in (0, 3):
                kb.wt("vector", mSeg)
                if seg == 0:
                    kb.op("vector", lambda e: e.memset(
                        kseg[0:64, 0:5120].rearrange(
                            "p (h n) -> p h n", h=8)[:, :, 0:64], 0.0))
                    kb.op("vector", lambda e: e.memset(vseg[:, 0:512], 0.0))
                    kb.op("vector", lambda e: e.memset(vseg_hi[:, 0:512], 0.0))
                else:
                    kb.op("vector", lambda e: e.memset(
                        kseg[0:64, 0:5120].rearrange(
                            "p (h n) -> p h n", h=8)[:, :, 576:640], 0.0))
                    kb.op("vector", lambda e: e.memset(vseg[:, 2048:2560], 0.0))
                    kb.op("vector", lambda e: e.memset(vseg_hi[:, 2048:2560], 0.0))
                mPad = kb.mk("vector")
            else:
                mPad = None
            gi = 0
            mPm = {}
            mPe = {}
            def pbank(g):
                return pb0 if g % 2 == 0 else pb1
            for m in range(4):
                ps = pbank(gi)
                kb.wt("tensor", mPe.get(gi - 2),
                      mSeg if gi < 2 else None,
                      mOutV.get(0 if gi % 2 == 0 else 1) if gi < 2 else None,
                      mPad if gi == 0 else None)
                for c in range(NCH):
                    kb.op("tensor", lambda e, m=m, c=c, t0=t0, ps=ps: e.matmul(
                        out=ps[:, 0:512],
                        lhsT=wslc(WQKV_O, c, 1536, 128, 128 * m),
                        rhs=ulch(c)[:, t0:t0 + 512],
                        start=(c == 0), stop=(c == 7)),
                        wait=[ld_w1] if (call == 1 and seg == 0 and m == 0
                                         and c == 0) else ())
                mPm[gi] = kb.mk("tensor")
                for half in range(2):
                    kb.op("scalar", lambda e, m=m, half=half, ps=ps: e.activation(
                        out=qseg[0:64, 512 * (2 * m + half):
                                 512 * (2 * m + half) + 512],
                        in_=ps[64 * half:64 * half + 64, 0:512],
                        func=AF.Identity, scale=0.125,
                        bias=bq_scaled[64 * half:64 * half + 64, m:m + 1]),
                        wait=[mPm[gi]] if half == 0 else ())
                mPe[gi] = kb.mk("scalar")
                gi += 1
            nsub = [(r0, min(r0 + 512, r1))]
            if r0 + 512 < r1:
                nsub.append((r0 + 512, r1))
            for m in range(4):
                for (a0, b0) in nsub:
                    ps = pbank(gi)
                    kb.wt("tensor", mPe.get(gi - 2))
                    for c in range(NCH):
                        kb.op("tensor", lambda e, m=m, c=c, a0=a0, b0=b0, ps=ps:
                              e.matmul(
                                  out=ps[:, 0:b0 - a0],
                                  lhsT=wslc(WQKV_O, c, 1536, 128, 512 + 128 * m),
                                  rhs=ulch(c)[:, a0:b0],
                                  start=(c == 0), stop=(c == 7)))
                    mPm[gi] = kb.mk("tensor")
                    for half in range(2):
                        kb.op("scalar", lambda e, m=m, half=half, a0=a0, b0=b0,
                              kv0=kv0, ps=ps: e.activation(
                            out=kseg[0:64, 640 * (2 * m + half) + (a0 - kv0):
                                     640 * (2 * m + half) + (b0 - kv0)],
                            in_=ps[64 * half:64 * half + 64, 0:b0 - a0],
                            func=AF.Identity,
                            bias=bqkv_col[64 * half:64 * half + 64, 4 + m:5 + m]),
                            wait=[mPm[gi]] if half == 0 else ())
                    mPe[gi] = kb.mk("scalar")
                    gi += 1
            for kcl in range(5):
                a0 = max(r0, kv0 + 128 * kcl)
                b0 = min(r1, kv0 + 128 * (kcl + 1))
                row_off = a0 - (kv0 + 128 * kcl)
                ps = pbank(gi)
                kb.wt("tensor", mPe.get(gi - 2))
                for c in range(NCH):
                    kb.op("tensor", lambda e, c=c, a0=a0, b0=b0, row_off=row_off,
                          ps=ps: e.matmul(
                        out=ps[row_off:row_off + (b0 - a0), 0:512],
                        lhsT=ulch(c)[:, a0:b0],
                        rhs=wslc(WQKV_O, c, 1536, 512, 1024),
                        start=(c == 0), stop=False))
                kb.op("tensor", lambda e, a0=a0, b0=b0, row_off=row_off, ps=ps:
                      e.matmul(
                          out=ps[row_off:row_off + (b0 - a0), 0:512],
                          lhsT=onesrow_b[0:1, 0:b0 - a0],
                          rhs=bv_row[0:1, 0:512], start=False, stop=True))
                mPm[gi] = kb.mk("tensor")
                kb.op("scalar", lambda e, kcl=kcl, a0=a0, b0=b0, row_off=row_off,
                      ps=ps: e.copy(
                    out=vseg[row_off:row_off + (b0 - a0),
                             512 * kcl:512 * kcl + 512],
                    in_=ps[row_off:row_off + (b0 - a0), 0:512]),
                    wait=[mPm[gi]])
                hr0, hr1 = max(64, row_off), min(128, row_off + (b0 - a0))
                if hr1 > hr0:
                    kb.op("scalar", lambda e, kcl=kcl, hr0=hr0, hr1=hr1, ps=ps:
                          e.copy(
                              out=vseg_hi[hr0 - 64:hr1 - 64,
                                          512 * kcl:512 * kcl + 512],
                              in_=ps[hr0:hr1, 0:512]))
                mPe[gi] = kb.mk("scalar")
                gi += 1
            mProjT = mPm[gi - 1]
            mProjS = mPe[gi - 1]
            mProjS2 = mPe[gi - 2]
            aslots = (pA[:, 0:256], pA[:, 512:768], pb0[:, 0:256], pb1[:, 0:256])
            mAe = {}
            for g in range(8):
                kb.wt("tensor", mAe.get(g - 1),
                      mProjS if g == 0 else None, mProjS2 if g == 0 else None)
                for gi2 in range(4):
                    ti = 4 * g + gi2
                    h, qtl = ti // 4, ti % 4
                    slot = aslots[gi2]
                    kb.op("tensor", lambda e, h=h, qtl=qtl, slot=slot: e.matmul(
                        out=slot, lhsT=qseg[0:64, 512 * h + 128 * qtl:
                                            512 * h + 128 * qtl + 128],
                        rhs=kseg[0:64, 640 * h + 128 * qtl:
                                 640 * h + 128 * qtl + 256],
                        start=True, stop=False))
                    ex = []
                    if seg == 0 and qtl == 0:
                        ex = [(0, 0)]
                    if seg == 3 and qtl == 3:
                        ex = [(1, 192)]
                    kb.op("tensor", lambda e, slot=slot, st=(not ex): e.matmul(
                        out=slot, lhsT=maskA_l[:], rhs=maskA_r[:],
                        start=False, stop=st))
                    for (ur, c0) in ex:
                        kb.op("tensor", lambda e, slot=slot, ur=ur, c0=c0:
                              e.matmul(
                                  out=slot,
                                  lhsT=(maskA_l[0:1, :] if ur == 0
                                        else u2row[0:1, :]),
                                  rhs=(edgeL_row[0:1, :] if c0 == 0
                                       else edgeR_row[0:1, :]),
                                  start=False, stop=True))
                mAm = kb.mk("tensor")
                for gi2 in range(4):
                    ti = 4 * g + gi2
                    kb.op("vector", lambda e, gi2=gi2, ti=ti, aslots=aslots:
                          e.tensor_reduce(
                              out=negm_seg[:, ti:ti + 1], in_=aslots[gi2],
                              axis=AX.X, op=ALU.max, negate=True),
                          wait=[mAm] if gi2 == 0 else ())
                mAv = kb.mk("vector")
                for gi2 in range(4):
                    ti = 4 * g + gi2
                    kb.op("scalar", lambda e, gi2=gi2, ti=ti, aslots=aslots:
                          e.activation(
                              out=A_bf[:, ADIS:ADIS + 256], in_=aslots[gi2],
                              func=AF.Exp, bias=negm_seg[:, ti:ti + 1],
                              accum_out=s_seg[:, ti:ti + 1]),
                          wait=[mAv] if gi2 == 0 else ())
                    kb.op("scalar", lambda e: e.drain())
                mAe[g] = kb.mk("scalar")
            kb.op("scalar", lambda e: e.activation(out=s_seg[:], in_=s_seg[:],
                                                   func=AF.Ln))
            mN1 = kb.mk("scalar")
            kb.op("vector", lambda e: e.tensor_tensor(
                out=negm_seg[:], in0=negm_seg[:], in1=s_seg[:],
                op=ALU.subtract), wait=[mN1])
            mN2 = kb.mk("vector")
            kb.op("tensor", lambda e: e.transpose(
                out=pb0[0:32, 0:128], in_=negm_seg[:, 0:32], identity=identf[:]),
                wait=[mN2])
            mN3 = kb.mk("tensor")
            kb.op("scalar", lambda e: e.copy(out=nmlT[:], in_=pb0[0:32, 0:128]),
                  wait=[mN3])
            mN4 = kb.mk("scalar")
            kb.op("vector", lambda e: e.tensor_copy(out=nmlT_h[:], in_=nmlT[:]),
                  wait=[mN4])
            kb.op("vector", lambda e: e.drain())
            kb.op("vector", lambda e: e.tensor_tensor(
                out=nmlT_l[:], in0=nmlT[:], in1=nmlT_h[:], op=ALU.subtract))
            mN5 = kb.mk("vector")
            kb.dma("sync", "gnm" + tg, nmlb_dram[0:1, :].rearrange(
                "r (p j) -> (r p) j", p=32), nmlT_h[:], wait=[mN5])
            kb.dma("sync", "gnm" + tg, nmlb_dram[1:2, :].rearrange(
                "r (p j) -> (r p) j", p=32), nmlT_l[:])
            wnm = kb.gv("gnm" + tg)
            kb.dma("gpsimd", "gnl" + tg, qseg[64:66, 0:4096], nmlb_dram[:, :],
                   wait=[wnm])
            wnl = kb.gv("gnl" + tg)
            kb.wt("tensor", mAe[7], wnl)
            for h2 in range(4):
                kb.op("tensor", lambda e, h2=h2: e.matmul(
                    out=obank[h2], lhsT=onesrow_b[0:1, 0:128],
                    rhs=zrow_b[0:1, 0:512], start=True, stop=False))
            mTe = {}
            mPVm = {}
            tiles = [(h, kcl) for h in range(8) for kcl in range(5)]

            def emit_pv(i):
                h, kcl = tiles[i]
                q0 = max(0, 128 * kcl - 128)
                q1 = min(512, 128 * kcl + 128)
                nq = q1 - q0
                ptb = A_bf[:, PT_O + 256 * (i % 2):PT_O + 256 * (i % 2) + 256]
                ptbh = A_bf[:, 17920 + 256 * (i % 2):
                            17920 + 256 * (i % 2) + 256]
                po = q0 - (128 * kcl - 128)
                ob = obank[h // 2]
                orow = 64 * (h % 2)
                kb.wt("tensor", mTe[i])
                npv = 0
                for (jr0, pc0, pc1) in ((0, 0, 192), (64, 64, 256)):
                    if seg == 0 and kcl == 0 and jr0 == 0:
                        continue
                    if seg == 3 and kcl == 4 and jr0 == 64:
                        continue
                    c0 = max(pc0, po) - po
                    c1 = min(pc1, po + nq) - po
                    if c1 <= c0:
                        continue
                    vs = vseg if jr0 == 0 else vseg_hi
                    pt_ = ptb if jr0 == 0 else ptbh
                    kb.op("tensor", lambda e, h=h, kcl=kcl, c0=c0, c1=c1,
                          q0=q0, ob=ob, orow=orow, vs=vs, pt_=pt_: e.matmul(
                        out=ob[orow:orow + 64, q0 + c0:q0 + c1],
                        lhsT=vs[0:64, 512 * kcl + 64 * h:
                                512 * kcl + 64 * h + 64],
                        rhs=pt_[0:64, c0:c1],
                        start=False, stop=False))
                    npv += 1
                mPVm[i] = kb.mk("tensor") if npv else mTe[i]

            for i, (h, kcl) in enumerate(tiles):
                q0 = max(0, 128 * kcl - 128)
                q1 = min(512, 128 * kcl + 128)
                nq = q1 - q0
                tp = (pb0 if i % 2 == 0 else pb1)
                kb.wt("tensor", mTe.get(i - 2))
                kb.op("tensor", lambda e, h=h, kcl=kcl, q0=q0, nq=nq, tp=tp:
                      e.matmul(
                          out=tp[:, 0:nq],
                          lhsT=kseg[0:66, 640 * h + 128 * kcl:
                                    640 * h + 128 * kcl + 128],
                          rhs=qseg[0:66, 512 * h + q0:512 * h + q0 + nq],
                          start=True, stop=True))
                mTm = kb.mk("tensor")
                ptb = A_bf[:, PT_O + 256 * (i % 2):PT_O + 256 * (i % 2) + 256]
                ptbh = A_bf[:, 17920 + 256 * (i % 2):
                            17920 + 256 * (i % 2) + 256]
                kb.op("scalar", lambda e, tp=tp, nq=nq, ptb=ptb: e.activation(
                    out=ptb[0:64, 0:nq], in_=tp[0:64, 0:nq], func=AF.Exp),
                    wait=[w for w in (mTm, mPVm.get(i - 2)) if w is not None])
                kb.op("scalar", lambda e, tp=tp, nq=nq, ptbh=ptbh: e.activation(
                    out=ptbh[0:64, 0:nq], in_=tp[64:128, 0:nq], func=AF.Exp))
                mTe[i] = kb.mk("scalar")
                if i > 0:
                    emit_pv(i - 1)
            emit_pv(len(tiles) - 1)
            for h2 in range(4):
                kb.op("tensor", lambda e, h2=h2: e.matmul(
                    out=obank[h2], lhsT=onesrow_b[0:1, 0:128],
                    rhs=zrow_b[0:1, 0:512], start=False, stop=True))
            mClose = kb.mk("tensor")
            for h2 in range(4):
                kb.op("scalar", lambda e, h2=h2: e.copy(
                    out=ofm[:, 512 * h2:512 * h2 + 512], in_=obank[h2]),
                    wait=[mClose] if h2 == 0 else ())
            mOe = kb.mk("scalar")
            mOm = {}
            for m in range(NCH):
                ps = pb0 if m % 2 == 0 else pb1
                kb.wt("tensor", mOe if m == 0 else None, mOutV.get(m % 2))
                for c in range(4):
                    kb.op("tensor", lambda e, m=m, c=c, ps=ps: e.matmul(
                        out=ps[:, 0:512],
                        lhsT=wslc(WOUTL_O, c, 1024, 128, 128 * m),
                        rhs=ofm[:, 512 * c:512 * c + 512],
                        start=(c == 0), stop=(c == 3)))
                mOm[m] = kb.mk("tensor")
                kb.op("vector", lambda e, m=m, t0=t0, ps=ps:
                      e.scalar_tensor_tensor(
                          out=xch(m)[:, t0:t0 + 512], in0=ps[:, 0:512],
                          scalar=0.5, in1=xch(m)[:, t0:t0 + 512],
                          op0=ALU.mult, op1=ALU.add), wait=[mOm[m]])
                mOutV[m % 2] = kb.mk("vector")
            mSeg = mOm[NCH - 1]
        kb.bar(("tensor", "vector", "scalar"))

    xq_dram = dram("xq_scr", (512, D), BF16, kind="Internal")
    xkv_dram = dram("xkv_scr", (1024, D), BF16, kind="Internal")
    rms_c = kb.sb("rms_c", (128, 12), F32)
    lntkv = kb.sb("lntkv", (128, 8), F32)
    sq5 = kb.sb("sq5", (128, 4), F32)
    negm_h = negm_seg
    s_h = s_seg

    def ffn_heavy(call):
        tg = "hv%d" % call
        QH, KH, VH, PT0, PT1 = 0, 4096, 12352, 16448, 16960
        wlt = kb.wv("bar_tensor")
        load_w("gw2" + tg, WQ_O, wq_d, 8, 512, wait=[wlt])
        load_w("gw2" + tg, WKV_O, wkv_d, 8, 1024)
        ld_w2 = load_w("gw2" + tg, WOUTH_O, wouth_d, 4, 1024)
        kb.plan["tensor"].append(lambda e, w=ld_w2: e.wait_ge(kb.sem(w[0]), w[1]))
        XTMP0, XTMP1, XQF, XKVF, OH = 0, 1024, 2048, 6144, 14336
        qh = A_bf[0:66, QH:QH + 4096]
        kh = A_bf[0:66, KH:KH + 8256]
        vh = A_bf[:, VH:VH + 4096]
        for g in range(16):
            kb.dma("sync", "ghp" + tg, hs_dram[128 * g:128 * (g + 1), :],
                   nullq_bh[:])
        w_hp = kb.gv("ghp" + tg)
        for g in range(12):
            tmp = xc_sb[:, (XTMP0 if g % 2 == 0 else XTMP1):
                        (XTMP0 if g % 2 == 0 else XTMP1) + 1024]
            wpr = [kb.gv("ghg%d" % (g % 2) + tg)] if g >= 2 else []
            kb.op("gpsimd", lambda e, g=g, tmp=tmp: e.indirect_dma_start(
                out=tmp, out_offset=None, in_=y_dram,
                in_offset=IOA(gidx[:, g:g + 1], 0)),
                wait=wpr, inc="g_gig%d" % (g % 2) + tg, incn=16)
            wg = kb.gv("gig%d" % (g % 2) + tg)
            kb.op("vector", lambda e, g=g, tmp=tmp: e.tensor_scalar(
                out=tmp, in0=tmp, scalar1=gmeta[:, 2 * g + 1:2 * g + 2],
                scalar2=None, op0=ALU.add), wait=[wg])
            mGv1 = kb.mk("vector")
            kb.op("scalar", lambda e, g=g, tmp=tmp: e.activation(
                out=xc_sb[:, OH:OH + 1024], in_=tmp, func=AF.Square,
                accum_out=rms_c[:, g:g + 1]), wait=[mGv1])
            kb.op("scalar", lambda e: e.drain())
            kb.op("scalar", lambda e, g=g: e.activation(
                out=rms_c[:, g:g + 1], in_=rms_c[:, g:g + 1], func=AF.Ln))
            kb.op("scalar", lambda e: e.drain())
            kb.op("scalar", lambda e, g=g: e.activation(
                out=rms_c[:, g:g + 1], in_=rms_c[:, g:g + 1], func=AF.Exp,
                scale=0.5))
            mGs = kb.mk("scalar")
            kb.op("vector", lambda e, g=g: e.tensor_scalar_max(
                out=rms_c[:, g:g + 1], in0=rms_c[:, g:g + 1], scalar1=1e-12),
                wait=[mGs])
            kb.op("vector", lambda e: e.drain())
            kb.op("vector", lambda e, g=g: e.reciprocal(
                out=rms_c[:, g:g + 1], in_=rms_c[:, g:g + 1]))
            kb.op("vector", lambda e: e.drain())
            kb.op("vector", lambda e, g=g, tmp=tmp: e.tensor_scalar(
                out=tmp, in0=tmp, scalar1=rms_c[:, g:g + 1], scalar2=32.0,
                op0=ALU.mult, op1=ALU.mult))
            mGv2 = kb.mk("vector")
            kb.plan["sync"].append(lambda e, w=mGv2: e.wait_ge(
                kb.sem(w[0]), w[1]))
            if g < 4:
                kb.dma("sync", "ghg%d" % (g % 2) + tg,
                       xq_dram[128 * g:128 * (g + 1), :], tmp)
            else:
                kb.dma("sync", "ghg%d" % (g % 2) + tg,
                       xkv_dram[128 * (g - 4):128 * (g - 3), :], tmp)
        w_g0 = kb.gv("ghg0" + tg)
        w_g1 = kb.gv("ghg1" + tg)
        for c in range(NCH):
            kb.dmat("ghq" + tg, xc_sb[:, XQF + 512 * c:XQF + 512 * (c + 1)],
                    xq_dram[:, 128 * c:128 * (c + 1)],
                    wait=[w_g0, w_g1] if c == 0 else ())
            kb.dmat("ghk" + tg, xc_sb[:, XKVF + 1024 * c:XKVF + 1024 * (c + 1)],
                    xkv_dram[:, 128 * c:128 * (c + 1)])
        w_tq = kb.gv("ghq" + tg)
        w_tk = kb.gv("ghk" + tg)
        kb.plan["vector"].append(lambda e, w=w_tq: e.wait_ge(kb.sem(w[0]), w[1]))
        kb.plan["vector"].append(lambda e, w=w_tk: e.wait_ge(kb.sem(w[0]), w[1]))
        for c in range(NCH):
            kb.op("vector", lambda e, c=c: e.tensor_scalar_mul(
                out=xc_sb[:, XQF + 512 * c:XQF + 512 * (c + 1)],
                in0=xc_sb[:, XQF + 512 * c:XQF + 512 * (c + 1)],
                scalar1=cols[:, 32 + c:33 + c]))
            kb.op("vector", lambda e, c=c: e.tensor_scalar_mul(
                out=xc_sb[:, XKVF + 1024 * c:XKVF + 1024 * (c + 1)],
                in0=xc_sb[:, XKVF + 1024 * c:XKVF + 1024 * (c + 1)],
                scalar1=cols[:, 32 + c:33 + c]))
        kb.bar(("vector",))
        kb.op("vector", lambda e: e.memset(A_bf[64:66, KH:KH + 8256], 1.0))
        kb.bar(("vector",))
        hgi = 0
        mHm = {}
        mHe = {}
        hpb = lambda g: (pb0 if g % 2 == 0 else pb1)
        for m in range(4):
            ps = hpb(hgi)
            kb.wt("tensor", mHe.get(hgi - 2))
            for c in range(NCH):
                kb.op("tensor", lambda e, m=m, c=c, ps=ps: e.matmul(
                    out=ps[:, 0:512],
                    lhsT=wslc(WQ_O, c, 512, 128, 128 * m),
                    rhs=xc_sb[:, XQF + 512 * c:XQF + 512 * (c + 1)],
                    start=(c == 0), stop=(c == 7)))
            mHm[hgi] = kb.mk("tensor")
            for half in range(2):
                kb.op("scalar", lambda e, m=m, half=half, ps=ps: e.activation(
                    out=A_bf[0:64, QH + 512 * (2 * m + half):
                             QH + 512 * (2 * m + half) + 512],
                    in_=ps[64 * half:64 * half + 64, 0:512],
                    func=AF.Identity, scale=0.125),
                    wait=[mHm[hgi]] if half == 0 else ())
            mHe[hgi] = kb.mk("scalar")
            hgi += 1
        for km in range(4):
            for nt in range(2):
                ps = hpb(hgi)
                kb.wt("tensor", mHe.get(hgi - 2))
                for c in range(NCH):
                    base = WKV_O + 1024 * c + 128 * km
                    kb.op("tensor", lambda e, km=km, nt=nt, c=c, base=base,
                          ps=ps: e.matmul(
                        out=ps[:, 0:512],
                        lhsT=w_arena[:, base:base + 128],
                        rhs=xc_sb[:, XKVF + 1024 * c + 512 * nt:
                                  XKVF + 1024 * c + 512 * (nt + 1)],
                        start=(c == 0), stop=(c == 7)))
                mHm[hgi] = kb.mk("tensor")
                for half in range(2):
                    kb.op("scalar", lambda e, km=km, nt=nt, half=half, ps=ps:
                          e.activation(
                        out=A_bf[0:64, KH + 1032 * (2 * km + half) + 512 * nt:
                                 KH + 1032 * (2 * km + half) + 512 * (nt + 1)],
                        in_=ps[64 * half:64 * half + 64, 0:512],
                        func=AF.Identity),
                        wait=[mHm[hgi]] if half == 0 else ())
                mHe[hgi] = kb.mk("scalar")
                hgi += 1
        for h in range(8):
            kb.op("vector", lambda e, h=h: e.tensor_copy(
                out=A_bf[0:64, KH + 1032 * h + 1024:KH + 1032 * h + 1025],
                in_=nullcol_b[0:64, h:h + 1]))
        kb.bar(("vector",))
        for kc in range(8):
            ps = hpb(hgi)
            kb.wt("tensor", mHe.get(hgi - 2))
            for c in range(NCH):
                base = WKV_O + 1024 * c + 512
                kb.op("tensor", lambda e, kc=kc, c=c, base=base, ps=ps: e.matmul(
                    out=ps[:, 0:512],
                    lhsT=xc_sb[:, XKVF + 1024 * c + 128 * kc:
                               XKVF + 1024 * c + 128 * (kc + 1)],
                    rhs=w_arena[:, base:base + 512],
                    start=(c == 0), stop=(c == 7)))
            mHm[hgi] = kb.mk("tensor")
            kb.op("scalar", lambda e, kc=kc, ps=ps: e.copy(
                out=A_bf[:, VH + 512 * kc:VH + 512 * (kc + 1)],
                in_=ps[:, 0:512]), wait=[mHm[hgi]])
            mHe[hgi] = kb.mk("scalar")
            hgi += 1
        mAs = None
        for h in range(8):
            for qt in range(4):
                ti = 4 * h + qt
                kb.wt("tensor", mAs, mHe.get(hgi - 1) if ti == 0 else None,
                      mHe.get(hgi - 2) if ti == 0 else None)
                for ks, n in ((0, 512), (512, 512), (1024, 1)):
                    kb.op("tensor", lambda e, h=h, qt=qt, ks=ks, n=n: e.matmul(
                        out=pA[:, ks:ks + n],
                        lhsT=A_bf[0:64, QH + 512 * h + 128 * qt:
                                  QH + 512 * h + 128 * qt + 128],
                        rhs=A_bf[0:64, KH + 1032 * h + ks:KH + 1032 * h + ks + n],
                        start=True, stop=True))
                mAt = kb.mk("tensor")
                kb.op("vector", lambda e, ti=ti: e.tensor_reduce(
                    out=negm_h[:, ti:ti + 1], in_=pA[:, 0:1025], axis=AX.X,
                    op=ALU.max, negate=True), wait=[mAt])
                mAv = kb.mk("vector")
                kb.op("scalar", lambda e, ti=ti: e.activation(
                    out=xc_sb[:, OH:OH + 1025], in_=pA[:, 0:1025],
                    func=AF.Exp, bias=negm_h[:, ti:ti + 1],
                    accum_out=s_h[:, ti:ti + 1]), wait=[mAv])
                kb.op("scalar", lambda e: e.drain())
                mAs = kb.mk("scalar")
        kb.op("scalar", lambda e: e.activation(out=s_h[:], in_=s_h[:], func=AF.Ln))
        kb.bar(("scalar",))
        kb.op("vector", lambda e: e.tensor_tensor(
            out=negm_h[:], in0=negm_h[:], in1=s_h[:], op=ALU.subtract))
        kb.bar(("vector",))
        kb.op("tensor", lambda e: e.transpose(
            out=pb0[0:32, 0:128], in_=negm_h[:, 0:32], identity=identf[:]))
        kb.bar(("tensor",))
        kb.op("scalar", lambda e: e.copy(out=nmlT[:], in_=pb0[0:32, 0:128]))
        kb.bar(("scalar",))
        kb.op("vector", lambda e: e.tensor_copy(out=nmlT_h[:], in_=nmlT[:]))
        kb.op("vector", lambda e: e.drain())
        kb.op("vector", lambda e: e.tensor_tensor(
            out=nmlT_l[:], in0=nmlT[:], in1=nmlT_h[:], op=ALU.subtract))
        kb.bar(("vector",))
        kb.dma("sync", "gnh" + tg, nmlb_dram[0:1, :].rearrange(
            "r (p j) -> (r p) j", p=32), nmlT_h[:])
        kb.dma("sync", "gnh" + tg, nmlb_dram[1:2, :].rearrange(
            "r (p j) -> (r p) j", p=32), nmlT_l[:])
        wnm = kb.gv("gnh" + tg)
        kb.dma("gpsimd", "gnk" + tg, A_bf[64:66, QH:QH + 4096], nmlb_dram[:, :],
               wait=[wnm])
        wnl = kb.gv("gnk" + tg)
        kb.plan["tensor"].append(lambda e, w=wnl: e.wait_ge(kb.sem(w[0]), w[1]))
        for kc in range(8):
            kb.op("scalar", lambda e, kc=kc: e.activation(
                out=lntkv[:, kc:kc + 1], in_=gmeta[:, 2 * (4 + kc):2 * (4 + kc) + 1],
                func=AF.Ln))
        kb.bar(("scalar",))
        obank2 = (pb0, pb1, pb2, pb3)
        tilesH = [(h, kc) for h in range(8) for kc in range(9)]
        mHTe = {}
        mHPV = {}

        def h_emit_pv(i):
            h, kc = tilesH[i]
            ob = obank2[h // 2]
            orow = 64 * (h % 2)
            ptb = A_bf[:, (PT0 if i % 2 == 0 else PT1):
                       (PT0 if i % 2 == 0 else PT1) + 512]
            kb.wt("tensor", mHTe[i])
            if kc < 8:
                kb.op("tensor", lambda e, h=h, kc=kc, ob=ob, orow=orow,
                      ptb=ptb: e.matmul(
                    out=ob[orow:orow + 64, 0:512],
                    lhsT=A_bf[:, VH + 512 * kc + 64 * h:
                              VH + 512 * kc + 64 * h + 64],
                    rhs=ptb[0:128, 0:512],
                    start=(kc == 0), stop=False))
            else:
                kb.op("tensor", lambda e, h=h, ob=ob, orow=orow, ptb=ptb:
                      e.matmul(
                    out=ob[orow:orow + 64, 0:512],
                    lhsT=nullvrow_b[0:1, 64 * h:64 * h + 64],
                    rhs=ptb[0:1, 0:512],
                    start=False, stop=True))
            mHPV[i] = kb.mk("tensor")

        for i, (h, kc) in enumerate(tilesH):
            n = 1 if kc == 8 else 128
            tp = pA[:, 0:512] if i % 2 == 0 else pA[:, 512:1024]
            kb.wt("tensor", mHTe.get(i - 2))
            kb.op("tensor", lambda e, h=h, kc=kc, n=n, tp=tp: e.matmul(
                out=tp[0:n, 0:512],
                lhsT=A_bf[0:66, KH + 1032 * h + 128 * kc:
                          KH + 1032 * h + 128 * kc + n],
                rhs=A_bf[0:66, QH + 512 * h:QH + 512 * h + 512],
                start=True, stop=True))
            mHTm = kb.mk("tensor")
            ptb = A_bf[:, (PT0 if i % 2 == 0 else PT1):
                       (PT0 if i % 2 == 0 else PT1) + 512]
            kb.op("scalar", lambda e, kc=kc, n=n, tp=tp, ptb=ptb: e.activation(
                out=ptb[0:n, 0:512], in_=tp[0:n, 0:512], func=AF.Exp,
                bias=(lntkv[0:n, kc:kc + 1] if kc < 8 else fconst[0:1, 2:3])),
                wait=[w for w in (mHTm, mHPV.get(i - 2)) if w is not None])
            mHTe[i] = kb.mk("scalar")
            if i > 0:
                h_emit_pv(i - 1)
        h_emit_pv(len(tilesH) - 1)
        mHlast = mHPV[len(tilesH) - 1]
        for h2 in range(4):
            kb.op("scalar", lambda e, h2=h2: e.copy(
                out=xc_sb[:, OH + 512 * h2:OH + 512 * (h2 + 1)],
                in_=obank2[h2][:, 0:512]),
                wait=[mHlast] if h2 == 0 else ())
        kb.bar(("scalar",))
        kb.op("vector", lambda e: e.tensor_scalar(
            out=sq5[:], in0=gmeta[:, 0:8].rearrange("p (g k) -> p g k", k=2)[:, :, 0],
            scalar1=0.5, scalar2=None, op0=ALU.mult))
        kb.bar(("vector",))
        for tc in range(4):
            for nt in range(2):
                for c2 in range(4):
                    kb.op("tensor", lambda e, tc=tc, nt=nt, c2=c2: e.matmul(
                        out=pA[:, 0:512],
                        lhsT=xc_sb[:, OH + 512 * c2 + 128 * tc:
                                   OH + 512 * c2 + 128 * tc + 128],
                        rhs=wslc(WOUTH_O, c2, 1024, 512, 512 * nt),
                        start=(c2 == 0), stop=(c2 == 3)))
                kb.bar(("tensor",))
                kb.op("scalar", lambda e, tc=tc, nt=nt: e.activation(
                    out=A_bf[:, 6144 + 1024 * tc + 512 * nt:
                             6144 + 1024 * tc + 512 * (nt + 1)],
                    in_=pA[:, 0:512], func=AF.Identity,
                    scale=sq5[:, tc:tc + 1]))
                kb.bar(("scalar",))
        kb.plan["gpsimd"].append(lambda e, w=w_hp: e.wait_ge(kb.sem(w[0]), w[1]))
        for tc in range(4):
            kb.op("gpsimd", lambda e, tc=tc: e.indirect_dma_start(
                out=hs_dram, out_offset=IOA(gidx[:, tc:tc + 1], 0),
                in_=A_bf[:, 6144 + 1024 * tc:6144 + 1024 * (tc + 1)],
                in_offset=None), inc="g_ghs" + tg, incn=16)
        w_hs = kb.gv("ghs" + tg)
        for c in range(NCH):
            buf = xc_sb[:, 2048 + 2048 * (c % 2):2048 + 2048 * (c % 2) + 2048]
            wt = [w_hs] if c < 2 else [kb.gv("ghb%d" % (c % 2) + tg)]
            kb.dmat("ghb%d" % (c % 2) + tg, buf, hs_dram[:, 128 * c:128 * (c + 1)],
                    wait=wt)
            wldb = kb.gv("ghb%d" % (c % 2) + tg)
            kb.op("vector", lambda e, c=c, buf=buf: e.tensor_tensor(
                out=xch(c), in0=xch(c), in1=buf, op=ALU.add), wait=[wldb],
                inc="hsadd")
            kb.plan["sync"].append(
                lambda e, w=kb.wv("hsadd"): e.wait_ge(kb.sem(w[0]), w[1]))
        kb.bar(("vector",))

    um_dram = dram("um_scr", (D, T), BF16, kind="Internal")
    g_dram = dram("g_scr", (INNER, T), BF16, kind="Internal")
    xs_dram = dram("xs_scr", (T, INNER), BF16, kind="Internal")
    m8c_d = dram("m8c_in", (8, 4))
    binr_d = dram("binr_in", (1, 2304))
    cmask_d = dram("cmask_in", (128, 128))
    Bfm = xc_sb[0:64, 640:2688]
    Cfm = xc_sb[0:64, 2688:4736]
    Btm = xc_sb[0:128, 4736:5760]
    Hst = xc_sb[0:64, 5760:7808]
    Dident = xc_sb[0:128, 7808:8832]
    binr_b = xc_sb[0:1, 8832:11136]
    bmx_row = xc_sb[0:1, 11136:13440]
    lt01b = xc_sb[0:128, 13440:13568]
    mr0 = lambda c0=0, n=T: repf[32:40, c0:c0 + n]
    mr1 = lambda c0=0, n=T: repf[64:72, c0:c0 + n]
    P8 = lambda c0=0, n=T: repf[0:8, c0:c0 + n]
    E8 = lambda c0=0, n=T: repb[0:8, c0:c0 + n]
    mcol = srt_q[:, 0:12]
    bias_mz = srt_kv[:, 0:16]
    bias_mx = gmeta[:, 0:18]
    dsk8 = cols[:, 86:94]
    kb.dma("sync", "gmc", cols[0:8, 82:86], m8c_d)
    kb.dma("sync", "gmc", cols[32:40, 82:86], m8c_d)
    ld_mc = kb.gv("gmc")

    def mamba():
        if DEBUG:
            for _e in ("vector", "scalar", "tensor"):
                kb.plan[_e].append(lambda e, w=kb.gv("gdbgr"): e.wait_ge(
                    kb.sem(w[0]), w[1]))
        w_rep2 = bcast_row(SL_INV, 2, repb[:], "mi",
                           wait_extra=[kb.wv("bar_vector")])
        for c in range(NCH):
            kb.op("vector", lambda e, c=c: e.tensor_tensor(
                out=A_bf[:, 0:2048], in0=xcch(c), in1=repb[:], op=ALU.mult),
                wait=[w_rep2] if c == 0 else ())
            kb.op("vector", lambda e: e.drain())
            kb.op("vector", lambda e, c=c: e.tensor_tensor(
                out=xch(c), in0=xch(c), in1=A_bf[:, 0:2048], op=ALU.add))
            kb.op("vector", lambda e: e.drain())
        kb.bar(("vector",))
        kb.op("vector", lambda e: e.tensor_tensor(out=statr(SL_T0), in0=statr(SL_INV),
                                                  in1=statr(SL_INV), op=ALU.mult))
        kb.op("vector", lambda e: e.drain())
        kb.op("vector", lambda e: e.tensor_tensor(out=statr(SL_T0), in0=statr(SL_T0),
                                                  in1=statr(SL_VAR), op=ALU.mult))
        kb.bar(("vector",))
        kb.op("scalar", lambda e: e.activation(out=statr(SL_T1), in_=statr(SL_T0),
                                               func=AF.Ln, bias=fconst[0:4, 0:1]))
        kb.op("scalar", lambda e: e.drain())
        kb.op("scalar", lambda e: e.activation(out=statr(SL_T1), in_=statr(SL_T1),
                                               func=AF.Exp, scale=-0.5))
        kb.bar(("scalar",))
        kb.op("vector", lambda e: e.tensor_tensor(out=statr(SL_S2), in0=statr(SL_INV),
                                                  in1=statr(SL_T1), op=ALU.mult))
        kb.bar(("vector",))
        w_rep3 = bcast_row(SL_S2, 3, repb[:], "ms",
                           wait_extra=[kb.wv("bar_vector")])
        for c in range(NCH):
            kb.op("vector", lambda e, c=c: e.scalar_tensor_tensor(
                out=xcch(c), in0=xcch(c), scalar=cols[:, 16 + c:17 + c],
                in1=repb[:], op0=ALU.mult, op1=ALU.mult),
                wait=[w_rep3] if c == 0 else ())
        kb.bar(("vector",))
        xc_r2 = xc_sb[:].rearrange("p (c t) -> p c t", c=NCH)
        um_r = um_dram.rearrange("(c p) t -> p c t", p=128)
        for i in range(2):
            kb.dma("sync", "gum", um_r[:, 4 * i:4 * i + 4, :],
                   xc_r2[:, 4 * i:4 * i + 4, :], wait=[kb.wv("bar_vector")])
        w_um = kb.gv("gum")
        kb.dma("gpsimd", "gmcb", binr_b, binr_d, wait=[w_um])
        kb.dma("gpsimd", "gmcb", lt01b, cmask_d)
        ld_mcb = kb.gv("gmcb")
        wmt = kb.wv("bar_tensor")
        win_r = win_d.rearrange("(c p) f -> p c f", p=128)
        kb.dma("gpsimd", "gwz",
               w_arena[:, 0:16384].rearrange("p (c f) -> p c f", c=8),
               win_r[:, :, 0:2048], wait=[wmt])
        ld_wz = kb.gv("gwz")
        kb.plan["tensor"].append(lambda e, w=ld_wz: e.wait_ge(kb.sem(w[0]), w[1]))
        for m in range(16):
            for c in range(NCH):
                kb.op("tensor", lambda e, m=m, c=c: e.matmul(
                    out=pb0[0:128, m:m + 1], lhsT=wslc(0, c, 2048, 128, 128 * m),
                    rhs=colsb[:, 24 + c:25 + c], start=(c == 0), stop=(c == 7)))
        kb.bar(("tensor",))
        kb.op("vector", lambda e: e.tensor_tensor(
            out=bias_mz, in0=pb0[0:128, 0:16], in1=cols[:, 40:56], op=ALU.add))
        kb.bar(("vector",))
        mZe = {}
        wZd = {}
        for tt in range(NTT):
            ub = A_bf[:, (0 if tt % 2 == 0 else 4096):(0 if tt % 2 == 0 else 4096) + 4096]
            wpr = ([kb.gv("gub%d" % (tt % 2))] if tt >= 2 else [w_um]) + \
                [kb.wv("bar_tensor"), kb.wv("bar_scalar")]
            kb.dma("gpsimd", "gub%d" % (tt % 2),
                   ub.rearrange("p (c t) -> p c t", c=8),
                   um_r[:, :, 512 * tt:512 * (tt + 1)], wait=wpr)
            wub = kb.gv("gub%d" % (tt % 2))
            kb.plan["tensor"].append(lambda e, w=wub: e.wait_ge(kb.sem(w[0]), w[1]))
            for m in range(16):
                zgi = 16 * tt + m
                zp = pb1 if zgi % 2 == 0 else pb2
                zb = A_bf[:, 8192 + 512 * (zgi % 2):8192 + 512 * (zgi % 2) + 512]
                kb.wt("tensor", mZe.get(zgi - 2))
                for c in range(NCH):
                    kb.op("tensor", lambda e, m=m, c=c, ub=ub, zp=zp: e.matmul(
                        out=zp[:, 0:512], lhsT=wslc(0, c, 2048, 128, 128 * m),
                        rhs=ub[:, 512 * c:512 * (c + 1)],
                        start=(c == 0), stop=(c == 7)))
                mZm = kb.mk("tensor")
                kb.op("scalar", lambda e, m=m, zp=zp, zb=zb: e.activation(
                    out=zb, in_=zp[:, 0:512],
                    func=AF.Silu, bias=bias_mz[:, m:m + 1]),
                    wait=[w for w in (mZm, wZd.get(zgi - 2)) if w is not None])
                mZe[zgi] = kb.mk("scalar")
                kb.plan["sync"].append(lambda e, w=mZe[zgi]: e.wait_ge(
                    kb.sem(w[0]), w[1]))
                kb.dma("sync", "ggst%d" % (zgi % 2),
                       g_dram[128 * m:128 * (m + 1), 512 * tt:512 * (tt + 1)],
                       zb)
                wZd[zgi] = kb.gv("ggst%d" % (zgi % 2))
        w_g = (kb.gv("ggst0"), kb.gv("ggst1"))
        if os.environ.get("ATH_MK_MM") == "1":
            kb.bar(("tensor", "vector", "scalar"))
            return
        wmt2 = kb.wv("bar_tensor")
        kb.dma("gpsimd", "gwx",
               w_arena[:, 0:18432].rearrange("p (c f) -> p c f", c=8),
               win_r[:, :, 2048:4352], wait=[wmt2])
        ld_wx = kb.gv("gwx")
        kb.plan["tensor"].append(lambda e, w=ld_wx: e.wait_ge(kb.sem(w[0]), w[1]))
        kb.plan["tensor"].append(lambda e, w=ld_mcb: e.wait_ge(kb.sem(w[0]), w[1]))
        kb.wt("tensor", mZe.get(62), mZe.get(63))
        for m in (16, 17):
            for c in range(NCH):
                kb.op("tensor", lambda e, m=m, c=c: e.matmul(
                    out=pb0[0:128, m - 16:m - 15],
                    lhsT=wslc(0, c, 2304, 128, 128 * m),
                    rhs=colsb[:, 24 + c:25 + c], start=(c == 0), stop=(c == 7)))
        kb.bar(("tensor",))
        kb.op("vector", lambda e: e.tensor_tensor(
            out=bias_mx[:, 16:18], in0=pb0[0:128, 0:2], in1=cols[:, 72:74],
            op=ALU.add))
        kb.bar(("vector",))
        for n in range(4):
            for c in range(NCH):
                kb.op("tensor", lambda e, n=n, c=c: e.matmul(
                    out=pb2[0:1, 0:512], lhsT=colsb[:, 24 + c:25 + c],
                    rhs=wslc(0, c, 2304, 512, 512 * n), start=(c == 0), stop=False))
            kb.op("tensor", lambda e, n=n: e.matmul(
                out=pb2[0:1, 0:512], lhsT=onesrow_b[0:1, 0:1],
                rhs=binr_b[0:1, 512 * n:512 * (n + 1)], start=False, stop=True))
            kb.bar(("tensor",))
            kb.op("scalar", lambda e, n=n: e.copy(
                out=bmx_row[0:1, 512 * n:512 * (n + 1)], in_=pb2[0:1, 0:512]),
                wait=[w_um] if n == 0 else ())
            kb.bar(("scalar",))
        kb.op("scalar", lambda e: e.activation(out=mcol[32:40, 0:1],
                                               in_=cols[32:40, 82:83], func=AF.Exp))
        kb.bar(("scalar",))
        kb.op("vector", lambda e: e.tensor_scalar_mul(out=mcol[32:40, 1:2],
                                                      in0=mcol[32:40, 0:1],
                                                      scalar1=-1.0))
        kb.bar(("vector",))
        mXe = {}
        wXd = {-2: wZd.get(62), -1: wZd.get(63)}
        mXbc = {}
        for tt in range(NTT):
            ub = A_bf[:, (0 if tt % 2 == 0 else 4096):(0 if tt % 2 == 0 else 4096) + 4096]
            wpr = ([kb.gv("gux%d" % (tt % 2))] if tt >= 2 else [w_um]) + \
                [kb.wv("bar_tensor"), kb.wv("bar_scalar")]
            kb.dma("gpsimd", "gux%d" % (tt % 2),
                   ub.rearrange("p (c t) -> p c t", c=8),
                   um_r[:, :, 512 * tt:512 * (tt + 1)], wait=wpr)
            wub = kb.gv("gux%d" % (tt % 2))
            kb.plan["tensor"].append(lambda e, w=wub: e.wait_ge(kb.sem(w[0]), w[1]))
            for tc4 in range(4):
                for n in range(4):
                    xgi = 16 * tt + 4 * tc4 + n
                    xp = pb0 if xgi % 2 == 0 else pb3
                    xb = A_bf[:, 8192 + 512 * (xgi % 2):
                              8192 + 512 * (xgi % 2) + 512]
                    kb.wt("tensor", mXe.get(xgi - 2))
                    for c in range(NCH):
                        kb.op("tensor", lambda e, tc4=tc4, n=n, c=c, ub=ub,
                              xp=xp: e.matmul(
                            out=xp[:, 0:512],
                            lhsT=ub[:, 512 * c + 128 * tc4:
                                    512 * c + 128 * tc4 + 128],
                            rhs=wslc(0, c, 2304, 512, 512 * n),
                            start=(c == 0), stop=False))
                    kb.op("tensor", lambda e, n=n, xp=xp: e.matmul(
                        out=xp[:, 0:512], lhsT=onesrow_b[0:1, :],
                        rhs=bmx_row[0:1, 512 * n:512 * (n + 1)],
                        start=False, stop=True))
                    mXm = kb.mk("tensor")
                    kb.op("scalar", lambda e, xp=xp, xb=xb: e.copy(
                        out=xb, in_=xp[:, 0:512]),
                        wait=[w for w in (mXm, wXd.get(xgi - 2))
                              if w is not None])
                    mXe[xgi] = kb.mk("scalar")
                    kb.plan["sync"].append(lambda e, w=mXe[xgi]: e.wait_ge(
                        kb.sem(w[0]), w[1]))
                    kb.dma("sync", "gxst%d" % (xgi % 2),
                           xs_dram[512 * tt + 128 * tc4:
                                   512 * tt + 128 * (tc4 + 1),
                                   512 * n:512 * (n + 1)], xb)
                    wXd[xgi] = kb.gv("gxst%d" % (xgi % 2))
            for mm2 in (16, 17):
                kb.wt("tensor", mXbc.get((tt, mm2 - 1)),
                      mXbc.get((tt - 1, 17)) if mm2 == 16 else None)
                for c in range(NCH):
                    kb.op("tensor", lambda e, mm2=mm2, c=c, ub=ub: e.matmul(
                        out=pb1[:, 0:512], lhsT=wslc(0, c, 2304, 128, 128 * mm2),
                        rhs=ub[:, 512 * c:512 * (c + 1)],
                        start=(c == 0), stop=(c == 7)))
                mXt = kb.mk("tensor")
                if mm2 == 16:
                    kb.op("scalar", lambda e, tt=tt: e.activation(
                        out=Bfm[:, 512 * tt:512 * (tt + 1)], in_=pb1[0:64, 0:512],
                        func=AF.Identity, bias=bias_mx[0:64, 16:17]),
                        wait=[mXt])
                    kb.op("scalar", lambda e, tt=tt: e.activation(
                        out=Cfm[:, 512 * tt:512 * (tt + 1)], in_=pb1[64:128, 0:512],
                        func=AF.Identity, bias=bias_mx[64:128, 16:17]))
                else:
                    kb.op("scalar", lambda e, tt=tt: e.copy(
                        out=mr0(512 * tt, 512), in_=pb1[0:8, 0:512]), wait=[mXt])
                mXbc[(tt, mm2)] = kb.mk("scalar")
        for tc in range(16):
            kb.op("tensor", lambda e, tc=tc: e.transpose(
                out=pT[:, 0:64], in_=Bfm[:, 128 * tc:128 * (tc + 1)],
                identity=identb[0:64, 0:64]))
            kb.bar(("tensor",))
            kb.op("scalar", lambda e, tc=tc: e.copy(
                out=Btm[:, 64 * tc:64 * (tc + 1)], in_=pT[:, 0:64]))
            kb.bar(("scalar",))
        if os.environ.get("ATH_MK_MM") == "2":
            kb.bar(("tensor", "vector", "scalar"))
            return
        kb.op("vector", lambda e: e.tensor_tensor(
            out=mcol[0:8, 2:3], in0=bias_mx[0:8, 17:18], in1=cols[0:8, 84:85],
            op=ALU.add))
        kb.bar(("vector",))
        kb.op("scalar", lambda e: e.copy(out=pb3[0:8, 256:257],
                                         in_=mcol[0:8, 2:3]))
        kb.op("scalar", lambda e: e.drain())
        kb.op("scalar", lambda e: e.copy(out=mcol[32:40, 2:3],
                                         in_=pb3[0:8, 256:257]))
        kb.bar(("scalar",))
        kb.op("vector", lambda e: e.tensor_scalar(
            out=mr0(), in0=mr0(), scalar1=mcol[32:40, 2:3], scalar2=None,
            op0=ALU.add))
        kb.bar(("vector",))
        for i4 in range(4):
            PP = pb3[0:8, 0:512]
            kb.op("scalar", lambda e, i4=i4: e.activation(
                out=PP, in_=mr0(512 * i4, 512), func=AF.Abs))
            kb.op("scalar", lambda e: e.drain())
            kb.op("scalar", lambda e: e.activation(out=PP, in_=PP, func=AF.Exp,
                                                   scale=-1.0))
            kb.op("scalar", lambda e: e.drain())
            kb.op("scalar", lambda e: e.activation(out=PP, in_=PP, func=AF.Ln,
                                                   bias=fconst[0:8, 1:2]))
            kb.bar(("scalar",))
            kb.op("vector", lambda e, i4=i4: e.tensor_scalar_max(
                out=mr0(512 * i4, 512), in0=mr0(512 * i4, 512), scalar1=0.0))
            kb.op("vector", lambda e: e.drain())
            kb.op("vector", lambda e, i4=i4: e.tensor_tensor(
                out=mr0(512 * i4, 512), in0=mr0(512 * i4, 512), in1=PP,
                op=ALU.add))
            kb.bar(("vector",))
        for cc2 in range(NTC):
            kb.op("scalar", lambda e, cc2=cc2: e.activation(
                out=pb3[0:8, 0:128], in_=mr0(128 * cc2, 128), func=AF.Ln))
            kb.bar(("scalar",))
            kb.op("vector", lambda e, cc2=cc2: e.tensor_scalar_mul(
                out=pb3[0:8, 128:256], in0=mr0(128 * cc2, 128),
                scalar1=mcol[32:40, 1:2]))
            kb.op("vector", lambda e: e.drain())
            kb.op("vector", lambda e, cc2=cc2: e.tensor_tensor_scan(
                out=P8(128 * cc2, 128), data0=pb3[0:8, 128:256],
                data1=nmlT[0:8, :], initial=0.0, op0=ALU.add, op1=ALU.bypass))
            kb.op("vector", lambda e: e.drain())
            kb.op("vector", lambda e, cc2=cc2: e.tensor_tensor(
                out=nmlT[0:8, :], in0=pb3[0:8, 0:128], in1=P8(128 * cc2, 128),
                op=ALU.subtract))
            kb.bar(("vector",))
            kb.op("tensor", lambda e, cc2=cc2: e.transpose(
                out=pA[0:128, 8 * cc2:8 * cc2 + 8], in_=nmlT[0:8, :],
                identity=identf[0:8, 0:8]))
            kb.bar(("tensor",))
        kb.op("scalar", lambda e: e.copy(out=sk_sc[:, 0:128],
                                         in_=pA[0:128, 0:128]))
        kb.bar(("scalar",))
        kb.op("tensor", lambda e: e.matmul(
            out=pb0[0:128, 0:8], lhsT=onesrow_f[0:1, :],
            rhs=smallrows[0:1, 8:16], start=True, stop=True))
        kb.bar(("tensor",))
        kb.op("vector", lambda e: e.tensor_copy(out=dsk8, in_=pb0[0:128, 0:8]))
        kb.bar(("vector",))
        for h in range(8):
            kb.op("vector", lambda e, h=h: e.tensor_scalar_mul(
                out=Dident[:, 128 * h:128 * (h + 1)], in0=identb[:],
                scalar1=dsk8[:, h:h + 1]), wait=[w_um] if h == 0 else ())
        kb.op("vector", lambda e: e.drain())
        kb.op("vector", lambda e: e.memset(Hst, 0.0))
        kb.bar(("vector",))
        if os.environ.get("ATH_MK_MM") == "3":
            kb.bar(("tensor", "vector", "scalar"))
            return
        wmt3 = kb.wv("bar_tensor")
        kb.dma("gpsimd", "gwo",
               w_arena[:, 0:16384].rearrange("p (c f) -> p c f", c=16),
               wout_d.rearrange("(c p) f -> p c f", p=128), wait=[wmt3])
        ld_wo = kb.gv("gwo")
        M2O, EO, CMEO, WXSO, EREPO, GMO = 16384, 16640, 16896, 17152, 17664, 17920
        m2b = lambda h: A_bf[:, M2O + 128 * (h % 2):M2O + 128 * (h % 2) + 128]
        eb = lambda h: A_bf[:, EO + 128 * (h % 2):EO + 128 * (h % 2) + 128]
        cmeb = lambda h: A_bf[0:64, CMEO + 128 * (h % 2):CMEO + 128 * (h % 2) + 128]
        wxb = lambda h: A_bf[:, WXSO + 256 * (h % 2):WXSO + 256 * (h % 2) + 256]
        erb = lambda h: A_bf[0:64, EREPO + 128 * (h % 2):EREPO + 128 * (h % 2) + 128]
        skrow = lambda h: (sk_row if h % 2 == 0 else sk_row2)
        dslot = lambda h: (0 if h % 2 == 0 else 384)
        wargc = lambda h: mcol[:, (9 if h % 2 == 0 else 3):(10 if h % 2 == 0 else 4)]
        wcolc = lambda h: mcol[:, (10 if h % 2 == 0 else 4):(11 if h % 2 == 0 else 5)]
        gamc = lambda h: mcol[:, (11 if h % 2 == 0 else 5):(12 if h % 2 == 0 else 6)]
        mS1 = {}; mS2 = {}; mVa = {}; mVm = {}; mVw = {}; mVy = {}
        mVh = {}; mTy = {}; mTx = {}
        mVh_prev = {}; mTo_last = None; mVo = {}; mGm = None
        hist = {}
        w_xst0, w_xst1 = kb.gv("gxst0"), kb.gv("gxst1")
        wbt0, wbv0 = kb.wv("bar_tensor"), kb.wv("bar_vector")
        SCN = int(os.environ.get("ATH_MK_SC", "16"))
        for cc in range(SCN):
            par = cc % 2
            h2 = hist.get(cc - 2, (None, None, None))
            wx = ([kb.gv("gxs%d" % par), h2[0], h2[1]]
                  if cc >= 2 else [w_xst0, w_xst1, wbt0, wbv0])
            kb.dma("gpsimd", "gxs%d" % par, A_bf[:, 2048 * par:2048 * (par + 1)],
                   xs_dram[128 * cc:128 * (cc + 1), :],
                   wait=[w for w in wx if w is not None])
            wxs_ = kb.gv("gxs%d" % par)
            wg2 = ([kb.gv("ggb%d" % par), h2[2]]
                   if cc >= 2 else [*w_g, wbt0, wbv0])
            kb.dma("gpsimd", "ggb%d" % par,
                   A_bf[:, 4096 + 2048 * par:4096 + 2048 * (par + 1)].rearrange(
                       "p (c t) -> p c t", c=16),
                   g_dram[:, 128 * cc:128 * (cc + 1)].rearrange(
                       "(c p) t -> p c t", p=128),
                   wait=[w for w in wg2 if w is not None])
            wgb = kb.gv("ggb%d" % par)
            xsb = A_bf[:, 2048 * par:2048 * (par + 1)]
            gb = A_bf[:, 4096 + 2048 * par:4096 + 2048 * (par + 1)]
            kb.wt("tensor", mGm)
            kb.op("tensor", lambda e, cc=cc: e.matmul(
                out=pb0[:, 0:128], lhsT=Bfm[:, 128 * cc:128 * (cc + 1)],
                rhs=Cfm[:, 128 * cc:128 * (cc + 1)], start=True, stop=True))
            mGt = kb.mk("tensor")
            kb.op("vector", lambda e: e.tensor_tensor(
                out=A_bf[:, GMO:GMO + 128], in0=pb0[:, 0:128], in1=lt01b,
                op=ALU.mult), wait=[mGt])
            mGm = kb.mk("vector")
            kb.wt("tensor", mS1.get(6), mS1.get(7))
            kb.op("tensor", lambda e, cc=cc: e.matmul(
                out=pb3[0:1, 384:512], lhsT=identf[0:8, 0:1],
                rhs=repf[0:8, 128 * cc:128 * (cc + 1)], start=True, stop=True))
            mT0 = kb.mk("tensor")
            kb.op("scalar", lambda e: e.copy(out=sk_row[0:1, :],
                                             in_=pb3[0:1, 384:512]),
                  wait=[mT0])
            mS0 = kb.mk("scalar")
            mS1.clear(); mS2.clear(); mVa.clear(); mVm.clear(); mVw.clear()
            mVh_prev = dict(mVh); mVy.clear(); mVh.clear(); mTy.clear()
            mTx.clear()
            mS1[-1] = mS0
            for h in range(8):
                bcol = sk_sc[:, 8 * cc + h:8 * cc + h + 1]
                kb.wt("tensor", mS1.get(h - 2), mVa.get(h - 1), mS2.get(h - 1),
                      mS1.get(h - 1))
                kb.op("tensor", lambda e, h=h: e.matmul(
                    out=pb1[:, dslot(h):dslot(h) + 128], lhsT=onesrow_f[0:1, :],
                    rhs=skrow(h)[0:1, :], start=True, stop=True))
                kb.op("tensor", lambda e, h=h: e.matmul(
                    out=pb1[0:64, 128:256], lhsT=onesrow_f[0:1, 0:64],
                    rhs=skrow(h)[0:1, :], start=True, stop=True))
                kb.op("tensor", lambda e, h=h: e.matmul(
                    out=pb1[:, 256:257], lhsT=onesrow_f[0:1, :],
                    rhs=skrow(h)[0:1, 127:128], start=True, stop=True))
                if h < 7:
                    kb.op("tensor", lambda e, h=h, cc=cc: e.matmul(
                        out=pb3[0:1, 384:512], lhsT=identf[0:8, h + 1:h + 2],
                        rhs=repf[0:8, 128 * cc:128 * (cc + 1)],
                        start=True, stop=True))
                mT1 = kb.mk("tensor")
                kb.op("vector", lambda e, h=h, bcol=bcol: e.tensor_scalar(
                    out=pb1[:, dslot(h):dslot(h) + 128],
                    in0=pb1[:, dslot(h):dslot(h) + 128], scalar1=bcol,
                    scalar2=20.0, op0=ALU.add, op1=ALU.min),
                    wait=[mT1])
                mTmin = kb.mk("vector")
                if h < 7:
                    kb.op("scalar", lambda e, h=h: e.copy(
                        out=skrow(h + 1)[0:1, :], in_=pb3[0:1, 384:512]),
                        wait=[mT1])
                kb.op("scalar", lambda e, h=h: e.activation(
                    out=eb(h), in_=pb1[:, dslot(h):dslot(h) + 128], func=AF.Exp),
                    wait=[mTmin])
                kb.op("scalar", lambda e, h=h: e.activation(
                    out=erb(h), in_=pb1[0:64, 128:256], func=AF.Exp))
                mS1[h] = kb.mk("scalar")
                kb.op("vector", lambda e, h=h, bcol=bcol: e.tensor_tensor(
                    out=wargc(h), in0=pb1[:, 256:257], in1=bcol, op=ALU.add),
                    wait=[mT1])
                mVa[h] = kb.mk("vector")
                kb.op("scalar", lambda e, h=h: e.activation(
                    out=wcolc(h), in_=wargc(h), func=AF.Exp), wait=[mVa[h]])
                kb.op("scalar", lambda e, h=h: e.activation(
                    out=gamc(h), in_=pb1[:, 256:257], func=AF.Exp))
                mS2[h] = kb.mk("scalar")
                kb.op("vector", lambda e, h=h, cc=cc: e.tensor_tensor(
                    out=cmeb(h), in0=Cfm[:, 128 * cc:128 * (cc + 1)],
                    in1=erb(h), op=ALU.mult), wait=[mS1[h]])
                kb.op("vector", lambda e, h=h: e.tensor_tensor(
                    out=m2b(h), in0=A_bf[:, GMO:GMO + 128], in1=eb(h),
                    op=ALU.mult))
                kb.op("vector", lambda e: e.drain())
                kb.op("vector", lambda e, h=h: e.tensor_tensor(
                    out=m2b(h), in0=m2b(h), in1=Dident[:, 128 * h:128 * (h + 1)],
                    op=ALU.add))
                mVm[h] = kb.mk("vector")
                kb.op("vector", lambda e, h=h, xsb=xsb: e.tensor_scalar_mul(
                    out=wxb(h), in0=xsb[:, 256 * h:256 * (h + 1)],
                    scalar1=wcolc(h)),
                    wait=[w for w in ([mS2[h], mTx.get(h - 2)] +
                                      ([wxs_] if h == 0 else []))
                          if w is not None])
                mVw[h] = kb.mk("vector")
                kb.wt("tensor", mVm[h], mVy.get(h - 1), mVh_prev.get(h))
                if h == 0:
                    kb.wt("tensor", wxs_)
                for dh2 in range(2):
                    kb.op("tensor", lambda e, h=h, dh2=dh2, xsb=xsb: e.matmul(
                        out=(pb2 if dh2 == 0 else pb3)[:, 0:128],
                        lhsT=xsb[:, 256 * h + 128 * dh2:256 * h + 128 * (dh2 + 1)],
                        rhs=m2b(h), start=True, stop=False))
                    kb.op("tensor", lambda e, h=h, dh2=dh2: e.matmul(
                        out=(pb2 if dh2 == 0 else pb3)[:, 0:128],
                        lhsT=Hst[:, 256 * h + 128 * dh2:256 * h + 128 * (dh2 + 1)],
                        rhs=cmeb(h), start=False, stop=True))
                mTy[h] = kb.mk("tensor")
                kb.wt("tensor", mVw[h], mVh.get(h - 1))
                kb.op("tensor", lambda e, h=h, cc=cc: e.matmul(
                    out=pA[0:64, 0:256], lhsT=Btm[:, 64 * cc:64 * (cc + 1)],
                    rhs=wxb(h), start=True, stop=True))
                mTx[h] = kb.mk("tensor")
                ygw = [mTy[h]] + \
                    ([wgb, w_xst0, w_xst1] if h == 0 else []) + \
                    ([mTo_last] if mTo_last is not None and h == 0 else [])
                for dh2 in range(2):
                    ygo = 8192 + 512 * (2 * h + dh2) + 128 * (cc % 4)
                    kb.op("vector", lambda e, h=h, dh2=dh2, ygo=ygo, gb=gb:
                          e.tensor_tensor(
                              out=A_bf[:, ygo:ygo + 128],
                              in0=(pb2 if dh2 == 0 else pb3)[:, 0:128],
                              in1=gb[:, 128 * (2 * h + dh2):
                                     128 * (2 * h + dh2 + 1)],
                              op=ALU.mult),
                          wait=ygw if dh2 == 0 else ())
                mVy[h] = kb.mk("vector")
                kb.op("vector", lambda e, h=h: e.scalar_tensor_tensor(
                    out=Hst[:, 256 * h:256 * (h + 1)],
                    in0=Hst[:, 256 * h:256 * (h + 1)], scalar=gamc(h)[0:64, :],
                    in1=pA[0:64, 0:256], op0=ALU.mult, op1=ALU.add),
                    wait=[mTx[h]])
                mVh[h] = kb.mk("vector")
            hist[cc] = (mTx[7], mVw[7], mVy[7])
            if cc % 4 == 3:
                kb.wt("tensor", ld_wo, mVy[7], mVo.get(6), mVo.get(7))
                tt4 = cc // 4
                mTo = {}
                for m in range(NCH):
                    op_ps = pA[:, 512:1024] if m % 2 == 0 else pA[:, 1024:1536]
                    kb.wt("tensor", mVo.get(m - 2))
                    for kc in range(16):
                        kb.op("tensor", lambda e, m=m, kc=kc, op_ps=op_ps:
                              e.matmul(
                                  out=op_ps,
                                  lhsT=wslc(0, kc, 1024, 128, 128 * m),
                                  rhs=A_bf[:, 8192 + 512 * kc:
                                           8192 + 512 * (kc + 1)],
                                  start=(kc == 0), stop=(kc == 15)))
                    mTo[m] = kb.mk("tensor")
                    kb.op("vector", lambda e, m=m, tt4=tt4, op_ps=op_ps:
                          e.scalar_tensor_tensor(
                              out=xch(m)[:, 512 * tt4:512 * (tt4 + 1)],
                              in0=op_ps, scalar=cols[:, 74 + m:75 + m],
                              in1=xch(m)[:, 512 * tt4:512 * (tt4 + 1)],
                              op0=ALU.add, op1=ALU.add), wait=[mTo[m]])
                    mVo[m] = kb.mk("vector")
                mTo_last = mTo[NCH - 1]
        kb.bar(("tensor", "vector", "scalar"))

    for _eng in ("tensor", "vector", "scalar", "gpsimd"):
        for _w in (ld_x, ld_const, ld_cb, ld_mc):
            kb.plan[_eng].append(lambda e, w=_w: e.wait_ge(kb.sem(w[0]), w[1]))
    STAGE = int(os.environ.get("ATH_MK_STAGE", "99"))
    boundary_stats()
    if STAGE >= 2:
        setup_bias_qkv()
        w_y = ffn_ustage(1)
    if STAGE >= 3:
        ffn_route(1)
    if STAGE >= 4:
        ffn_light(1)
    if STAGE >= 5:
        ffn_heavy(1)
    if STAGE >= 6:
        boundary_stats()
        mamba()

    if DEBUG:
        dum = kb.debug_out("um", (D, T), BF16)
        dg = kb.debug_out("g", (INNER, T), BF16)
        dxs = kb.debug_out("xs", (T, INNER), BF16)
        dp8 = kb.debug_out("p8", (8, T))
        dbfm = kb.debug_out("bfm", (64, T), BF16)
        dcfm = kb.debug_out("cfm", (64, T), BF16)
        kb.dma("gpsimd", "gdbg3", dum, um_dram)
        kb.dma("gpsimd", "gdbg3", dg, g_dram)
        kb.dma("gpsimd", "gdbg3", dxs, xs_dram)
        kb.dma("sync", "gdbg4", dp8, repf[0:8, :], wait=[kb.wv("bar_vector")])
        kb.dma("sync", "gdbg4", dbfm, Bfm)
        kb.dma("sync", "gdbg4", dcfm, Cfm)
        for _e in ("vector", "scalar", "tensor", "gpsimd"):
            for _g in ("gdbg3", "gdbg4"):
                kb.plan[_e].append(lambda e, w=kb.gv(_g): e.wait_ge(
                    kb.sem(w[0]), w[1]))

    if STAGE >= 7:
        boundary_stats()
        _wl3 = kb.wv("bar_tensor")
        load_w("gw1b", WQKV_O, wqkv_d, 8, 1536, wait=[_wl3])
        ld_w1b = load_w("gw1b", WOUTL_O, woutl_d, 4, 1024)
        kb.plan["tensor"].append(lambda e, w=ld_w1b: e.wait_ge(kb.sem(w[0]), w[1]))
        ffn_ustage(3)
        ffn_route(3)
    if STAGE >= 8:
        ffn_light(3)
    if STAGE >= 9:
        ffn_heavy(3)

    if DEBUG:
        dmu = kb.debug_out("mu", (4, 512))
        dvar = kb.debug_out("var", (4, 512))
        dinv = kb.debug_out("inv", (4, 512))
        dxc = kb.debug_out("xc", (128, NCH * T), BF16)
        dul = kb.debug_out("ul", (128, NCH * T), BF16)
        dy = kb.debug_out("y", (T, D), BF16)
        didx = kb.debug_out("idx", (1664, 4))
        kb.dma("sync", "gdbg", dmu, statr(SL_MU))
        kb.dma("sync", "gdbg", dvar, statr(SL_VAR))
        kb.dma("sync", "gdbg", dinv, statr(SL_INV))
        kb.dma("sync", "gdbg", dxc, xc_sb[:])
        kb.dma("sync", "gdbg", dul, A_bf[:, 0:16384])
        kb.dma("gpsimd", "gdbg2", dy, y_dram, wait=w_y)
        kb.dma("gpsimd", "gdbg2", didx, idx_dram)

    kb.bar(("vector", "scalar", "tensor"))
    wfin = [kb.wv("bar_vector"), kb.wv("bar_scalar"), kb.wv("bar_tensor")]
    out_r = out_d.rearrange("(c p) t -> p c t", p=128)
    for i in range(4):
        kb.dma("sync", "gout", out_r[:, 2 * i:2 * i + 2, :],
               x_r[:, 2 * i:2 * i + 2, :], wait=wfin)
    fin = [kb.gv("gout")] + ([kb.gv("gdbg"), kb.gv("gdbg2"), kb.gv("gdbg3"),
                              kb.gv("gdbg4")] if DEBUG else [])
    kb.op("sync", lambda e: e.nop(), wait=fin)
    return kb


def make_consts():
    c = {}
    c["ident"] = np.eye(128, dtype=np.float32)
    c["onesrow"] = np.ones((1, 128), np.float32)
    e4 = np.zeros((128, 16), np.float32)
    for tt in range(4):
        e4[:, 4 * tt + tt] = 1.0
    c["e4"] = e4
    ind2 = np.zeros((128, 2), np.float32)
    ind2[0:64, 0] = 1.0
    ind2[64:128, 1] = 1.0
    c["ind2"] = ind2
    c["ind2t"] = ind2.T.copy()
    iota = (np.arange(16)[None, :] * 128 + np.arange(128)[:, None]).astype(np.float32)
    c["iota_t_in"] = iota
    u1 = np.zeros((2, 128), np.float32)
    u1[0, 0:64] = 1.0
    u1[1, 64:128] = 1.0
    c["maskA_l_in"] = u1
    vA = np.zeros((2, 256), np.float32)
    vA[0, 192:256] = -1e9
    vA[1, 0:64] = -1e9
    c["maskA_r_in"] = vA
    c["maskT_r_in"] = vA.copy()
    c["logk_in"] = np.log(np.array([[512 * 9.0 / 8.0, 1024 * 9.0 / 8.0]], np.float32))
    pc = np.zeros((128, 2), np.float32)
    pc[0:64, 0] = -1e9
    pc[64:128, 1] = -1e9
    c["padcolT_in"] = pc
    fc = np.zeros((128, 4), np.float32)
    fc[:, 0] = 1e-5
    fc[:, 1] = 1.0
    fc[:, 3] = 1e-12
    c["fconst"] = fc
    c["iota2_in"] = (np.arange(128)[:, None] * 16 + np.arange(16)[None, :]
                     ).astype(np.float32)
    c["jmat"] = np.broadcast_to(np.arange(16, dtype=np.float32), (128, 16)).copy()
    c["neg1e9"] = np.full((1, 256), -1e9, np.float32)
    eg = np.zeros((2, 256), np.float32)
    eg[0, 0:64] = -1e9
    eg[1, 192:256] = -1e9
    c["edgeLR"] = eg
    lk = np.log(np.array([576.0, 1152.0], np.float32)).astype(np.float32)
    c["skinit"] = (lk - np.float32(np.log(np.float32(2048.0)))).reshape(1, 2)
    s_i = np.arange(128)[:, None]
    t_i = np.arange(128)[None, :]
    c["cmask_in"] = np.where(s_i <= t_i, 1.0, 0.0).astype(np.float32)
    return c


def kernel(L0_attn_input_tensor, L0_attn_ln_gamma, L0_attn_ln_beta, L0_attn_W_in,
           L0_attn_b_in, L0_attn_A_log, L0_attn_D_skip, L0_attn_dt_bias,
           L0_attn_W_out, L0_attn_b_out, L0_ffn_ln_w, L0_ffn_ln_b, L0_ffn_Wqkv,
           L0_ffn_Wout_l, L0_ffn_rt_q, L0_ffn_rt_kv, L0_ffn_gamma, L0_ffn_Wq,
           L0_ffn_Wkv, L0_ffn_Wout_h, L0_ffn_null_kv, L0_ffn_null_q):
    f32 = lambda a: np.ascontiguousarray(np.asarray(a), dtype=np.float32)
    x = f32(L0_attn_input_tensor)
    colchunk = lambda v: np.ascontiguousarray(f32(v).reshape(-1, 128).T)
    shared = {
        "wqkv": np.ascontiguousarray(f32(L0_ffn_Wqkv).T),
        "woutl": np.ascontiguousarray(f32(L0_ffn_Wout_l).T),
        "wouth": np.ascontiguousarray(f32(L0_ffn_Wout_h).T),
        "wq": np.ascontiguousarray(f32(L0_ffn_Wq).T),
        "wkv": np.ascontiguousarray(
            f32(L0_ffn_Wkv).T.reshape(1024, 8, 2, 64).transpose(0, 2, 1, 3)
            .reshape(1024, 1024)),
        "win": np.ascontiguousarray(np.pad(f32(L0_attn_W_in), ((0, 0), (0, 120)))),
        "wout": f32(L0_attn_W_out),
        "lnw_c": colchunk(L0_ffn_ln_w),
        "lnb_c": colchunk(L0_ffn_ln_b),
        "mgam_c": colchunk(L0_attn_ln_gamma),
        "mbet_c": colchunk(L0_attn_ln_beta),
        "hgam_c": colchunk(L0_ffn_gamma),
        "bin_c": colchunk(np.pad(f32(L0_attn_b_in), (0, 120))),
        "bout_c": colchunk(L0_attn_b_out),
        "alog": f32(L0_attn_A_log).reshape(1, 8),
        "dskip": f32(L0_attn_D_skip).reshape(1, 8),
        "dtbias": f32(L0_attn_dt_bias).reshape(1, 8),
        "m8c_in": np.ascontiguousarray(np.stack(
            [f32(L0_attn_A_log), f32(L0_attn_D_skip), f32(L0_attn_dt_bias),
             np.zeros(8, np.float32)], axis=1)),
        "binr_in": np.ascontiguousarray(
            np.pad(f32(L0_attn_b_in), (0, 120))[2048:4352].reshape(1, 2304)),
        "rt2": np.ascontiguousarray(
            np.stack([f32(L0_ffn_rt_q).ravel(), f32(L0_ffn_rt_kv).ravel()], axis=1)),
        "nullqrep_in": np.ascontiguousarray(
            np.broadcast_to(f32(L0_ffn_null_q), (128, D))),
        "nullvrow_in": f32(L0_ffn_null_kv)[1].reshape(1, 512),
    }
    nk = f32(L0_ffn_null_kv)
    nullcol = np.zeros((D,), np.float32)
    for h in range(8):
        nullcol[128 * h:128 * h + 64] = nk[0, h]
        nullcol[128 * h + 64:128 * h + 128] = nk[1, h]
    shared["nullcol_c"] = colchunk(nullcol)
    shared.update(make_consts())

    kb = build()
    nc = kb.finish()
    in_maps = []
    for b in range(8):
        m = dict(shared)
        m["x0"] = np.ascontiguousarray(x[b].T)
        in_maps.append(m)
    res = run_bass_kernel_spmd(nc, in_maps, core_ids=list(range(8)))
    global _last_results
    _last_results = res.results[0]
    out = np.stack([np.ascontiguousarray(r["out"].T) for r in res.results], axis=0)
    return out


_last_results = None


# version 30 (final)
# speedup vs baseline: 1.0180x; 1.0180x over previous
import os
import numpy as np
from contextlib import ExitStack
import concourse.bass as bass
import concourse.mybir as mybir
from concourse.bass_utils import run_bass_kernel_spmd

F32 = mybir.dt.float32
BF16 = mybir.dt.bfloat16
I32 = mybir.dt.int32
AF = mybir.ActivationFunctionType
ALU = mybir.AluOpType
AX = mybir.AxisListType
IOA = bass.IndirectOffsetOnAxis

T = 2048
D = 1024
NCH = D // 128
NTT = T // 512
NTC = T // 128
INNER = 2048
NQ, NKV = 512, 1024
EPS_LN = 1e-5

DEBUG = False


class KB:

    ENGS = ("tensor", "vector", "scalar", "gpsimd", "sync")

    def __init__(self):
        self.nc = bass.Bass("TRN2", target_bir_lowering=False)
        self.ctx = ExitStack()
        self.plan = {e: [] for e in self.ENGS}
        self.sems = {}
        self.semv = {}
        self.dbg = {}

    def sem(self, name):
        if name not in self.sems:
            self.sems[name] = self.ctx.enter_context(self.nc.semaphore(name))
            self.semv[name] = 0
        return self.sems[name]

    def sb(self, name, shape, dt=F32):
        return self.ctx.enter_context(self.nc.sbuf_tensor(name, list(shape), dt))

    def psum(self, name, shape, dt=F32):
        return self.ctx.enter_context(self.nc.psum_tensor(name, list(shape), dt))

    def op(self, eng, fn, wait=(), inc=None, incn=1):
        if inc is not None:
            self.sem(inc)
            self.semv[inc] += incn
            target = self.semv[inc]
        waits = tuple(wait)
        cell = {"extra": None}

        def closure(e, fn=fn, waits=waits, inc=inc, incn=incn, cell=cell):
            for s, v in waits:
                e.wait_ge(self.sems[s], v)
            ins = fn(e)
            if inc is not None:
                assert ins is not None
                ins.then_inc(self.sems[inc], incn)
            elif cell["extra"] is not None:
                assert ins is not None, "bar attached to non-instruction op"
                ins.then_inc(self.sems[cell["extra"][0]], cell["extra"][1])
        self.plan[eng].append(closure)
        if not hasattr(self, "_last"):
            self._last = {}
        self._last[eng] = (cell, inc is not None)
        if inc is not None:
            return (inc, target)
        return None

    def dma(self, eng, group, out, in_, wait=(), **kw):
        return self.op(eng, lambda e: e.dma_start(out=out, in_=in_, **kw),
                       wait=wait, inc="g_" + group, incn=16)

    def dmat(self, group, out, in_, wait=()):
        return self.op("sync", lambda e: e.dma_start_transpose(out, in_),
                       wait=wait, inc="g_" + group, incn=16)

    def gv(self, group):
        return ("g_" + group, self.semv.get("g_" + group, 0))

    def bar(self, producers, waiters=None):
        targets = []
        for eng in producers:
            bn = "bar_" + eng
            self.sem(bn)
            cell, has_inc = self._last.get(eng, (None, True))
            if cell is None or has_inc or cell["extra"] is not None:
                if eng == "vector":
                    def _fv(e):
                        e.drain()
                        return e.memset(self.dummy["vector"][:], 0.0)
                    self.op("vector", _fv, inc=bn)
                elif eng in ("scalar", "gpsimd"):
                    def _fs(e, eng=eng):
                        e.drain()
                        return e.memzero(self.dummy[eng][:])
                    self.op(eng, _fs, inc=bn)
                elif eng == "tensor":
                    self.op("tensor", self.tdummy, inc=bn)
                else:
                    raise AssertionError(f"bar: no op to carry inc on {eng}")
            else:
                self.semv[bn] += 1
                cell["extra"] = (bn, 1)
            targets.append((bn, self.semv[bn]))
        for eng in (waiters if waiters is not None else self.ENGS):
            for bn, v in targets:
                self.plan[eng].append(
                    lambda e, bn=bn, v=v: e.wait_ge(self.sems[bn], v))
        return targets

    def wv(self, semname):
        return (semname, self.semv.get(semname, 0))

    def mk(self, eng):
        return self.bar((eng,), waiters=())[0]

    def wt(self, eng, *ws):
        for w in ws:
            if w is None:
                continue
            self.plan[eng].append(
                lambda e, w=w: e.wait_ge(self.sem(w[0]), w[1]))

    def debug_out(self, name, shape, dt=F32):
        h = self.nc.dram_tensor("dbg_" + name, list(shape), dt, kind="ExternalOutput")
        self.dbg[name] = h.ap()
        return self.dbg[name]

    def finish(self):
        plan = self.plan
        with self.ctx:
            with self.nc.Block() as block:
                @block.tensor
                def _(e):
                    for fn in plan["tensor"]:
                        fn(e)

                @block.vector
                def _(e):
                    for fn in plan["vector"]:
                        fn(e)

                @block.scalar
                def _(e):
                    for fn in plan["scalar"]:
                        fn(e)

                @block.gpsimd
                def _(e):
                    for fn in plan["gpsimd"]:
                        fn(e)

                @block.sync
                def _(e):
                    for fn in plan["sync"]:
                        fn(e)
        return self.nc


def build():
    kb = KB()
    nc = kb.nc
    kb.dummy = {e: kb.sb("dummy_" + e, (1, 8), F32)
                for e in ("vector", "scalar", "gpsimd")}
    kb.dummyb = kb.sb("dummyb_sb", (128, 8), BF16)
    kb.tdummy = lambda e: e.transpose(out=kb.pdummy[0:8, 512:520],
                                      in_=kb.dummyb[0:8, 0:8],
                                      identity=kb.dummyb[0:8, 0:8])
    dram = lambda n, shp, dt=F32, kind="ExternalInput": nc.dram_tensor(
        n, list(shp), dt, kind=kind).ap()

    x0_d = dram("x0", (D, T))
    out_d = dram("out", (D, T), kind="ExternalOutput")
    wqkv_d = dram("wqkv", (D, 1536))
    woutl_d = dram("woutl", (512, D))
    wq_d = dram("wq", (D, 512))
    wkv_d = dram("wkv", (D, 1024))
    wouth_d = dram("wouth", (512, D))
    win_d = dram("win", (D, 4352))
    wout_d = dram("wout", (INNER, D))
    lnw_d = dram("lnw_c", (128, 8))
    lnb_d = dram("lnb_c", (128, 8))
    mgam_d = dram("mgam_c", (128, 8))
    mbet_d = dram("mbet_c", (128, 8))
    hgam_d = dram("hgam_c", (128, 8))
    bin_d = dram("bin_c", (128, 34))
    bout_d = dram("bout_c", (128, 8))
    alog_d = dram("alog", (1, 8))
    dskip_d = dram("dskip", (1, 8))
    dtb_d = dram("dtbias", (1, 8))
    rt2_d = dram("rt2", (D, 2))
    nullcol_d = dram("nullcol_c", (128, 8))
    nullvrow_d = dram("nullvrow_in", (1, 512))
    nullqrep_d = dram("nullqrep_in", (128, D))
    ident_d = dram("ident", (128, 128))
    ones_d = dram("onesrow", (1, 128))
    e4_d = dram("e4", (128, 16))
    ind2_d = dram("ind2", (128, 2))
    ind2t_d = dram("ind2t", (2, 128))
    iota_t_d = dram("iota_t_in", (128, 16))
    maskA_l_d = dram("maskA_l_in", (2, 128))
    maskA_r_d = dram("maskA_r_in", (2, 256))
    maskT_r_d = dram("maskT_r_in", (2, 256))
    logk_d = dram("logk_in", (1, 2))
    padcolT_d = dram("padcolT_in", (128, 2))
    fconst_d = dram("fconst", (128, 4))
    iota2_d = dram("iota2_in", (128, 16))
    skinit_d = dram("skinit", (1, 2))
    jmat_d = dram("jmat", (128, 16))
    neg1e9_d = dram("neg1e9", (1, 256))
    edgeLR_d = dram("edgeLR", (2, 256))
    y_dram = dram("y_scr", (T, D), BF16, kind="Internal")
    rows_dram = dram("rows_scr", (16, T), F32, kind="Internal")
    idx_dram = dram("idx_scr", (1664, 4), F32, kind="Internal")
    hs_dram = dram("hs_scr", (T, D), BF16, kind="Internal")

    x_sb = kb.sb("x_sb", (128, NCH * T), F32)
    xc_sb = kb.sb("xc_sb", (128, NCH * T), BF16)
    repf = kb.sb("repf", (128, T), F32)
    repb = kb.sb("repb", (128, T), BF16)
    stat = kb.sb("stat", (4, 6 * 512), F32)
    SL_MU, SL_VAR, SL_INV, SL_S2, SL_T0, SL_T1 = range(6)
    cols = kb.sb("cols", (128, 96), F32)
    colsb = kb.sb("colsb", (128, 96), BF16)
    onescol_b = kb.sb("onescol_b", (128, 1), BF16)
    onescol_f = kb.sb("onescol_f", (128, 1), F32)
    e4_b = kb.sb("e4_b", (128, 16), BF16)
    identb = kb.sb("identb", (128, 128), BF16)
    identf = kb.sb("identf", (128, 128), F32)
    onesrow_b = kb.sb("onesrow_b", (1, 128), BF16)
    ind2_b = kb.sb("ind2_b", (128, 2), BF16)
    ind2t_b = kb.sb("ind2t_b", (2, 128), BF16)
    ptgt2 = kb.sb("ptgt2", (128, 16), I32)
    maskA_l = kb.sb("maskA_l", (2, 128), BF16)
    maskA_r = kb.sb("maskA_r", (2, 256), BF16)
    maskT_r = kb.sb("maskT_r", (2, 256), BF16)
    logk_row = kb.sb("logk_row", (1, 2), F32)
    padcolT = kb.sb("padcolT", (128, 2), F32)
    fconst = kb.sb("fconst_sb", (128, 4), F32)
    nullq_bh = kb.sb("nullq_bh", (128, D), BF16)
    nullvrow_b = kb.sb("nullvrow_b", (1, 512), BF16)
    nullcol_b = kb.sb("nullcol_b", (128, 8), BF16)
    smallrows = kb.sb("smallrows", (1, 32), F32)
    rt2_sb = kb.sb("rt2_sb", (128, 16), F32)

    w_arena = kb.sb("w_arena", (128, 18432), BF16)
    A_bf = kb.sb("A_bf", (128, 18432), BF16)
    ab = lambda off, n: A_bf[:, off:off + n]
    xs_scr = lambda off, n: xc_sb[:, off:off + n]

    pA = kb.psum("pA", (128, 1536), F32)
    pb0 = kb.psum("pb0", (128, 512), F32)
    pb1 = kb.psum("pb1", (128, 512), F32)
    pb2 = kb.psum("pb2", (128, 512), F32)
    pb3 = kb.psum("pb3", (128, 512), F32)
    pT = kb.psum("pT", (128, 1024), BF16)
    kb.pdummy = pT

    xch = lambda c: x_sb[:, c * T:(c + 1) * T]
    xcch = lambda c: xc_sb[:, c * T:(c + 1) * T]
    statr = lambda s: stat[:, s * 512:(s + 1) * 512]
    stat_tile = lambda s, tt: stat[tt:tt + 1, s * 512:(s + 1) * 512]

    iota2 = kb.sb("iota2", (128, 16), F32)
    ptgt_i = kb.sb("ptgt_i", (128, 16), I32)
    gld_f = kb.sb("gld_f", (128, 48), F32)
    zrow = kb.sb("zrow", (128, 52), F32)
    onesrow_f = kb.sb("onesrow_f", (1, 128), F32)
    skinit_row = kb.sb("skinit_row", (1, 2), F32)
    jmat = kb.sb("jmat_sb", (128, 16), F32)
    zrow_r = zrow[:].rearrange("p (g k) -> p g k", k=4)

    nmlb_dram = dram("nmlb_scr", (2, 4096), BF16, kind="Internal")
    negm_seg = kb.sb("negm_seg", (128, 32), F32)
    s_seg = kb.sb("s_seg", (128, 32), F32)
    nmlT = kb.sb("nmlT", (32, 128), F32)
    nmlT_h = kb.sb("nmlT_h", (32, 128), BF16)
    nmlT_l = kb.sb("nmlT_l", (32, 128), BF16)
    zrow_b = kb.sb("zrow_b", (1, 512), BF16)
    neg1e9_row = kb.sb("neg1e9_row", (1, 256), BF16)
    bv_row = kb.sb("bv_row", (1, 512), BF16)
    u2row = kb.sb("u2row", (1, 128), BF16)
    edgeL_row = kb.sb("edgeL_row", (1, 256), BF16)
    edgeR_row = kb.sb("edgeR_row", (1, 256), BF16)
    bqkv_col = kb.sb("bqkv_col", (128, 12), F32)
    bvT = kb.sb("bvT", (4, 128), F32)
    bq_scaled = kb.sb("bq_scaled", (128, 4), F32)

    x0_r = x0_d.rearrange("(c p) t -> p c t", p=128)
    x_r = x_sb[:].rearrange("p (c t) -> p c t", c=NCH)
    for i in range(4):
        kb.dma("sync", "gx", x_r[:, 2 * i:2 * i + 2, :], x0_r[:, 2 * i:2 * i + 2, :])
    ld_x = kb.gv("gx")

    kb.dma("sync", "gc", identf[:], ident_d)
    kb.dma("sync", "gc", logk_row[:], logk_d)
    kb.dma("sync", "gc", padcolT[:], padcolT_d)
    kb.dma("sync", "gc", fconst[:], fconst_d)
    kb.dma("sync", "gc", cols[:, 0:8], lnw_d)
    kb.dma("sync", "gc", cols[:, 8:16], lnb_d)
    kb.dma("sync", "gc", cols[:, 16:24], mgam_d)
    kb.dma("sync", "gc", cols[:, 24:32], mbet_d)
    kb.dma("sync", "gc", cols[:, 32:40], hgam_d)
    kb.dma("sync", "gc", cols[:, 40:74], bin_d)
    kb.dma("sync", "gc", cols[:, 74:82], bout_d)
    kb.dma("sync", "gc", smallrows[0:1, 0:8], alog_d)
    kb.dma("sync", "gc", smallrows[0:1, 8:16], dskip_d)
    kb.dma("sync", "gc", smallrows[0:1, 16:24], dtb_d)
    kb.dma("sync", "gc", onescol_f[:], ones_d.rearrange("r p -> p r"))
    kb.dma("sync", "gc", onesrow_f[:], ones_d)
    kb.dma("sync", "gc", iota2[:], iota2_d)
    kb.dma("sync", "gc", jmat[:], jmat_d)
    kb.dma("sync", "gc", skinit_row[:], skinit_d)
    kb.dma("sync", "gc", rt2_sb[:].rearrange("p (c r) -> p c r", c=8),
           rt2_d.rearrange("(c p) r -> p c r", p=128))
    ld_const = kb.gv("gc")

    for dst, src in ((identb[:], ident_d), (onescol_b[:], ones_d.rearrange("r p -> p r")),
                     (e4_b[:], e4_d), (onesrow_b[:], ones_d), (ind2_b[:], ind2_d),
                     (ind2t_b[:], ind2t_d), (maskA_l[:], maskA_l_d),
                     (maskA_r[:], maskA_r_d), (maskT_r[:], maskT_r_d),
                     (neg1e9_row[:], neg1e9_d), (u2row[:], maskA_l_d[1:2, :]),
                     (edgeL_row[:], edgeLR_d[0:1, :]), (edgeR_row[:], edgeLR_d[1:2, :]),
                     (nullvrow_b[:], nullvrow_d), (nullcol_b[:], nullcol_d),
                     (kb.dummyb[:], ident_d[:, 0:8])):
        kb.dma("gpsimd", "gcb", dst, src)
    ld_cb = kb.gv("gcb")

    WQKV_O, WOUTL_O = 0, 12288
    WQ_O, WKV_O, WOUTH_O = 0, 4096, 12288
    WINZ_O, WINX_O, WOUT_O = 0, 0, 0

    def load_w(grp, off, src_d, nk, f, wait=()):
        kb.dma("gpsimd", grp,
               w_arena[:, off:off + nk * f].rearrange("p (c f) -> p c f", c=nk),
               src_d.rearrange("(c p) f -> p c f", p=128), wait=wait)
        return kb.gv(grp)

    load_w("gw1", WQKV_O, wqkv_d, 8, 1536)
    ld_w1 = load_w("gw1", WOUTL_O, woutl_d, 4, 1024)
    wslc = lambda off, kc, fdim, ncols=128, col0=0: w_arena[
        :, off + kc * fdim + col0: off + kc * fdim + col0 + ncols]

    kb.dma("gpsimd", "gnq", nullq_bh[:], nullqrep_d)
    ld_cb2 = kb.gv("gnq")
    kb.op("scalar", lambda e: e.activation(out=nullq_bh[:], in_=nullq_bh[:],
                                           func=AF.Identity, scale=0.5),
          wait=[ld_cb2])
    kb.bar(("scalar",))

    kb.op("vector", lambda e: e.memset(zrow[:], 0.0), wait=[ld_const])
    kb.bar(("vector",))

    sqb0 = ab(0, 2048)
    sqb1 = ab(2048, 2048)

    def boundary_stats():
        for c in range(NCH):
            kb.op("vector", lambda e, c=c: e.tensor_copy(out=xcch(c), in_=xch(c)))
        kb.bar(("vector",))
        for tt in range(NTT):
            for c in range(NCH):
                kb.op("tensor", lambda e, tt=tt, c=c: e.matmul(
                    out=pb0[0:4, :], lhsT=e4_b[:, 4 * tt:4 * tt + 4],
                    rhs=xcch(c)[:, tt * 512:(tt + 1) * 512],
                    start=(tt == 0 and c == 0), stop=(tt == 3 and c == 7)))
        kb.bar(("tensor",))
        kb.op("scalar", lambda e: e.activation(out=statr(SL_MU), in_=pb0[0:4, :],
                                               func=AF.Identity, scale=1.0 / D))
        kb.bar(("scalar",))
        kb.op("vector", lambda e: e.tensor_scalar_mul(out=statr(SL_T0),
                                                      in0=statr(SL_MU), scalar1=-1.0))
        kb.bar(("vector",))
        kb.dma("sync", "gsmu", rows_dram[0:1, :].rearrange("r (p j) -> (r p) j", p=4),
               statr(SL_T0))
        w1 = kb.gv("gsmu")
        kb.dma("gpsimd", "gbmu", repf[:], rows_dram[0:1, :].to_broadcast((128, T)),
               wait=[w1])
        w2 = kb.gv("gbmu")
        for c in range(NCH):
            kb.op("vector", lambda e, c=c: e.tensor_tensor(
                out=xcch(c), in0=xch(c), in1=repf[:], op=ALU.add),
                wait=[w2] if c == 0 else ())
        kb.bar(("vector",))
        sq = (sqb0, sqb1)
        for c in range(NCH):
            buf = sq[c % 2]
            w_mm = (("sqmm", kb.semv.get("sqmm", 0)),) if c >= 2 else ()
            wv = kb.op("scalar", lambda e, buf=buf, c=c: e.activation(
                out=buf, in_=xcch(c), func=AF.Square), wait=w_mm, inc="sq")
            for tt in range(NTT):
                kb.op("tensor", lambda e, buf=buf, tt=tt, c=c: e.matmul(
                    out=pb1[0:4, :], lhsT=e4_b[:, 4 * tt:4 * tt + 4],
                    rhs=buf[:, tt * 512:(tt + 1) * 512],
                    start=(c == 0 and tt == 0), stop=(c == 7 and tt == 3)),
                    wait=[wv] if tt == 0 else (),
                    inc="sqmm" if tt == 3 else None)
        kb.bar(("tensor",))
        kb.op("scalar", lambda e: e.activation(out=statr(SL_VAR), in_=pb1[0:4, :],
                                               func=AF.Identity, scale=1.0 / D))
        kb.op("scalar", lambda e: e.drain())
        kb.op("scalar", lambda e: e.activation(out=statr(SL_T1), in_=statr(SL_VAR),
                                               func=AF.Ln, bias=fconst[0:4, 0:1]))
        kb.op("scalar", lambda e: e.drain())
        kb.op("scalar", lambda e: e.activation(out=statr(SL_INV), in_=statr(SL_T1),
                                               func=AF.Exp, scale=-0.5))
        kb.bar(("scalar",))

    def bcast_row(slot, dram_slot, dest, grp, wait_extra=()):
        kb.dma("sync", "gs" + grp, rows_dram[dram_slot:dram_slot + 1, :].rearrange(
            "r (p j) -> (r p) j", p=4), statr(slot), wait=wait_extra)
        w1 = kb.gv("gs" + grp)
        kb.dma("gpsimd", "gb" + grp, dest,
               rows_dram[dram_slot:dram_slot + 1, :].to_broadcast((128, T)), wait=[w1])
        return kb.gv("gb" + grp)

    yf_dram = dram("yf_scr", (D, T), BF16, kind="Internal")
    ytm0 = ab(16384, 1024)
    ytm1 = ab(17408, 1024)
    inv_tm = kb.sb("inv_tm", (128, 16), F32)
    ulch = lambda c: ab(c * T, T)

    def ffn_ustage(call):
        tg = f"f{call}"
        kb.op("vector", lambda e: e.tensor_tensor(out=statr(SL_T0), in0=statr(SL_INV),
                                                  in1=statr(SL_INV), op=ALU.mult))
        kb.op("vector", lambda e: e.drain())
        kb.op("vector", lambda e: e.tensor_tensor(out=statr(SL_T0), in0=statr(SL_T0),
                                                  in1=statr(SL_VAR), op=ALU.mult))
        kb.bar(("vector",))
        kb.op("scalar", lambda e: e.activation(out=statr(SL_T1), in_=statr(SL_T0),
                                               func=AF.Ln, bias=fconst[0:4, 0:1]))
        kb.op("scalar", lambda e: e.drain())
        kb.op("scalar", lambda e: e.activation(out=statr(SL_T1), in_=statr(SL_T1),
                                               func=AF.Exp, scale=-0.5))
        kb.op("scalar", lambda e: e.drain())
        kb.bar(("scalar",))
        kb.op("vector", lambda e: e.tensor_tensor(out=statr(SL_S2), in0=statr(SL_INV),
                                                  in1=statr(SL_T1), op=ALU.mult))
        kb.bar(("vector",))
        w_rep = bcast_row(SL_S2, 2, repb[:], "sl")
        for j in range(4):
            kb.op("tensor", lambda e, j=j: e.transpose(
                out=pb2[0:128, 0:16].rearrange("p (a b) -> p a b", b=4)[:, :, j],
                in_=stat[0:4, SL_INV * 512 + 128 * j:SL_INV * 512 + 128 * (j + 1)],
                identity=identf[0:4, 0:4]))
        kb.bar(("tensor",))
        kb.op("scalar", lambda e: e.copy(out=inv_tm[:], in_=pb2[0:128, 0:16]),
              inc="invtm")
        w_invtm = kb.wv("invtm")
        for c in range(NCH):
            kb.op("vector", lambda e, c=c: e.scalar_tensor_tensor(
                out=ulch(c), in0=xcch(c), scalar=cols[:, c:c + 1],
                in1=repb[:], op0=ALU.mult, op1=ALU.mult),
                wait=[w_rep] if c == 0 else ())
        kb.bar(("vector",))
        xc_r = xc_sb[:].rearrange("p (c t) -> p c t", c=NCH)
        yf_r = yf_dram.rearrange("(c p) t -> p c t", p=128)
        for i in range(2):
            kb.dma("sync", "gyf", yf_r[:, 4 * i:4 * i + 4, :], xc_r[:, 4 * i:4 * i + 4, :])
        w_yf = kb.gv("gyf")
        ybufs = (ytm0, ytm1)
        for tc in range(NTC):
            par = tc % 2
            buf = ybufs[par]
            wt = [w_yf] if tc < 2 else [kb.gv("gyt%d" % par)]
            kb.dmat("gytt%d" % par, buf, yf_dram[:, tc * 128:(tc + 1) * 128],
                    wait=wt)
            w_t = kb.gv("gytt%d" % par)
            kb.op("vector", lambda e, buf=buf, tc=tc: e.tensor_scalar_mul(
                out=buf, in0=buf, scalar1=inv_tm[:, tc:tc + 1]),
                wait=[w_t, w_invtm] if tc < 2 else [w_t], inc="yts")
            mYs = kb.mk("vector")
            kb.plan["sync"].append(lambda e, w=mYs: e.wait_ge(
                kb.sem(w[0]), w[1]))
            kb.dma("sync", "gyt%d" % par, y_dram[tc * 128:(tc + 1) * 128, :], buf)
        return [kb.gv("gyt0"), kb.gv("gyt1")]

    srt_q = kb.sb("srt_q", (128, 16), F32)
    srt_kv = kb.sb("srt_kv", (128, 16), F32)
    sk_sc = kb.sb("sk_sc", (128, 16 * 8), F32)
    skr = lambda i: sk_sc[:, 16 * i:16 * (i + 1)]
    sk_cols = kb.sb("sk_cols", (128, 8), F32)
    sk_small = kb.sb("sk_small", (2, 16), F32)
    sk_row = kb.sb("sk_row", (1, 128), F32)
    sk_row2 = kb.sb("sk_row2", (1, 128), F32)
    msr_col = kb.sb("msr_col", (2, 1), F32)
    scatv = kb.sb("scatv", (128, 64), F32)
    gidx = kb.sb("gidx", (128, 12), I32)
    gmeta = kb.sb("gmeta", (128, 36), F32)

    route_w = {}

    def route_ms(ridx, s_tm):
        o = 7 * ridx
        c0 = 0 if ridx == 0 else 7
        kb.op("vector", lambda e, c0=c0: e.tensor_reduce(
            out=sk_cols[:, c0:c0 + 1], in_=s_tm[:], axis=AX.X, op=ALU.max))
        mv = kb.mk("vector")
        kb.op("tensor", lambda e, c0=c0: e.transpose(
            out=pb3[0:1, 0:128], in_=sk_cols[:, c0:c0 + 1], identity=identf[:]),
            wait=[mv])
        mt = kb.mk("tensor")
        kb.op("vector", lambda e, o=o: e.tensor_reduce(
            out=sk_small[0:1, o + 6:o + 7], in_=pb3[0:1, 0:128], axis=AX.X,
            op=ALU.max), wait=[mt])
        kb.op("vector", lambda e: e.drain())
        kb.op("vector", lambda e, o=o, ridx=ridx: e.tensor_copy(
            out=sk_small[0:1, o:o + 1],
            in_=skinit_row[0:1, ridx:ridx + 1]))
        kb.op("vector", lambda e: e.drain())

    def route_iters():
        pT2 = None
        pVc = None
        pS1 = None
        pS2 = None
        for it in range(49):
            for o in (0, 7):
                kb.op("vector", lambda e, o=o: e.tensor_scalar_mul(
                    out=sk_small[0:1, o + 1:o + 2], in0=sk_small[0:1, o:o + 1],
                    scalar1=-1.0), wait=[pS2] if pS2 and o == 0 else ())
            kb.op("vector", lambda e: e.drain())
            for o in (0, 7):
                kb.op("vector", lambda e, o=o: e.tensor_tensor(
                    out=sk_small[0:1, o + 3:o + 4],
                    in0=sk_small[0:1, o + 6:o + 7],
                    in1=sk_small[0:1, o + 1:o + 2], op=ALU.min))
            kb.op("vector", lambda e: e.drain())
            for o in (0, 7):
                kb.op("vector", lambda e, o=o: e.tensor_scalar_mul(
                    out=sk_small[0:1, o + 2:o + 3], in0=sk_small[0:1, o + 3:o + 4],
                    scalar1=-1.0))
            mV1 = kb.mk("vector")
            for r, o in ((0, 0), (1, 7)):
                kb.op("tensor", lambda e, r=r, o=o: e.matmul(
                    out=pb3[0:128, 4 * r:4 * r + 2], lhsT=onesrow_f[0:1, :],
                    rhs=sk_small[0:1, o + 1:o + 3], start=True, stop=True),
                    wait=([mV1] if r == 0 else []) +
                    ([pVc] if pVc and r == 0 else []))
            mT1 = kb.mk("tensor")
            for r in (0, 1):
                kb.op("vector", lambda e, r=r: e.tensor_copy(
                    out=sk_cols[:, 1 + 3 * r:3 + 3 * r],
                    in_=pb3[0:128, 4 * r:4 * r + 2]),
                    wait=([mT1] if r == 0 else []) +
                    ([pS1] if pS1 and r == 0 else []))
            kb.op("vector", lambda e: e.drain())
            pVc = kb.mk("vector")
            for r, s_tm in ((0, srt_q), (1, srt_kv)):
                kb.op("vector", lambda e, r=r, s_tm=s_tm: e.tensor_scalar(
                    out=skr(3 + r), in0=s_tm[:],
                    scalar1=sk_cols[:, 1 + 3 * r:2 + 3 * r], scalar2=None,
                    op0=ALU.min))
            mV2 = kb.mk("vector")
            for r in (0, 1):
                kb.op("scalar", lambda e, r=r: e.activation(
                    out=skr(3 + r), in_=skr(3 + r), func=AF.Exp,
                    bias=sk_cols[:, 2 + 3 * r:3 + 3 * r],
                    accum_out=sk_cols[:, 3 + 3 * r:4 + 3 * r]),
                    wait=([mV2] if r == 0 else []) +
                    ([pT2] if pT2 and r == 0 else []))
                kb.op("scalar", lambda e: e.drain())
            mS1 = kb.mk("scalar")
            pS1 = mS1
            for r in (0, 1):
                kb.op("tensor", lambda e, r=r: e.matmul(
                    out=pb3[0:1, 130 + r:131 + r],
                    lhsT=sk_cols[:, 3 + 3 * r:4 + 3 * r], rhs=onescol_f[:],
                    start=True, stop=True),
                    wait=([mS1] if r == 0 else []) +
                    ([pS2] if pS2 and r == 0 else []))
            pT2 = kb.mk("tensor")
            for r, o in ((0, 0), (1, 7)):
                kb.op("scalar", lambda e, r=r, o=o: e.activation(
                    out=sk_small[0:1, o + 4:o + 5],
                    in_=pb3[0:1, 130 + r:131 + r], func=AF.Ln),
                    wait=[pT2] if r == 0 else ())
            kb.op("scalar", lambda e: e.drain())
            pS2 = kb.mk("scalar")
            for r, o in ((0, 0), (1, 7)):
                kb.op("vector", lambda e, r=r, o=o: e.tensor_tensor(
                    out=sk_small[0:1, o + 5:o + 6],
                    in0=logk_row[0:1, r:r + 1],
                    in1=sk_small[0:1, o + 2:o + 3], op=ALU.add),
                    wait=[pS2] if r == 0 else ())
            kb.op("vector", lambda e: e.drain())
            for o in (0, 7):
                kb.op("vector", lambda e, o=o: e.tensor_tensor(
                    out=sk_small[0:1, o:o + 1], in0=sk_small[0:1, o + 5:o + 6],
                    in1=sk_small[0:1, o + 4:o + 5], op=ALU.subtract))
            kb.op("vector", lambda e: e.drain())
        kb.bar(("vector", "scalar", "tensor"))

    def route_one(call, ridx, s_tm, kk, base, trash):
        tg = "r%d_%d" % (call, ridx)
        U = 3; EE = 3; ONES = 4; PFX = 5; CC = 6; TSC = 7; PT = 6; D_ = 0
        kb.op("tensor", lambda e, ridx=ridx: e.matmul(
            out=pb3[0:128, 2:3], lhsT=onesrow_f[0:1, :],
            rhs=sk_small[0:1, 7 * ridx:7 * ridx + 1], start=True, stop=True))
        kb.bar(("tensor",))
        kb.op("vector", lambda e: e.tensor_copy(out=sk_cols[:, 4:5],
                                                in_=pb3[0:128, 2:3]))
        kb.op("vector", lambda e: e.drain())
        kb.op("vector", lambda e: e.tensor_scalar(
            out=skr(U), in0=s_tm[:], scalar1=sk_cols[:, 4:5], scalar2=None,
            op0=ALU.add))
        kb.op("vector", lambda e: e.drain())
        kb.op("vector", lambda e: e.tensor_single_scalar(
            out=skr(ONES), in_=skr(U), scalar=0.0, op=ALU.is_ge))
        kb.op("vector", lambda e: e.tensor_scalar_min(
            out=skr(CC), in0=skr(U), scalar1=0.0))
        kb.bar(("vector",))
        kb.op("scalar", lambda e: e.activation(
            out=skr(CC), in_=skr(CC), func=AF.Exp))
        kb.bar(("scalar",))
        kb.op("vector", lambda e: e.tensor_scalar(
            out=skr(TSC), in0=skr(CC), scalar1=-1.0, scalar2=1.0,
            op0=ALU.mult, op1=ALU.add))
        kb.op("vector", lambda e: e.drain())
        kb.op("vector", lambda e: e.tensor_tensor(
            out=skr(TSC), in0=skr(TSC), in1=skr(CC), op=ALU.add))
        kb.op("vector", lambda e: e.tensor_tensor_scan(
            out=skr(PFX), data0=skr(ONES), data1=skr(ONES), initial=0.0,
            op0=ALU.add, op1=ALU.bypass))
        kb.bar(("vector",))
        kb.op("tensor", lambda e: e.transpose(
            out=pb3[0:1, 128:256], in_=skr(PFX)[:, 15:16], identity=identf[:]))
        kb.bar(("tensor",))
        kb.op("vector", lambda e: e.tensor_copy(out=sk_row2[0:1, :],
                                                in_=pb3[0:1, 128:256]))
        kb.op("vector", lambda e: e.drain())
        kb.op("vector", lambda e: e.tensor_tensor_scan(
            out=sk_row[0:1, :], data0=sk_row2[0:1, :], data1=sk_row2[0:1, :],
            initial=0.0, op0=ALU.add, op1=ALU.bypass))
        kb.bar(("vector",))
        kb.op("tensor", lambda e: e.transpose(
            out=pb3[0:128, 3:4], in_=sk_row[0:1, :], identity=identf[0:1, 0:1]))
        kb.bar(("tensor",))
        kb.op("vector", lambda e: e.tensor_tensor(
            out=sk_cols[:, 5:6], in0=pb3[0:128, 3:4], in1=skr(PFX)[:, 15:16],
            op=ALU.subtract))
        kb.op("vector", lambda e: e.drain())
        kb.op("vector", lambda e: e.tensor_scalar(
            out=skr(PFX), in0=skr(PFX), scalar1=sk_cols[:, 5:6], scalar2=None,
            op0=ALU.add))
        kb.op("vector", lambda e: e.drain())
        kb.op("vector", lambda e: e.tensor_single_scalar(
            out=skr(D_), in_=skr(PFX), scalar=float(kk), op=ALU.is_le))
        kb.op("vector", lambda e: e.drain())
        kb.op("vector", lambda e: e.tensor_tensor(
            out=skr(ONES), in0=skr(ONES), in1=skr(D_), op=ALU.mult))
        kb.op("vector", lambda e: e.drain())
        kb.op("vector", lambda e: e.tensor_scalar(
            out=skr(D_), in0=skr(PFX), scalar1=1.0 - float(base) + float(trash),
            scalar2=None, op0=ALU.subtract))
        kb.op("vector", lambda e: e.drain())
        kb.op("vector", lambda e: e.tensor_tensor(
            out=skr(D_), in0=skr(D_), in1=jmat[:], op=ALU.subtract))
        kb.op("vector", lambda e: e.drain())
        kb.op("vector", lambda e: e.tensor_tensor(
            out=skr(D_), in0=skr(D_), in1=skr(ONES), op=ALU.mult))
        kb.op("vector", lambda e: e.drain())
        kb.op("vector", lambda e: e.tensor_scalar(
            out=skr(PT), in0=skr(D_), scalar1=float(trash), scalar2=None,
            op0=ALU.add))
        kb.op("vector", lambda e: e.drain())
        kb.op("vector", lambda e: e.tensor_tensor(
            out=skr(PT), in0=skr(PT), in1=jmat[:], op=ALU.add))
        kb.op("vector", lambda e: e.drain())
        return TSC, PT

    def route_scatter(call, ridx, TSC, PT, wait_prev=()):
        tg = "sc%d_%d" % (call, ridx)
        ptgt = ptgt_i if ridx == 0 else ptgt2
        svb = scatv[:] if ridx == 0 else sk_sc[:, 0:64]
        kb.op("vector", lambda e: e.tensor_scalar(
            out=skr(PT), in0=skr(PT), scalar1=0.0, scalar2=1663.0,
            op0=ALU.max, op1=ALU.min), wait=wait_prev)
        kb.op("vector", lambda e: e.drain())
        kb.op("vector", lambda e, ptgt=ptgt: e.tensor_copy(
            out=ptgt[:], in_=skr(PT)))
        sv = svb.rearrange("p (j k) -> p j k", k=4)
        kb.op("vector", lambda e, sv=sv: e.tensor_copy(out=sv[:, :, 0],
                                                       in_=iota2[:]))
        kb.op("vector", lambda e, sv=sv: e.tensor_copy(out=sv[:, :, 1],
                                                       in_=skr(TSC)))
        kb.op("vector", lambda e, sv=sv: e.tensor_copy(out=sv[:, :, 2],
                                                       in_=gld_f[:, 0:16]))
        kb.op("vector", lambda e, sv=sv: e.tensor_copy(out=sv[:, :, 3],
                                                       in_=gld_f[:, 0:16]))
        mScv = kb.mk("vector")
        kb.plan["gpsimd"].append(lambda e, w=mScv: e.wait_ge(
            kb.sem(w[0]), w[1]))
        ws_all = []
        for j in range(16):
            gname = "gsc%d" % (j % 8)
            w = [kb.gv(gname)] if kb.semv.get("g_" + gname, 0) else []
            kb.op("gpsimd", lambda e, j=j, ptgt=ptgt, svb=svb:
                  e.indirect_dma_start(
                      out=idx_dram, out_offset=IOA(ptgt[:, j:j + 1], 0),
                      in_=svb[:, 4 * j:4 * j + 4], in_offset=None),
                  wait=w, inc="g_" + gname, incn=16)
        for g in range(8):
            ws_all.append(kb.gv("gsc%d" % g))
        return ws_all

    def ffn_route(call):
        tg = "rt%d" % call
        kb.plan["tensor"].append(
            lambda e, w=kb.wv("invtm"): e.wait_ge(kb.sem(w[0]), w[1]))
        for tt in range(NTT):
            bp = 0 if tt % 2 == 0 else 32
            for c in range(NCH):
                kb.op("tensor", lambda e, tt=tt, c=c, bp=bp: e.matmul(
                    out=pb2[bp:bp + 2, 0:512],
                    lhsT=rt2_sb[:, 2 * c:2 * c + 2],
                    rhs=xch(c)[:, tt * 512:(tt + 1) * 512],
                    start=(c == 0), stop=(c == 7)))
            kb.bar(("tensor",))
            slot = (SL_T0, SL_T1, SL_S2, SL_VAR)[tt]
            kb.op("scalar", lambda e, slot=slot, bp=bp: e.copy(
                out=stat[0:2, slot * 512:slot * 512 + 512],
                in_=pb2[bp:bp + 2, 0:512]))
            kb.bar(("scalar",))
        kb.bar(("scalar",))
        for tt in range(NTT):
            slot = (SL_T0, SL_T1, SL_S2, SL_VAR)[tt]
            for r in range(2):
                kb.dma("sync", "gsr" + tg,
                       rows_dram[3 + r:4 + r, tt * 512:(tt + 1) * 512],
                       stat[r:r + 1, slot * 512:slot * 512 + 512])
        w_sr = kb.gv("gsr" + tg)
        for c in range(NCH):
            kb.op("tensor", lambda e, c=c: e.matmul(
                out=pb3[0:1, 256:258], lhsT=onescol_f[:],
                rhs=rt2_sb[:, 2 * c:2 * c + 2],
                start=(c == 0), stop=(c == 7)))
        kb.bar(("tensor",))
        kb.op("vector", lambda e: e.tensor_copy(out=sk_small[0:1, 14:16],
                                                in_=pb3[0:1, 256:258]))
        kb.op("vector", lambda e: e.drain())
        kb.op("vector", lambda e: e.tensor_tensor(
            out=statr(SL_T0), in0=statr(SL_MU), in1=statr(SL_INV), op=ALU.mult),
            wait=[w_sr])
        kb.op("vector", lambda e: e.drain())
        kb.op("vector", lambda e: e.tensor_scalar_mul(
            out=statr(SL_T0), in0=statr(SL_T0), scalar1=-1.0))
        kb.bar(("vector",))
        kb.dma("sync", "gsh" + tg,
               rows_dram[9:10, :].rearrange("r (p j) -> (r p) j", p=4),
               statr(SL_T0))
        w_sh = kb.gv("gsh" + tg)
        def ldr(row, dst, w):
            kb.dma("gpsimd", "gld" + tg, dst,
                   rows_dram[row:row + 1, :].rearrange("r (p j) -> p (r j)", p=128),
                   wait=w)
        kb.dma("sync", "gsi" + tg,
               rows_dram[1:2, :].rearrange("r (p j) -> (r p) j", p=4),
               statr(SL_INV))
        w_si = kb.gv("gsi" + tg)
        ldr(3, srt_q[:], [w_sr])
        ldr(4, srt_kv[:], [w_sr])
        ldr(0, skr(0), [])
        ldr(1, skr(1), [w_si])
        ldr(9, skr(2), [w_sh])
        w_ld = kb.gv("gld" + tg)
        kb.plan["vector"].append(
            lambda e, w=w_ld: e.wait_ge(kb.sem(w[0]), w[1]))
        for ridx, s_tm in ((0, srt_q), (1, srt_kv)):
            kb.bar(("vector",))
            kb.op("tensor", lambda e, ridx=ridx: e.matmul(
                out=pb3[0:128, 4:5], lhsT=onesrow_f[0:1, :],
                rhs=sk_small[0:1, 14 + ridx:15 + ridx], start=True, stop=True))
            kb.bar(("tensor",))
            kb.op("vector", lambda e: e.tensor_copy(out=sk_cols[:, 6:7],
                                                    in_=pb3[0:128, 4:5]))
            kb.op("vector", lambda e: e.drain())
            kb.op("vector", lambda e: e.tensor_scalar(
                out=skr(3), in0=skr(0), scalar1=sk_cols[:, 6:7], scalar2=None,
                op0=ALU.mult))
            kb.op("vector", lambda e: e.drain())
            kb.op("vector", lambda e, s_tm=s_tm: e.tensor_tensor(
                out=s_tm[:], in0=s_tm[:], in1=skr(3), op=ALU.add))
            kb.op("vector", lambda e: e.drain())
            kb.op("vector", lambda e, s_tm=s_tm: e.tensor_tensor(
                out=s_tm[:], in0=s_tm[:], in1=skr(1), op=ALU.mult))
            kb.op("vector", lambda e: e.drain())
        kb.op("vector", lambda e: e.tensor_copy(out=gld_f[:, 0:16],
                                                in_=skr(2)))
        kb.op("vector", lambda e: e.drain())
        kb.dma("sync", "gzi" + tg,
               idx_dram.rearrange("(p a) k -> p (a k)", p=128),
               zrow[:, 0:52])
        w_zi = kb.gv("gzi" + tg)
        kb.plan["gpsimd"].append(
            lambda e, w=w_zi: e.wait_ge(kb.sem(w[0]), w[1]))
        route_ms(0, srt_q)
        route_ms(1, srt_kv)
        route_iters()
        TSCq, PTq = route_one(call, 0, srt_q, 512, 0, 512)
        wsq = route_scatter(call, 0, TSCq, PTq, [w_zi])
        TSCk, PTk = route_one(call, 1, srt_kv, 1024, 528, 1552)
        if DEBUG and call == 1:
            dbg_pt = kb.debug_out("kv_pt", (128, 16))
            dbg_pf = kb.debug_out("kv_pfx", (128, 16))
            dbg_on = kb.debug_out("kv_ones", (128, 16))
            dbg_s = kb.debug_out("kv_s", (128, 16))
            kb.bar(("vector",))
            kb.dma("sync", "gdbgr", dbg_pt, skr(6))
            kb.dma("sync", "gdbgr", dbg_pf, skr(5))
            kb.dma("sync", "gdbgr", dbg_on, skr(4))
            kb.dma("sync", "gdbgr", dbg_s, srt_kv[:])
            dbg_rf = kb.debug_out("kv_rf", (128, 16))
            kb.dma("sync", "gdbgr", dbg_rf, skr(3))
        wsk = route_scatter(call, 1, TSCk, PTk, [])
        route_w[call] = (wsq, wsk)


    def setup_bias_qkv():
        kb.op("vector", lambda e: e.tensor_copy(out=colsb[:, 0:82],
                                                in_=cols[:, 0:82]),
              wait=[ld_const])
        kb.bar(("vector",))
        for m in range(12):
            for c in range(NCH):
                kb.op("tensor", lambda e, m=m, c=c: e.matmul(
                    out=pb0[0:128, m:m + 1],
                    lhsT=wslc(WQKV_O, c, 1536, 128, 128 * m),
                    rhs=colsb[:, 8 + c:9 + c],
                    start=(c == 0), stop=(c == 7)), wait=[ld_w1] if m == 0 and c == 0 else ())
        kb.bar(("tensor",))
        kb.op("scalar", lambda e: e.copy(out=bqkv_col[:], in_=pb0[0:128, 0:12]))
        kb.bar(("scalar",))
        kb.op("vector", lambda e: e.tensor_scalar_mul(
            out=bq_scaled[:], in0=bqkv_col[:, 0:4], scalar1=0.125))
        kb.bar(("vector",))
        kb.op("tensor", lambda e: e.transpose(
            out=pb1[0:4, 0:128], in_=bqkv_col[:, 8:12], identity=identf[:]))
        kb.bar(("tensor",))
        kb.op("scalar", lambda e: e.copy(out=bvT[:], in_=pb1[0:4, 0:128]))
        kb.bar(("scalar",))
        kb.dma("sync", "gbv", rows_dram[10:11, 0:512].rearrange(
            "r (m p) -> (r m) p", m=4), bvT[:])
        wbv = kb.gv("gbv")
        kb.dma("gpsimd", "gbv2", bv_row[:], rows_dram[10:11, 0:512], wait=[wbv])
        wbv2 = kb.gv("gbv2")
        kb.plan["tensor"].append(lambda e, w=wbv2: e.wait_ge(kb.sem(w[0]), w[1]))

    def ffn_light(call):
        tg = "lt%d" % call
        QSEG, KSEG, VSEG, OFM = 0, 4096, 9216, 11776
        qseg = xc_sb[0:66, QSEG:QSEG + 4096]
        kseg = xc_sb[0:66, KSEG:KSEG + 5120]
        vseg = xc_sb[:, VSEG:VSEG + 2560]
        vseg_hi = xc_sb[0:64, 13824:16384]
        ofm = xc_sb[:, OFM:OFM + 2048]
        PT_O, ADIS = 16384, 16896
        obank = [pA[:, 0:512], pA[:, 512:1024], pb2[:], pb3[:]]
        for _e in ("vector", "scalar", "tensor"):
            for _g in ("gyt0", "gyt1"):
                kb.plan[_e].append(
                    lambda e, w=kb.gv(_g): e.wait_ge(kb.sem(w[0]), w[1]))
        kb.op("vector", lambda e: e.memset(xc_sb[64:66, KSEG:KSEG + 5120], 1.0))
        kb.op("vector", lambda e: e.memset(zrow_b[:], 0.0))
        kb.bar(("vector",))
        mSeg = None
        mOutV = {}
        for seg in range(4):
            t0 = 512 * seg
            kv0 = t0 - 64
            r0, r1 = max(0, kv0), min(T, t0 + 576)
            if seg in (0, 3):
                kb.wt("vector", mSeg)
                if seg == 0:
                    kb.op("vector", lambda e: e.memset(
                        kseg[0:64, 0:5120].rearrange(
                            "p (h n) -> p h n", h=8)[:, :, 0:64], 0.0))
                    kb.op("vector", lambda e: e.memset(vseg[:, 0:512], 0.0))
                    kb.op("vector", lambda e: e.memset(vseg_hi[:, 0:512], 0.0))
                else:
                    kb.op("vector", lambda e: e.memset(
                        kseg[0:64, 0:5120].rearrange(
                            "p (h n) -> p h n", h=8)[:, :, 576:640], 0.0))
                    kb.op("vector", lambda e: e.memset(vseg[:, 2048:2560], 0.0))
                    kb.op("vector", lambda e: e.memset(vseg_hi[:, 2048:2560], 0.0))
                mPad = kb.mk("vector")
            else:
                mPad = None
            gi = 0
            mPm = {}
            mPe = {}
            def pbank(g):
                return pb0 if g % 2 == 0 else pb1
            for m in range(4):
                ps = pbank(gi)
                kb.wt("tensor", mPe.get(gi - 2),
                      mSeg if gi < 2 else None,
                      mOutV.get(0 if gi % 2 == 0 else 1) if gi < 2 else None,
                      mPad if gi == 0 else None)
                for c in range(NCH):
                    kb.op("tensor", lambda e, m=m, c=c, t0=t0, ps=ps: e.matmul(
                        out=ps[:, 0:512],
                        lhsT=wslc(WQKV_O, c, 1536, 128, 128 * m),
                        rhs=ulch(c)[:, t0:t0 + 512],
                        start=(c == 0), stop=(c == 7)),
                        wait=[ld_w1] if (call == 1 and seg == 0 and m == 0
                                         and c == 0) else ())
                mPm[gi] = kb.mk("tensor")
                for half in range(2):
                    kb.op("scalar", lambda e, m=m, half=half, ps=ps: e.activation(
                        out=qseg[0:64, 512 * (2 * m + half):
                                 512 * (2 * m + half) + 512],
                        in_=ps[64 * half:64 * half + 64, 0:512],
                        func=AF.Identity, scale=0.125,
                        bias=bq_scaled[64 * half:64 * half + 64, m:m + 1]),
                        wait=[mPm[gi]] if half == 0 else ())
                mPe[gi] = kb.mk("scalar")
                gi += 1
            nsub = [(r0, min(r0 + 512, r1))]
            if r0 + 512 < r1:
                nsub.append((r0 + 512, r1))
            for m in range(4):
                for (a0, b0) in nsub:
                    ps = pbank(gi)
                    kb.wt("tensor", mPe.get(gi - 2))
                    for c in range(NCH):
                        kb.op("tensor", lambda e, m=m, c=c, a0=a0, b0=b0, ps=ps:
                              e.matmul(
                                  out=ps[:, 0:b0 - a0],
                                  lhsT=wslc(WQKV_O, c, 1536, 128, 512 + 128 * m),
                                  rhs=ulch(c)[:, a0:b0],
                                  start=(c == 0), stop=(c == 7)))
                    mPm[gi] = kb.mk("tensor")
                    for half in range(2):
                        kb.op("scalar", lambda e, m=m, half=half, a0=a0, b0=b0,
                              kv0=kv0, ps=ps: e.activation(
                            out=kseg[0:64, 640 * (2 * m + half) + (a0 - kv0):
                                     640 * (2 * m + half) + (b0 - kv0)],
                            in_=ps[64 * half:64 * half + 64, 0:b0 - a0],
                            func=AF.Identity,
                            bias=bqkv_col[64 * half:64 * half + 64, 4 + m:5 + m]),
                            wait=[mPm[gi]] if half == 0 else ())
                    mPe[gi] = kb.mk("scalar")
                    gi += 1
            for kcl in range(5):
                a0 = max(r0, kv0 + 128 * kcl)
                b0 = min(r1, kv0 + 128 * (kcl + 1))
                row_off = a0 - (kv0 + 128 * kcl)
                ps = pbank(gi)
                kb.wt("tensor", mPe.get(gi - 2))
                for c in range(NCH):
                    kb.op("tensor", lambda e, c=c, a0=a0, b0=b0, row_off=row_off,
                          ps=ps: e.matmul(
                        out=ps[row_off:row_off + (b0 - a0), 0:512],
                        lhsT=ulch(c)[:, a0:b0],
                        rhs=wslc(WQKV_O, c, 1536, 512, 1024),
                        start=(c == 0), stop=False))
                kb.op("tensor", lambda e, a0=a0, b0=b0, row_off=row_off, ps=ps:
                      e.matmul(
                          out=ps[row_off:row_off + (b0 - a0), 0:512],
                          lhsT=onesrow_b[0:1, 0:b0 - a0],
                          rhs=bv_row[0:1, 0:512], start=False, stop=True))
                mPm[gi] = kb.mk("tensor")
                kb.op("scalar", lambda e, kcl=kcl, a0=a0, b0=b0, row_off=row_off,
                      ps=ps: e.copy(
                    out=vseg[row_off:row_off + (b0 - a0),
                             512 * kcl:512 * kcl + 512],
                    in_=ps[row_off:row_off + (b0 - a0), 0:512]),
                    wait=[mPm[gi]])
                hr0, hr1 = max(64, row_off), min(128, row_off + (b0 - a0))
                if hr1 > hr0:
                    kb.op("scalar", lambda e, kcl=kcl, hr0=hr0, hr1=hr1, ps=ps:
                          e.copy(
                              out=vseg_hi[hr0 - 64:hr1 - 64,
                                          512 * kcl:512 * kcl + 512],
                              in_=ps[hr0:hr1, 0:512]))
                mPe[gi] = kb.mk("scalar")
                gi += 1
            mProjT = mPm[gi - 1]
            mProjS = mPe[gi - 1]
            mProjS2 = mPe[gi - 2]
            aslots = (pA[:, 0:256], pA[:, 512:768], pb0[:, 0:256], pb1[:, 0:256])
            mAe = {}
            for g in range(8):
                kb.wt("tensor", mAe.get(g - 1),
                      mProjS if g == 0 else None, mProjS2 if g == 0 else None)
                for gi2 in range(4):
                    ti = 4 * g + gi2
                    h, qtl = ti // 4, ti % 4
                    slot = aslots[gi2]
                    kb.op("tensor", lambda e, h=h, qtl=qtl, slot=slot: e.matmul(
                        out=slot, lhsT=qseg[0:64, 512 * h + 128 * qtl:
                                            512 * h + 128 * qtl + 128],
                        rhs=kseg[0:64, 640 * h + 128 * qtl:
                                 640 * h + 128 * qtl + 256],
                        start=True, stop=False))
                    ex = []
                    if seg == 0 and qtl == 0:
                        ex = [(0, 0)]
                    if seg == 3 and qtl == 3:
                        ex = [(1, 192)]
                    kb.op("tensor", lambda e, slot=slot, st=(not ex): e.matmul(
                        out=slot, lhsT=maskA_l[:], rhs=maskA_r[:],
                        start=False, stop=st))
                    for (ur, c0) in ex:
                        kb.op("tensor", lambda e, slot=slot, ur=ur, c0=c0:
                              e.matmul(
                                  out=slot,
                                  lhsT=(maskA_l[0:1, :] if ur == 0
                                        else u2row[0:1, :]),
                                  rhs=(edgeL_row[0:1, :] if c0 == 0
                                       else edgeR_row[0:1, :]),
                                  start=False, stop=True))
                mAm = kb.mk("tensor")
                for gi2 in range(4):
                    ti = 4 * g + gi2
                    kb.op("vector", lambda e, gi2=gi2, ti=ti, aslots=aslots:
                          e.tensor_reduce(
                              out=negm_seg[:, ti:ti + 1], in_=aslots[gi2],
                              axis=AX.X, op=ALU.max, negate=True),
                          wait=[mAm] if gi2 == 0 else ())
                mAv = kb.mk("vector")
                for gi2 in range(4):
                    ti = 4 * g + gi2
                    kb.op("scalar", lambda e, gi2=gi2, ti=ti, aslots=aslots:
                          e.activation(
                              out=A_bf[:, ADIS:ADIS + 256], in_=aslots[gi2],
                              func=AF.Exp, bias=negm_seg[:, ti:ti + 1],
                              accum_out=s_seg[:, ti:ti + 1]),
                          wait=[mAv] if gi2 == 0 else ())
                    kb.op("scalar", lambda e: e.drain())
                mAe[g] = kb.mk("scalar")
            kb.op("scalar", lambda e: e.activation(out=s_seg[:], in_=s_seg[:],
                                                   func=AF.Ln))
            mN1 = kb.mk("scalar")
            kb.op("vector", lambda e: e.tensor_tensor(
                out=negm_seg[:], in0=negm_seg[:], in1=s_seg[:],
                op=ALU.subtract), wait=[mN1])
            mN2 = kb.mk("vector")
            kb.op("tensor", lambda e: e.transpose(
                out=pb0[0:32, 0:128], in_=negm_seg[:, 0:32], identity=identf[:]),
                wait=[mN2])
            mN3 = kb.mk("tensor")
            kb.op("scalar", lambda e: e.copy(out=nmlT[:], in_=pb0[0:32, 0:128]),
                  wait=[mN3])
            mN4 = kb.mk("scalar")
            kb.op("vector", lambda e: e.tensor_copy(out=nmlT_h[:], in_=nmlT[:]),
                  wait=[mN4])
            kb.op("vector", lambda e: e.drain())
            kb.op("vector", lambda e: e.tensor_tensor(
                out=nmlT_l[:], in0=nmlT[:], in1=nmlT_h[:], op=ALU.subtract))
            mN5 = kb.mk("vector")
            kb.dma("sync", "gnm" + tg, nmlb_dram[0:1, :].rearrange(
                "r (p j) -> (r p) j", p=32), nmlT_h[:], wait=[mN5])
            kb.dma("sync", "gnm" + tg, nmlb_dram[1:2, :].rearrange(
                "r (p j) -> (r p) j", p=32), nmlT_l[:])
            wnm = kb.gv("gnm" + tg)
            kb.dma("sync", "gnl" + tg, qseg[64:66, 0:4096], nmlb_dram[:, :],
                   wait=[wnm])
            wnl = kb.gv("gnl" + tg)
            kb.wt("tensor", mAe[7], wnl)
            for h2 in range(4):
                kb.op("tensor", lambda e, h2=h2: e.matmul(
                    out=obank[h2], lhsT=onesrow_b[0:1, 0:128],
                    rhs=zrow_b[0:1, 0:512], start=True, stop=False))
            mTe = {}
            mPVm = {}
            tiles = [(h, kcl) for h in range(8) for kcl in range(5)]

            def emit_pv(i):
                h, kcl = tiles[i]
                q0 = max(0, 128 * kcl - 128)
                q1 = min(512, 128 * kcl + 128)
                nq = q1 - q0
                ptb = A_bf[:, PT_O + 256 * (i % 2):PT_O + 256 * (i % 2) + 256]
                ptbh = A_bf[:, 17920 + 256 * (i % 2):
                            17920 + 256 * (i % 2) + 256]
                po = q0 - (128 * kcl - 128)
                ob = obank[h // 2]
                orow = 64 * (h % 2)
                kb.wt("tensor", mTe[i])
                npv = 0
                for (jr0, pc0, pc1) in ((0, 0, 192), (64, 64, 256)):
                    if seg == 0 and kcl == 0 and jr0 == 0:
                        continue
                    if seg == 3 and kcl == 4 and jr0 == 64:
                        continue
                    c0 = max(pc0, po) - po
                    c1 = min(pc1, po + nq) - po
                    if c1 <= c0:
                        continue
                    vs = vseg if jr0 == 0 else vseg_hi
                    pt_ = ptb if jr0 == 0 else ptbh
                    kb.op("tensor", lambda e, h=h, kcl=kcl, c0=c0, c1=c1,
                          q0=q0, ob=ob, orow=orow, vs=vs, pt_=pt_: e.matmul(
                        out=ob[orow:orow + 64, q0 + c0:q0 + c1],
                        lhsT=vs[0:64, 512 * kcl + 64 * h:
                                512 * kcl + 64 * h + 64],
                        rhs=pt_[0:64, c0:c1],
                        start=False, stop=False))
                    npv += 1
                mPVm[i] = kb.mk("tensor") if npv else mTe[i]

            for i, (h, kcl) in enumerate(tiles):
                q0 = max(0, 128 * kcl - 128)
                q1 = min(512, 128 * kcl + 128)
                nq = q1 - q0
                tp = (pb0 if i % 2 == 0 else pb1)
                kb.wt("tensor", mTe.get(i - 2))
                kb.op("tensor", lambda e, h=h, kcl=kcl, q0=q0, nq=nq, tp=tp:
                      e.matmul(
                          out=tp[:, 0:nq],
                          lhsT=kseg[0:66, 640 * h + 128 * kcl:
                                    640 * h + 128 * kcl + 128],
                          rhs=qseg[0:66, 512 * h + q0:512 * h + q0 + nq],
                          start=True, stop=True))
                mTm = kb.mk("tensor")
                ptb = A_bf[:, PT_O + 256 * (i % 2):PT_O + 256 * (i % 2) + 256]
                ptbh = A_bf[:, 17920 + 256 * (i % 2):
                            17920 + 256 * (i % 2) + 256]
                kb.op("scalar", lambda e, tp=tp, nq=nq, ptb=ptb: e.activation(
                    out=ptb[0:64, 0:nq], in_=tp[0:64, 0:nq], func=AF.Exp),
                    wait=[w for w in (mTm, mPVm.get(i - 2)) if w is not None])
                kb.op("scalar", lambda e, tp=tp, nq=nq, ptbh=ptbh: e.activation(
                    out=ptbh[0:64, 0:nq], in_=tp[64:128, 0:nq], func=AF.Exp))
                mTe[i] = kb.mk("scalar")
                if i > 0:
                    emit_pv(i - 1)
            emit_pv(len(tiles) - 1)
            for h2 in range(4):
                kb.op("tensor", lambda e, h2=h2: e.matmul(
                    out=obank[h2], lhsT=onesrow_b[0:1, 0:128],
                    rhs=zrow_b[0:1, 0:512], start=False, stop=True))
            mClose = kb.mk("tensor")
            for h2 in range(4):
                kb.op("scalar", lambda e, h2=h2: e.copy(
                    out=ofm[:, 512 * h2:512 * h2 + 512], in_=obank[h2]),
                    wait=[mClose] if h2 == 0 else ())
            mOe = kb.mk("scalar")
            mOm = {}
            for m in range(NCH):
                ps = pb0 if m % 2 == 0 else pb1
                kb.wt("tensor", mOe if m == 0 else None, mOutV.get(m % 2))
                for c in range(4):
                    kb.op("tensor", lambda e, m=m, c=c, ps=ps: e.matmul(
                        out=ps[:, 0:512],
                        lhsT=wslc(WOUTL_O, c, 1024, 128, 128 * m),
                        rhs=ofm[:, 512 * c:512 * c + 512],
                        start=(c == 0), stop=(c == 3)))
                mOm[m] = kb.mk("tensor")
                kb.op("vector", lambda e, m=m, t0=t0, ps=ps:
                      e.scalar_tensor_tensor(
                          out=xch(m)[:, t0:t0 + 512], in0=ps[:, 0:512],
                          scalar=0.5, in1=xch(m)[:, t0:t0 + 512],
                          op0=ALU.mult, op1=ALU.add), wait=[mOm[m]])
                mOutV[m % 2] = kb.mk("vector")
            mSeg = mOm[NCH - 1]
            kb.bar(("tensor", "vector", "scalar"))
        kb.bar(("tensor", "vector", "scalar"))

    xq_dram = dram("xq_scr", (512, D), BF16, kind="Internal")
    xkv_dram = dram("xkv_scr", (1024, D), BF16, kind="Internal")
    rms_c = kb.sb("rms_c", (128, 12), F32)
    lntkv = kb.sb("lntkv", (128, 8), F32)
    sq5 = kb.sb("sq5", (128, 4), F32)
    negm_h = negm_seg
    s_h = s_seg

    def ffn_heavy(call):
        tg = "hv%d" % call
        QH, KH, VH, PT0, PT1 = 0, 4096, 12352, 16448, 16960
        wsq, wsk = route_w[call]
        for (row0, ntile, base) in ((0, 4, 0), (528, 8, 4)):
            for g in range(ntile):
                kb.dma("sync", "gli" + tg,
                       gld_f[:, 4 * (base + g):4 * (base + g) + 4],
                       idx_dram[row0 + 128 * g: row0 + 128 * (g + 1), :],
                       wait=(wsq + wsk) if (row0 == 0 and g == 0) else ())
        w_li = kb.gv("gli" + tg)
        kb.plan["vector"].append(
            lambda e, w=w_li: e.wait_ge(kb.sem(w[0]), w[1]))
        for g in range(12):
            kb.op("vector", lambda e, g=g: e.tensor_scalar(
                out=gld_f[:, 4 * g:4 * g + 1], in0=gld_f[:, 4 * g:4 * g + 1],
                scalar1=0.0, scalar2=2047.0, op0=ALU.max, op1=ALU.min))
        kb.op("vector", lambda e: e.drain())
        for g in range(12):
            kb.op("vector", lambda e, g=g: e.tensor_copy(
                out=gidx[:, g:g + 1], in_=gld_f[:, 4 * g:4 * g + 1]))
            kb.op("vector", lambda e, g=g: e.tensor_copy(
                out=gmeta[:, 2 * g:2 * g + 2], in_=gld_f[:, 4 * g + 1:4 * g + 3]))
        kb.op("vector", lambda e: e.drain())
        mGc = kb.mk("vector")
        kb.plan["gpsimd"].append(lambda e, w=mGc: e.wait_ge(
            kb.sem(w[0]), w[1]))
        wlt = kb.wv("bar_tensor")
        load_w("gw2" + tg, WQ_O, wq_d, 8, 512, wait=[wlt])
        load_w("gw2" + tg, WKV_O, wkv_d, 8, 1024)
        ld_w2 = load_w("gw2" + tg, WOUTH_O, wouth_d, 4, 1024)
        kb.plan["tensor"].append(lambda e, w=ld_w2: e.wait_ge(kb.sem(w[0]), w[1]))
        XTMP0, XTMP1, XQF, XKVF, OH = 0, 1024, 2048, 6144, 14336
        qh = A_bf[0:66, QH:QH + 4096]
        kh = A_bf[0:66, KH:KH + 8256]
        vh = A_bf[:, VH:VH + 4096]
        for g in range(16):
            kb.dma("sync", "ghp" + tg, hs_dram[128 * g:128 * (g + 1), :],
                   nullq_bh[:])
        w_hp = kb.gv("ghp" + tg)
        for g in range(12):
            tmp = xc_sb[:, (XTMP0 if g % 2 == 0 else XTMP1):
                        (XTMP0 if g % 2 == 0 else XTMP1) + 1024]
            wpr = [kb.gv("ghg%d" % (g % 2) + tg)] if g >= 2 else []
            kb.op("gpsimd", lambda e, g=g, tmp=tmp: e.indirect_dma_start(
                out=tmp, out_offset=None, in_=y_dram,
                in_offset=IOA(gidx[:, g:g + 1], 0)),
                wait=wpr, inc="g_gig%d" % (g % 2) + tg, incn=16)
            wg = kb.gv("gig%d" % (g % 2) + tg)
            kb.op("vector", lambda e, g=g, tmp=tmp: e.tensor_scalar(
                out=tmp, in0=tmp, scalar1=gmeta[:, 2 * g + 1:2 * g + 2],
                scalar2=None, op0=ALU.add), wait=[wg])
            mGv1 = kb.mk("vector")
            kb.op("scalar", lambda e, g=g, tmp=tmp: e.activation(
                out=xc_sb[:, OH:OH + 1024], in_=tmp, func=AF.Square,
                accum_out=rms_c[:, g:g + 1]), wait=[mGv1])
            kb.op("scalar", lambda e: e.drain())
            kb.op("scalar", lambda e, g=g: e.activation(
                out=rms_c[:, g:g + 1], in_=rms_c[:, g:g + 1], func=AF.Ln))
            kb.op("scalar", lambda e: e.drain())
            kb.op("scalar", lambda e, g=g: e.activation(
                out=rms_c[:, g:g + 1], in_=rms_c[:, g:g + 1], func=AF.Exp,
                scale=0.5))
            mGs = kb.mk("scalar")
            kb.op("vector", lambda e, g=g: e.tensor_scalar_max(
                out=rms_c[:, g:g + 1], in0=rms_c[:, g:g + 1], scalar1=1e-12),
                wait=[mGs])
            kb.op("vector", lambda e: e.drain())
            kb.op("vector", lambda e, g=g: e.reciprocal(
                out=rms_c[:, g:g + 1], in_=rms_c[:, g:g + 1]))
            kb.op("vector", lambda e: e.drain())
            kb.op("vector", lambda e, g=g, tmp=tmp: e.tensor_scalar(
                out=tmp, in0=tmp, scalar1=rms_c[:, g:g + 1], scalar2=32.0,
                op0=ALU.mult, op1=ALU.mult))
            mGv2 = kb.mk("vector")
            kb.plan["sync"].append(lambda e, w=mGv2: e.wait_ge(
                kb.sem(w[0]), w[1]))
            if g < 4:
                kb.dma("sync", "ghg%d" % (g % 2) + tg,
                       xq_dram[128 * g:128 * (g + 1), :], tmp)
            else:
                kb.dma("sync", "ghg%d" % (g % 2) + tg,
                       xkv_dram[128 * (g - 4):128 * (g - 3), :], tmp)
        w_g0 = kb.gv("ghg0" + tg)
        w_g1 = kb.gv("ghg1" + tg)
        for c in range(NCH):
            kb.dmat("ghq" + tg, xc_sb[:, XQF + 512 * c:XQF + 512 * (c + 1)],
                    xq_dram[:, 128 * c:128 * (c + 1)],
                    wait=[w_g0, w_g1] if c == 0 else ())
            kb.dmat("ghk" + tg, xc_sb[:, XKVF + 1024 * c:XKVF + 1024 * (c + 1)],
                    xkv_dram[:, 128 * c:128 * (c + 1)])
        w_tq = kb.gv("ghq" + tg)
        w_tk = kb.gv("ghk" + tg)
        kb.plan["vector"].append(lambda e, w=w_tq: e.wait_ge(kb.sem(w[0]), w[1]))
        kb.plan["vector"].append(lambda e, w=w_tk: e.wait_ge(kb.sem(w[0]), w[1]))
        for c in range(NCH):
            kb.op("vector", lambda e, c=c: e.tensor_scalar_mul(
                out=xc_sb[:, XQF + 512 * c:XQF + 512 * (c + 1)],
                in0=xc_sb[:, XQF + 512 * c:XQF + 512 * (c + 1)],
                scalar1=cols[:, 32 + c:33 + c]))
            kb.op("vector", lambda e, c=c: e.tensor_scalar_mul(
                out=xc_sb[:, XKVF + 1024 * c:XKVF + 1024 * (c + 1)],
                in0=xc_sb[:, XKVF + 1024 * c:XKVF + 1024 * (c + 1)],
                scalar1=cols[:, 32 + c:33 + c]))
        kb.bar(("vector",))
        kb.op("vector", lambda e: e.memset(A_bf[64:66, KH:KH + 8256], 1.0))
        kb.bar(("vector",))
        hgi = 0
        mHm = {}
        mHe = {}
        hpb = lambda g: (pb0 if g % 2 == 0 else pb1)
        for m in range(4):
            ps = hpb(hgi)
            kb.wt("tensor", mHe.get(hgi - 2))
            for c in range(NCH):
                kb.op("tensor", lambda e, m=m, c=c, ps=ps: e.matmul(
                    out=ps[:, 0:512],
                    lhsT=wslc(WQ_O, c, 512, 128, 128 * m),
                    rhs=xc_sb[:, XQF + 512 * c:XQF + 512 * (c + 1)],
                    start=(c == 0), stop=(c == 7)))
            mHm[hgi] = kb.mk("tensor")
            for half in range(2):
                kb.op("scalar", lambda e, m=m, half=half, ps=ps: e.activation(
                    out=A_bf[0:64, QH + 512 * (2 * m + half):
                             QH + 512 * (2 * m + half) + 512],
                    in_=ps[64 * half:64 * half + 64, 0:512],
                    func=AF.Identity, scale=0.125),
                    wait=[mHm[hgi]] if half == 0 else ())
            mHe[hgi] = kb.mk("scalar")
            hgi += 1
        for km in range(4):
            for nt in range(2):
                ps = hpb(hgi)
                kb.wt("tensor", mHe.get(hgi - 2))
                for c in range(NCH):
                    base = WKV_O + 1024 * c + 128 * km
                    kb.op("tensor", lambda e, km=km, nt=nt, c=c, base=base,
                          ps=ps: e.matmul(
                        out=ps[:, 0:512],
                        lhsT=w_arena[:, base:base + 128],
                        rhs=xc_sb[:, XKVF + 1024 * c + 512 * nt:
                                  XKVF + 1024 * c + 512 * (nt + 1)],
                        start=(c == 0), stop=(c == 7)))
                mHm[hgi] = kb.mk("tensor")
                for half in range(2):
                    kb.op("scalar", lambda e, km=km, nt=nt, half=half, ps=ps:
                          e.activation(
                        out=A_bf[0:64, KH + 1032 * (2 * km + half) + 512 * nt:
                                 KH + 1032 * (2 * km + half) + 512 * (nt + 1)],
                        in_=ps[64 * half:64 * half + 64, 0:512],
                        func=AF.Identity),
                        wait=[mHm[hgi]] if half == 0 else ())
                mHe[hgi] = kb.mk("scalar")
                hgi += 1
        for h in range(8):
            kb.op("vector", lambda e, h=h: e.tensor_copy(
                out=A_bf[0:64, KH + 1032 * h + 1024:KH + 1032 * h + 1025],
                in_=nullcol_b[0:64, h:h + 1]))
        kb.bar(("vector",))
        for kc in range(8):
            ps = hpb(hgi)
            kb.wt("tensor", mHe.get(hgi - 2))
            for c in range(NCH):
                base = WKV_O + 1024 * c + 512
                kb.op("tensor", lambda e, kc=kc, c=c, base=base, ps=ps: e.matmul(
                    out=ps[:, 0:512],
                    lhsT=xc_sb[:, XKVF + 1024 * c + 128 * kc:
                               XKVF + 1024 * c + 128 * (kc + 1)],
                    rhs=w_arena[:, base:base + 512],
                    start=(c == 0), stop=(c == 7)))
            mHm[hgi] = kb.mk("tensor")
            kb.op("scalar", lambda e, kc=kc, ps=ps: e.copy(
                out=A_bf[:, VH + 512 * kc:VH + 512 * (kc + 1)],
                in_=ps[:, 0:512]), wait=[mHm[hgi]])
            mHe[hgi] = kb.mk("scalar")
            hgi += 1
        kb.bar(("tensor", "vector", "scalar"))
        mAs = None
        for h in range(8):
            for qt in range(4):
                ti = 4 * h + qt
                kb.wt("tensor", mAs, mHe.get(hgi - 1) if ti == 0 else None,
                      mHe.get(hgi - 2) if ti == 0 else None)
                for ks, n in ((0, 512), (512, 512), (1024, 1)):
                    kb.op("tensor", lambda e, h=h, qt=qt, ks=ks, n=n: e.matmul(
                        out=pA[:, ks:ks + n],
                        lhsT=A_bf[0:64, QH + 512 * h + 128 * qt:
                                  QH + 512 * h + 128 * qt + 128],
                        rhs=A_bf[0:64, KH + 1032 * h + ks:KH + 1032 * h + ks + n],
                        start=True, stop=True))
                mAt = kb.mk("tensor")
                kb.op("vector", lambda e, ti=ti: e.tensor_reduce(
                    out=negm_h[:, ti:ti + 1], in_=pA[:, 0:1025], axis=AX.X,
                    op=ALU.max, negate=True), wait=[mAt])
                mAv = kb.mk("vector")
                kb.op("scalar", lambda e, ti=ti: e.activation(
                    out=xc_sb[:, OH:OH + 1025], in_=pA[:, 0:1025],
                    func=AF.Exp, bias=negm_h[:, ti:ti + 1],
                    accum_out=s_h[:, ti:ti + 1]), wait=[mAv])
                kb.op("scalar", lambda e: e.drain())
                mAs = kb.mk("scalar")
        kb.op("scalar", lambda e: e.activation(out=s_h[:], in_=s_h[:], func=AF.Ln))
        kb.bar(("scalar",))
        kb.op("vector", lambda e: e.tensor_tensor(
            out=negm_h[:], in0=negm_h[:], in1=s_h[:], op=ALU.subtract))
        kb.bar(("vector",))
        kb.op("tensor", lambda e: e.transpose(
            out=pb0[0:32, 0:128], in_=negm_h[:, 0:32], identity=identf[:]))
        kb.bar(("tensor",))
        kb.op("scalar", lambda e: e.copy(out=nmlT[:], in_=pb0[0:32, 0:128]))
        kb.bar(("scalar",))
        kb.op("vector", lambda e: e.tensor_copy(out=nmlT_h[:], in_=nmlT[:]))
        kb.op("vector", lambda e: e.drain())
        kb.op("vector", lambda e: e.tensor_tensor(
            out=nmlT_l[:], in0=nmlT[:], in1=nmlT_h[:], op=ALU.subtract))
        kb.bar(("vector",))
        kb.dma("sync", "gnh" + tg, nmlb_dram[0:1, :].rearrange(
            "r (p j) -> (r p) j", p=32), nmlT_h[:])
        kb.dma("sync", "gnh" + tg, nmlb_dram[1:2, :].rearrange(
            "r (p j) -> (r p) j", p=32), nmlT_l[:])
        wnm = kb.gv("gnh" + tg)
        kb.dma("sync", "gnk" + tg, A_bf[64:66, QH:QH + 4096], nmlb_dram[:, :],
               wait=[wnm])
        wnl = kb.gv("gnk" + tg)
        kb.plan["tensor"].append(lambda e, w=wnl: e.wait_ge(kb.sem(w[0]), w[1]))
        for kc in range(8):
            kb.op("scalar", lambda e, kc=kc: e.activation(
                out=lntkv[:, kc:kc + 1], in_=gmeta[:, 2 * (4 + kc):2 * (4 + kc) + 1],
                func=AF.Ln))
        kb.bar(("scalar",))
        obank2 = (pb0, pb1, pb2, pb3)
        tilesH = [(h, kc) for h in range(8) for kc in range(9)]
        mHTe = {}
        mHPV = {}

        def h_emit_pv(i):
            h, kc = tilesH[i]
            ob = obank2[h // 2]
            orow = 64 * (h % 2)
            ptb = A_bf[:, (PT0 if i % 2 == 0 else PT1):
                       (PT0 if i % 2 == 0 else PT1) + 512]
            kb.wt("tensor", mHTe[i])
            if kc < 8:
                kb.op("tensor", lambda e, h=h, kc=kc, ob=ob, orow=orow,
                      ptb=ptb: e.matmul(
                    out=ob[orow:orow + 64, 0:512],
                    lhsT=A_bf[:, VH + 512 * kc + 64 * h:
                              VH + 512 * kc + 64 * h + 64],
                    rhs=ptb[0:128, 0:512],
                    start=(kc == 0), stop=False))
            else:
                kb.op("tensor", lambda e, h=h, ob=ob, orow=orow, ptb=ptb:
                      e.matmul(
                    out=ob[orow:orow + 64, 0:512],
                    lhsT=nullvrow_b[0:1, 64 * h:64 * h + 64],
                    rhs=ptb[0:1, 0:512],
                    start=False, stop=True))
            mHPV[i] = kb.mk("tensor")

        for i, (h, kc) in enumerate(tilesH):
            n = 1 if kc == 8 else 128
            tp = pA[:, 0:512] if i % 2 == 0 else pA[:, 512:1024]
            kb.wt("tensor", mHTe.get(i - 2))
            kb.op("tensor", lambda e, h=h, kc=kc, n=n, tp=tp: e.matmul(
                out=tp[0:n, 0:512],
                lhsT=A_bf[0:66, KH + 1032 * h + 128 * kc:
                          KH + 1032 * h + 128 * kc + n],
                rhs=A_bf[0:66, QH + 512 * h:QH + 512 * h + 512],
                start=True, stop=True))
            mHTm = kb.mk("tensor")
            ptb = A_bf[:, (PT0 if i % 2 == 0 else PT1):
                       (PT0 if i % 2 == 0 else PT1) + 512]
            kb.op("scalar", lambda e, kc=kc, n=n, tp=tp, ptb=ptb: e.activation(
                out=ptb[0:n, 0:512], in_=tp[0:n, 0:512], func=AF.Exp,
                bias=(lntkv[0:n, kc:kc + 1] if kc < 8 else fconst[0:1, 2:3])),
                wait=[w for w in (mHTm, mHPV.get(i - 2)) if w is not None])
            mHTe[i] = kb.mk("scalar")
            if i > 0:
                h_emit_pv(i - 1)
        h_emit_pv(len(tilesH) - 1)
        mHlast = mHPV[len(tilesH) - 1]
        kb.bar(("tensor", "vector", "scalar"))
        for h2 in range(4):
            kb.op("scalar", lambda e, h2=h2: e.copy(
                out=xc_sb[:, OH + 512 * h2:OH + 512 * (h2 + 1)],
                in_=obank2[h2][:, 0:512]),
                wait=[mHlast] if h2 == 0 else ())
        kb.bar(("scalar",))
        kb.op("vector", lambda e: e.tensor_scalar(
            out=sq5[:], in0=gmeta[:, 0:8].rearrange("p (g k) -> p g k", k=2)[:, :, 0],
            scalar1=0.5, scalar2=None, op0=ALU.mult))
        kb.bar(("vector",))
        for tc in range(4):
            for nt in range(2):
                for c2 in range(4):
                    kb.op("tensor", lambda e, tc=tc, nt=nt, c2=c2: e.matmul(
                        out=pA[:, 0:512],
                        lhsT=xc_sb[:, OH + 512 * c2 + 128 * tc:
                                   OH + 512 * c2 + 128 * tc + 128],
                        rhs=wslc(WOUTH_O, c2, 1024, 512, 512 * nt),
                        start=(c2 == 0), stop=(c2 == 3)))
                kb.bar(("tensor",))
                kb.op("scalar", lambda e, tc=tc, nt=nt: e.activation(
                    out=A_bf[:, 6144 + 1024 * tc + 512 * nt:
                             6144 + 1024 * tc + 512 * (nt + 1)],
                    in_=pA[:, 0:512], func=AF.Identity,
                    scale=sq5[:, tc:tc + 1]))
                kb.bar(("scalar",))
        kb.plan["gpsimd"].append(lambda e, w=w_hp: e.wait_ge(kb.sem(w[0]), w[1]))
        for tc in range(4):
            kb.op("gpsimd", lambda e, tc=tc: e.indirect_dma_start(
                out=hs_dram, out_offset=IOA(gidx[:, tc:tc + 1], 0),
                in_=A_bf[:, 6144 + 1024 * tc:6144 + 1024 * (tc + 1)],
                in_offset=None), inc="g_ghs" + tg, incn=16)
        w_hs = kb.gv("ghs" + tg)
        for c in range(NCH):
            buf = xc_sb[:, 2048 + 2048 * (c % 2):2048 + 2048 * (c % 2) + 2048]
            wt = [w_hs] if c < 2 else [kb.gv("ghb%d" % (c % 2) + tg)]
            kb.dmat("ghb%d" % (c % 2) + tg, buf, hs_dram[:, 128 * c:128 * (c + 1)],
                    wait=wt)
            wldb = kb.gv("ghb%d" % (c % 2) + tg)
            kb.op("vector", lambda e, c=c, buf=buf: e.tensor_tensor(
                out=xch(c), in0=xch(c), in1=buf, op=ALU.add), wait=[wldb],
                inc="hsadd")
            kb.plan["sync"].append(
                lambda e, w=kb.wv("hsadd"): e.wait_ge(kb.sem(w[0]), w[1]))
        kb.bar(("vector",))

    um_dram = dram("um_scr", (D, T), BF16, kind="Internal")
    g_dram = dram("g_scr", (INNER, T), BF16, kind="Internal")
    xs_dram = dram("xs_scr", (T, INNER), BF16, kind="Internal")
    m8c_d = dram("m8c_in", (8, 4))
    binr_d = dram("binr_in", (1, 2304))
    cmask_d = dram("cmask_in", (128, 128))
    Bfm = xc_sb[0:64, 640:2688]
    Cfm = xc_sb[0:64, 2688:4736]
    Btm = xc_sb[0:128, 4736:5760]
    Hst = xc_sb[0:64, 5760:7808]
    Dident = xc_sb[0:128, 7808:8832]
    binr_b = xc_sb[0:1, 8832:11136]
    bmx_row = xc_sb[0:1, 11136:13440]
    lt01b = xc_sb[0:128, 13440:13568]
    mr0 = lambda c0=0, n=T: repf[32:40, c0:c0 + n]
    mr1 = lambda c0=0, n=T: repf[64:72, c0:c0 + n]
    P8 = lambda c0=0, n=T: repf[0:8, c0:c0 + n]
    E8 = lambda c0=0, n=T: repb[0:8, c0:c0 + n]
    mcol = srt_q[:, 0:12]
    bias_mz = srt_kv[:, 0:16]
    bias_mx = gmeta[:, 0:18]
    dsk8 = cols[:, 86:94]
    kb.dma("sync", "gmc", cols[0:8, 82:86], m8c_d)
    kb.dma("sync", "gmc", cols[32:40, 82:86], m8c_d)
    ld_mc = kb.gv("gmc")

    def mamba():
        if DEBUG:
            for _e in ("vector", "scalar", "tensor"):
                kb.plan[_e].append(lambda e, w=kb.gv("gdbgr"): e.wait_ge(
                    kb.sem(w[0]), w[1]))
        w_rep2 = bcast_row(SL_INV, 2, repb[:], "mi",
                           wait_extra=[kb.wv("bar_vector")])
        for c in range(NCH):
            kb.op("vector", lambda e, c=c: e.tensor_tensor(
                out=A_bf[:, 0:2048], in0=xcch(c), in1=repb[:], op=ALU.mult),
                wait=[w_rep2] if c == 0 else ())
            kb.op("vector", lambda e: e.drain())
            kb.op("vector", lambda e, c=c: e.tensor_tensor(
                out=xch(c), in0=xch(c), in1=A_bf[:, 0:2048], op=ALU.add))
            kb.op("vector", lambda e: e.drain())
        kb.bar(("vector",))
        kb.op("vector", lambda e: e.tensor_tensor(out=statr(SL_T0), in0=statr(SL_INV),
                                                  in1=statr(SL_INV), op=ALU.mult))
        kb.op("vector", lambda e: e.drain())
        kb.op("vector", lambda e: e.tensor_tensor(out=statr(SL_T0), in0=statr(SL_T0),
                                                  in1=statr(SL_VAR), op=ALU.mult))
        kb.bar(("vector",))
        kb.op("scalar", lambda e: e.activation(out=statr(SL_T1), in_=statr(SL_T0),
                                               func=AF.Ln, bias=fconst[0:4, 0:1]))
        kb.op("scalar", lambda e: e.drain())
        kb.op("scalar", lambda e: e.activation(out=statr(SL_T1), in_=statr(SL_T1),
                                               func=AF.Exp, scale=-0.5))
        kb.bar(("scalar",))
        kb.op("vector", lambda e: e.tensor_tensor(out=statr(SL_S2), in0=statr(SL_INV),
                                                  in1=statr(SL_T1), op=ALU.mult))
        kb.bar(("vector",))
        w_rep3 = bcast_row(SL_S2, 3, repb[:], "ms",
                           wait_extra=[kb.wv("bar_vector")])
        for c in range(NCH):
            kb.op("vector", lambda e, c=c: e.scalar_tensor_tensor(
                out=xcch(c), in0=xcch(c), scalar=cols[:, 16 + c:17 + c],
                in1=repb[:], op0=ALU.mult, op1=ALU.mult),
                wait=[w_rep3] if c == 0 else ())
        kb.bar(("vector",))
        xc_r2 = xc_sb[:].rearrange("p (c t) -> p c t", c=NCH)
        um_r = um_dram.rearrange("(c p) t -> p c t", p=128)
        for i in range(2):
            kb.dma("sync", "gum", um_r[:, 4 * i:4 * i + 4, :],
                   xc_r2[:, 4 * i:4 * i + 4, :], wait=[kb.wv("bar_vector")])
        w_um = kb.gv("gum")
        kb.dma("gpsimd", "gmcb", binr_b, binr_d, wait=[w_um])
        kb.dma("gpsimd", "gmcb", lt01b, cmask_d)
        ld_mcb = kb.gv("gmcb")
        wmt = kb.wv("bar_tensor")
        win_r = win_d.rearrange("(c p) f -> p c f", p=128)
        kb.dma("gpsimd", "gwz",
               w_arena[:, 0:16384].rearrange("p (c f) -> p c f", c=8),
               win_r[:, :, 0:2048], wait=[wmt])
        ld_wz = kb.gv("gwz")
        kb.plan["tensor"].append(lambda e, w=ld_wz: e.wait_ge(kb.sem(w[0]), w[1]))
        for m in range(16):
            for c in range(NCH):
                kb.op("tensor", lambda e, m=m, c=c: e.matmul(
                    out=pb0[0:128, m:m + 1], lhsT=wslc(0, c, 2048, 128, 128 * m),
                    rhs=colsb[:, 24 + c:25 + c], start=(c == 0), stop=(c == 7)))
        kb.bar(("tensor",))
        kb.op("vector", lambda e: e.tensor_tensor(
            out=bias_mz, in0=pb0[0:128, 0:16], in1=cols[:, 40:56], op=ALU.add))
        kb.bar(("vector",))
        mZe = {}
        wZd = {}
        for tt in range(NTT):
            ub = A_bf[:, (0 if tt % 2 == 0 else 4096):(0 if tt % 2 == 0 else 4096) + 4096]
            wpr = ([kb.gv("gub%d" % (tt % 2))] if tt >= 2 else [w_um]) + \
                [kb.wv("bar_tensor"), kb.wv("bar_scalar")]
            kb.dma("gpsimd", "gub%d" % (tt % 2),
                   ub.rearrange("p (c t) -> p c t", c=8),
                   um_r[:, :, 512 * tt:512 * (tt + 1)], wait=wpr)
            wub = kb.gv("gub%d" % (tt % 2))
            kb.plan["tensor"].append(lambda e, w=wub: e.wait_ge(kb.sem(w[0]), w[1]))
            for m in range(16):
                zgi = 16 * tt + m
                zp = pb1 if zgi % 2 == 0 else pb2
                zb = A_bf[:, 8192 + 512 * (zgi % 2):8192 + 512 * (zgi % 2) + 512]
                kb.wt("tensor", mZe.get(zgi - 2))
                for c in range(NCH):
                    kb.op("tensor", lambda e, m=m, c=c, ub=ub, zp=zp: e.matmul(
                        out=zp[:, 0:512], lhsT=wslc(0, c, 2048, 128, 128 * m),
                        rhs=ub[:, 512 * c:512 * (c + 1)],
                        start=(c == 0), stop=(c == 7)))
                mZm = kb.mk("tensor")
                kb.op("scalar", lambda e, m=m, zp=zp, zb=zb: e.activation(
                    out=zb, in_=zp[:, 0:512],
                    func=AF.Silu, bias=bias_mz[:, m:m + 1]),
                    wait=[w for w in (mZm, wZd.get(zgi - 2)) if w is not None])
                mZe[zgi] = kb.mk("scalar")
                kb.plan["sync"].append(lambda e, w=mZe[zgi]: e.wait_ge(
                    kb.sem(w[0]), w[1]))
                kb.dma("sync", "ggst%d" % (zgi % 2),
                       g_dram[128 * m:128 * (m + 1), 512 * tt:512 * (tt + 1)],
                       zb)
                wZd[zgi] = kb.gv("ggst%d" % (zgi % 2))
        kb.bar(("tensor", "vector", "scalar"))
        w_g = (kb.gv("ggst0"), kb.gv("ggst1"))
        if os.environ.get("ATH_MK_MM") == "1":
            kb.bar(("tensor", "vector", "scalar"))
            return
        wmt2 = kb.wv("bar_tensor")
        kb.dma("gpsimd", "gwx",
               w_arena[:, 0:18432].rearrange("p (c f) -> p c f", c=8),
               win_r[:, :, 2048:4352], wait=[wmt2])
        ld_wx = kb.gv("gwx")
        kb.plan["tensor"].append(lambda e, w=ld_wx: e.wait_ge(kb.sem(w[0]), w[1]))
        kb.plan["tensor"].append(lambda e, w=ld_mcb: e.wait_ge(kb.sem(w[0]), w[1]))
        kb.wt("tensor", mZe.get(62), mZe.get(63))
        for m in (16, 17):
            for c in range(NCH):
                kb.op("tensor", lambda e, m=m, c=c: e.matmul(
                    out=pb0[0:128, m - 16:m - 15],
                    lhsT=wslc(0, c, 2304, 128, 128 * m),
                    rhs=colsb[:, 24 + c:25 + c], start=(c == 0), stop=(c == 7)))
        kb.bar(("tensor",))
        kb.op("vector", lambda e: e.tensor_tensor(
            out=bias_mx[:, 16:18], in0=pb0[0:128, 0:2], in1=cols[:, 72:74],
            op=ALU.add))
        kb.bar(("vector",))
        for n in range(4):
            for c in range(NCH):
                kb.op("tensor", lambda e, n=n, c=c: e.matmul(
                    out=pb2[0:1, 0:512], lhsT=colsb[:, 24 + c:25 + c],
                    rhs=wslc(0, c, 2304, 512, 512 * n), start=(c == 0), stop=False))
            kb.op("tensor", lambda e, n=n: e.matmul(
                out=pb2[0:1, 0:512], lhsT=onesrow_b[0:1, 0:1],
                rhs=binr_b[0:1, 512 * n:512 * (n + 1)], start=False, stop=True))
            kb.bar(("tensor",))
            kb.op("scalar", lambda e, n=n: e.copy(
                out=bmx_row[0:1, 512 * n:512 * (n + 1)], in_=pb2[0:1, 0:512]),
                wait=[w_um] if n == 0 else ())
            kb.bar(("scalar",))
        kb.op("scalar", lambda e: e.activation(out=mcol[32:40, 0:1],
                                               in_=cols[32:40, 82:83], func=AF.Exp))
        kb.bar(("scalar",))
        kb.op("vector", lambda e: e.tensor_scalar_mul(out=mcol[32:40, 1:2],
                                                      in0=mcol[32:40, 0:1],
                                                      scalar1=-1.0))
        kb.bar(("vector",))
        mXe = {}
        wXd = {-2: wZd.get(62), -1: wZd.get(63)}
        mXbc = {}
        for tt in range(NTT):
            ub = A_bf[:, (0 if tt % 2 == 0 else 4096):(0 if tt % 2 == 0 else 4096) + 4096]
            wpr = ([kb.gv("gux%d" % (tt % 2))] if tt >= 2 else [w_um]) + \
                [kb.wv("bar_tensor"), kb.wv("bar_scalar")]
            kb.dma("gpsimd", "gux%d" % (tt % 2),
                   ub.rearrange("p (c t) -> p c t", c=8),
                   um_r[:, :, 512 * tt:512 * (tt + 1)], wait=wpr)
            wub = kb.gv("gux%d" % (tt % 2))
            kb.plan["tensor"].append(lambda e, w=wub: e.wait_ge(kb.sem(w[0]), w[1]))
            for tc4 in range(4):
                for n in range(4):
                    xgi = 16 * tt + 4 * tc4 + n
                    xp = pb0 if xgi % 2 == 0 else pb3
                    xb = A_bf[:, 8192 + 512 * (xgi % 2):
                              8192 + 512 * (xgi % 2) + 512]
                    kb.wt("tensor", mXe.get(xgi - 2))
                    for c in range(NCH):
                        kb.op("tensor", lambda e, tc4=tc4, n=n, c=c, ub=ub,
                              xp=xp: e.matmul(
                            out=xp[:, 0:512],
                            lhsT=ub[:, 512 * c + 128 * tc4:
                                    512 * c + 128 * tc4 + 128],
                            rhs=wslc(0, c, 2304, 512, 512 * n),
                            start=(c == 0), stop=False))
                    kb.op("tensor", lambda e, n=n, xp=xp: e.matmul(
                        out=xp[:, 0:512], lhsT=onesrow_b[0:1, :],
                        rhs=bmx_row[0:1, 512 * n:512 * (n + 1)],
                        start=False, stop=True))
                    mXm = kb.mk("tensor")
                    kb.op("scalar", lambda e, xp=xp, xb=xb: e.copy(
                        out=xb, in_=xp[:, 0:512]),
                        wait=[w for w in (mXm, wXd.get(xgi - 2))
                              if w is not None])
                    mXe[xgi] = kb.mk("scalar")
                    kb.plan["sync"].append(lambda e, w=mXe[xgi]: e.wait_ge(
                        kb.sem(w[0]), w[1]))
                    kb.dma("sync", "gxst%d" % (xgi % 2),
                           xs_dram[512 * tt + 128 * tc4:
                                   512 * tt + 128 * (tc4 + 1),
                                   512 * n:512 * (n + 1)], xb)
                    wXd[xgi] = kb.gv("gxst%d" % (xgi % 2))
            for mm2 in (16, 17):
                kb.wt("tensor", mXbc.get((tt, mm2 - 1)),
                      mXbc.get((tt - 1, 17)) if mm2 == 16 else None)
                for c in range(NCH):
                    kb.op("tensor", lambda e, mm2=mm2, c=c, ub=ub: e.matmul(
                        out=pb1[:, 0:512], lhsT=wslc(0, c, 2304, 128, 128 * mm2),
                        rhs=ub[:, 512 * c:512 * (c + 1)],
                        start=(c == 0), stop=(c == 7)))
                mXt = kb.mk("tensor")
                if mm2 == 16:
                    kb.op("scalar", lambda e, tt=tt: e.activation(
                        out=Bfm[:, 512 * tt:512 * (tt + 1)], in_=pb1[0:64, 0:512],
                        func=AF.Identity, bias=bias_mx[0:64, 16:17]),
                        wait=[mXt])
                    kb.op("scalar", lambda e, tt=tt: e.activation(
                        out=Cfm[:, 512 * tt:512 * (tt + 1)], in_=pb1[64:128, 0:512],
                        func=AF.Identity, bias=bias_mx[64:128, 16:17]))
                else:
                    kb.op("scalar", lambda e, tt=tt: e.copy(
                        out=mr0(512 * tt, 512), in_=pb1[0:8, 0:512]), wait=[mXt])
                mXbc[(tt, mm2)] = kb.mk("scalar")
            kb.bar(("tensor", "vector", "scalar"))
        for tc in range(16):
            kb.op("tensor", lambda e, tc=tc: e.transpose(
                out=pT[:, 0:64], in_=Bfm[:, 128 * tc:128 * (tc + 1)],
                identity=identb[0:64, 0:64]))
            kb.bar(("tensor",))
            kb.op("scalar", lambda e, tc=tc: e.copy(
                out=Btm[:, 64 * tc:64 * (tc + 1)], in_=pT[:, 0:64]))
            kb.bar(("scalar",))
        if os.environ.get("ATH_MK_MM") == "2":
            kb.bar(("tensor", "vector", "scalar"))
            return
        kb.op("vector", lambda e: e.tensor_tensor(
            out=mcol[0:8, 2:3], in0=bias_mx[0:8, 17:18], in1=cols[0:8, 84:85],
            op=ALU.add))
        kb.bar(("vector",))
        kb.op("scalar", lambda e: e.copy(out=pb3[0:8, 256:257],
                                         in_=mcol[0:8, 2:3]))
        kb.op("scalar", lambda e: e.drain())
        kb.op("scalar", lambda e: e.copy(out=mcol[32:40, 2:3],
                                         in_=pb3[0:8, 256:257]))
        kb.bar(("scalar",))
        kb.op("vector", lambda e: e.tensor_scalar(
            out=mr0(), in0=mr0(), scalar1=mcol[32:40, 2:3], scalar2=None,
            op0=ALU.add))
        kb.bar(("vector",))
        for i4 in range(4):
            PP = pb3[0:8, 0:512]
            kb.op("scalar", lambda e, i4=i4: e.activation(
                out=PP, in_=mr0(512 * i4, 512), func=AF.Abs))
            kb.op("scalar", lambda e: e.drain())
            kb.op("scalar", lambda e: e.activation(out=PP, in_=PP, func=AF.Exp,
                                                   scale=-1.0))
            kb.op("scalar", lambda e: e.drain())
            kb.op("scalar", lambda e: e.activation(out=PP, in_=PP, func=AF.Ln,
                                                   bias=fconst[0:8, 1:2]))
            kb.bar(("scalar",))
            kb.op("vector", lambda e, i4=i4: e.tensor_scalar_max(
                out=mr0(512 * i4, 512), in0=mr0(512 * i4, 512), scalar1=0.0))
            kb.op("vector", lambda e: e.drain())
            kb.op("vector", lambda e, i4=i4: e.tensor_tensor(
                out=mr0(512 * i4, 512), in0=mr0(512 * i4, 512), in1=PP,
                op=ALU.add))
            kb.bar(("vector",))
        for cc2 in range(NTC):
            kb.op("scalar", lambda e, cc2=cc2: e.activation(
                out=pb3[0:8, 0:128], in_=mr0(128 * cc2, 128), func=AF.Ln))
            kb.bar(("scalar",))
            kb.op("vector", lambda e, cc2=cc2: e.tensor_scalar_mul(
                out=pb3[0:8, 128:256], in0=mr0(128 * cc2, 128),
                scalar1=mcol[32:40, 1:2]))
            kb.op("vector", lambda e: e.drain())
            kb.op("vector", lambda e, cc2=cc2: e.tensor_tensor_scan(
                out=P8(128 * cc2, 128), data0=pb3[0:8, 128:256],
                data1=nmlT[0:8, :], initial=0.0, op0=ALU.add, op1=ALU.bypass))
            kb.op("vector", lambda e: e.drain())
            kb.op("vector", lambda e, cc2=cc2: e.tensor_tensor(
                out=nmlT[0:8, :], in0=pb3[0:8, 0:128], in1=P8(128 * cc2, 128),
                op=ALU.subtract))
            kb.bar(("vector",))
            kb.op("tensor", lambda e, cc2=cc2: e.transpose(
                out=pA[0:128, 8 * cc2:8 * cc2 + 8], in_=nmlT[0:8, :],
                identity=identf[0:8, 0:8]))
            kb.bar(("tensor",))
        kb.op("scalar", lambda e: e.copy(out=sk_sc[:, 0:128],
                                         in_=pA[0:128, 0:128]))
        kb.bar(("scalar",))
        kb.op("tensor", lambda e: e.matmul(
            out=pb0[0:128, 0:8], lhsT=onesrow_f[0:1, :],
            rhs=smallrows[0:1, 8:16], start=True, stop=True))
        kb.bar(("tensor",))
        kb.op("vector", lambda e: e.tensor_copy(out=dsk8, in_=pb0[0:128, 0:8]))
        kb.bar(("vector",))
        for h in range(8):
            kb.op("vector", lambda e, h=h: e.tensor_scalar_mul(
                out=Dident[:, 128 * h:128 * (h + 1)], in0=identb[:],
                scalar1=dsk8[:, h:h + 1]), wait=[w_um] if h == 0 else ())
        kb.op("vector", lambda e: e.drain())
        kb.op("vector", lambda e: e.memset(Hst, 0.0))
        kb.bar(("vector",))
        if os.environ.get("ATH_MK_MM") == "3":
            kb.bar(("tensor", "vector", "scalar"))
            return
        wmt3 = kb.wv("bar_tensor")
        kb.dma("gpsimd", "gwo",
               w_arena[:, 0:16384].rearrange("p (c f) -> p c f", c=16),
               wout_d.rearrange("(c p) f -> p c f", p=128), wait=[wmt3])
        ld_wo = kb.gv("gwo")
        M2O, EO, CMEO, WXSO, EREPO, GMO = 16384, 16640, 16896, 17152, 17664, 17920
        m2b = lambda h: A_bf[:, M2O + 128 * (h % 2):M2O + 128 * (h % 2) + 128]
        eb = lambda h: A_bf[:, EO + 128 * (h % 2):EO + 128 * (h % 2) + 128]
        cmeb = lambda h: A_bf[0:64, CMEO + 128 * (h % 2):CMEO + 128 * (h % 2) + 128]
        wxb = lambda h: A_bf[:, WXSO + 256 * (h % 2):WXSO + 256 * (h % 2) + 256]
        erb = lambda h: A_bf[0:64, EREPO + 128 * (h % 2):EREPO + 128 * (h % 2) + 128]
        skrow = lambda h: (sk_row if h % 2 == 0 else sk_row2)
        dslot = lambda h: (0 if h % 2 == 0 else 384)
        wargc = lambda h: mcol[:, (9 if h % 2 == 0 else 3):(10 if h % 2 == 0 else 4)]
        wcolc = lambda h: mcol[:, (10 if h % 2 == 0 else 4):(11 if h % 2 == 0 else 5)]
        gamc = lambda h: mcol[:, (11 if h % 2 == 0 else 5):(12 if h % 2 == 0 else 6)]
        mS1 = {}; mS2 = {}; mVa = {}; mVm = {}; mVw = {}; mVy = {}
        mVh = {}; mTy = {}; mTx = {}
        mVh_prev = {}; mTo_last = None; mVo = {}; mGm = None
        hist = {}
        w_xst0, w_xst1 = kb.gv("gxst0"), kb.gv("gxst1")
        wbt0, wbv0 = kb.wv("bar_tensor"), kb.wv("bar_vector")
        SCN = int(os.environ.get("ATH_MK_SC", "16"))
        SBAR = int(os.environ.get("ATH_MK_SBAR", "2"))
        for cc in range(SCN):
            par = cc % 2
            h2 = hist.get(cc - 2, (None, None, None))
            wx = ([kb.gv("gxs%d" % par), h2[0], h2[1]]
                  if cc >= 2 else [w_xst0, w_xst1, wbt0, wbv0])
            kb.dma("gpsimd", "gxs%d" % par, A_bf[:, 2048 * par:2048 * (par + 1)],
                   xs_dram[128 * cc:128 * (cc + 1), :],
                   wait=[w for w in wx if w is not None])
            wxs_ = kb.gv("gxs%d" % par)
            wg2 = ([kb.gv("ggb%d" % par), h2[2]]
                   if cc >= 2 else [*w_g, wbt0, wbv0])
            kb.dma("gpsimd", "ggb%d" % par,
                   A_bf[:, 4096 + 2048 * par:4096 + 2048 * (par + 1)].rearrange(
                       "p (c t) -> p c t", c=16),
                   g_dram[:, 128 * cc:128 * (cc + 1)].rearrange(
                       "(c p) t -> p c t", p=128),
                   wait=[w for w in wg2 if w is not None])
            wgb = kb.gv("ggb%d" % par)
            xsb = A_bf[:, 2048 * par:2048 * (par + 1)]
            gb = A_bf[:, 4096 + 2048 * par:4096 + 2048 * (par + 1)]
            kb.wt("tensor", mGm)
            kb.op("tensor", lambda e, cc=cc: e.matmul(
                out=pb0[:, 0:128], lhsT=Bfm[:, 128 * cc:128 * (cc + 1)],
                rhs=Cfm[:, 128 * cc:128 * (cc + 1)], start=True, stop=True))
            mGt = kb.mk("tensor")
            kb.op("vector", lambda e: e.tensor_tensor(
                out=A_bf[:, GMO:GMO + 128], in0=pb0[:, 0:128], in1=lt01b,
                op=ALU.mult), wait=[mGt])
            mGm = kb.mk("vector")
            kb.wt("tensor", mS1.get(6), mS1.get(7))
            kb.op("tensor", lambda e, cc=cc: e.matmul(
                out=pb3[0:1, 384:512], lhsT=identf[0:8, 0:1],
                rhs=repf[0:8, 128 * cc:128 * (cc + 1)], start=True, stop=True))
            mT0 = kb.mk("tensor")
            kb.op("scalar", lambda e: e.copy(out=sk_row[0:1, :],
                                             in_=pb3[0:1, 384:512]),
                  wait=[mT0])
            mS0 = kb.mk("scalar")
            mS1.clear(); mS2.clear(); mVa.clear(); mVm.clear(); mVw.clear()
            mVh_prev = dict(mVh); mVy.clear(); mVh.clear(); mTy.clear()
            mTx.clear()
            mS1[-1] = mS0
            if SBAR >= 2:
                kb.bar(("tensor", "vector", "scalar"))
            for h in range(8):
                if SBAR >= 1:
                    kb.bar(("tensor", "vector", "scalar"))
                bcol = sk_sc[:, 8 * cc + h:8 * cc + h + 1]
                kb.wt("tensor", mS1.get(h - 2), mVa.get(h - 1), mS2.get(h - 1),
                      mS1.get(h - 1))
                kb.op("tensor", lambda e, h=h: e.matmul(
                    out=pb1[:, dslot(h):dslot(h) + 128], lhsT=onesrow_f[0:1, :],
                    rhs=skrow(h)[0:1, :], start=True, stop=True))
                kb.op("tensor", lambda e, h=h: e.matmul(
                    out=pb1[0:64, 128:256], lhsT=onesrow_f[0:1, 0:64],
                    rhs=skrow(h)[0:1, :], start=True, stop=True))
                kb.op("tensor", lambda e, h=h: e.matmul(
                    out=pb1[:, 256:257], lhsT=onesrow_f[0:1, :],
                    rhs=skrow(h)[0:1, 127:128], start=True, stop=True))
                if h < 7:
                    kb.op("tensor", lambda e, h=h, cc=cc: e.matmul(
                        out=pb3[0:1, 384:512], lhsT=identf[0:8, h + 1:h + 2],
                        rhs=repf[0:8, 128 * cc:128 * (cc + 1)],
                        start=True, stop=True))
                mT1 = kb.mk("tensor")
                kb.op("vector", lambda e, h=h, bcol=bcol: e.tensor_scalar(
                    out=pb1[:, dslot(h):dslot(h) + 128],
                    in0=pb1[:, dslot(h):dslot(h) + 128], scalar1=bcol,
                    scalar2=20.0, op0=ALU.add, op1=ALU.min),
                    wait=[mT1])
                mTmin = kb.mk("vector")
                if h < 7:
                    kb.op("scalar", lambda e, h=h: e.copy(
                        out=skrow(h + 1)[0:1, :], in_=pb3[0:1, 384:512]),
                        wait=[mT1])
                kb.op("scalar", lambda e, h=h: e.activation(
                    out=eb(h), in_=pb1[:, dslot(h):dslot(h) + 128], func=AF.Exp),
                    wait=[mTmin])
                kb.op("scalar", lambda e, h=h: e.activation(
                    out=erb(h), in_=pb1[0:64, 128:256], func=AF.Exp))
                mS1[h] = kb.mk("scalar")
                kb.op("vector", lambda e, h=h, bcol=bcol: e.tensor_tensor(
                    out=wargc(h), in0=pb1[:, 256:257], in1=bcol, op=ALU.add),
                    wait=[mT1])
                mVa[h] = kb.mk("vector")
                kb.op("scalar", lambda e, h=h: e.activation(
                    out=wcolc(h), in_=wargc(h), func=AF.Exp), wait=[mVa[h]])
                kb.op("scalar", lambda e, h=h: e.activation(
                    out=gamc(h), in_=pb1[:, 256:257], func=AF.Exp))
                mS2[h] = kb.mk("scalar")
                kb.op("vector", lambda e, h=h, cc=cc: e.tensor_tensor(
                    out=cmeb(h), in0=Cfm[:, 128 * cc:128 * (cc + 1)],
                    in1=erb(h), op=ALU.mult), wait=[mS1[h]])
                kb.op("vector", lambda e, h=h: e.tensor_tensor(
                    out=m2b(h), in0=A_bf[:, GMO:GMO + 128], in1=eb(h),
                    op=ALU.mult))
                kb.op("vector", lambda e: e.drain())
                kb.op("vector", lambda e, h=h: e.tensor_tensor(
                    out=m2b(h), in0=m2b(h), in1=Dident[:, 128 * h:128 * (h + 1)],
                    op=ALU.add))
                mVm[h] = kb.mk("vector")
                kb.op("vector", lambda e, h=h, xsb=xsb: e.tensor_scalar_mul(
                    out=wxb(h), in0=xsb[:, 256 * h:256 * (h + 1)],
                    scalar1=wcolc(h)),
                    wait=[w for w in ([mS2[h], mTx.get(h - 2)] +
                                      ([wxs_] if h == 0 else []))
                          if w is not None])
                mVw[h] = kb.mk("vector")
                kb.wt("tensor", mVm[h], mVy.get(h - 1), mVh_prev.get(h))
                if h == 0:
                    kb.wt("tensor", wxs_)
                for dh2 in range(2):
                    kb.op("tensor", lambda e, h=h, dh2=dh2, xsb=xsb: e.matmul(
                        out=(pb2 if dh2 == 0 else pb3)[:, 0:128],
                        lhsT=xsb[:, 256 * h + 128 * dh2:256 * h + 128 * (dh2 + 1)],
                        rhs=m2b(h), start=True, stop=False))
                    kb.op("tensor", lambda e, h=h, dh2=dh2: e.matmul(
                        out=(pb2 if dh2 == 0 else pb3)[:, 0:128],
                        lhsT=Hst[:, 256 * h + 128 * dh2:256 * h + 128 * (dh2 + 1)],
                        rhs=cmeb(h), start=False, stop=True))
                mTy[h] = kb.mk("tensor")
                kb.wt("tensor", mVw[h], mVh.get(h - 1))
                kb.op("tensor", lambda e, h=h, cc=cc: e.matmul(
                    out=pA[0:64, 0:256], lhsT=Btm[:, 64 * cc:64 * (cc + 1)],
                    rhs=wxb(h), start=True, stop=True))
                mTx[h] = kb.mk("tensor")
                ygw = [mTy[h]] + \
                    ([wgb, w_xst0, w_xst1] if h == 0 else []) + \
                    ([mTo_last] if mTo_last is not None and h == 0 else [])
                for dh2 in range(2):
                    ygo = 8192 + 512 * (2 * h + dh2) + 128 * (cc % 4)
                    kb.op("vector", lambda e, h=h, dh2=dh2, ygo=ygo, gb=gb:
                          e.tensor_tensor(
                              out=A_bf[:, ygo:ygo + 128],
                              in0=(pb2 if dh2 == 0 else pb3)[:, 0:128],
                              in1=gb[:, 128 * (2 * h + dh2):
                                     128 * (2 * h + dh2 + 1)],
                              op=ALU.mult),
                          wait=ygw if dh2 == 0 else ())
                mVy[h] = kb.mk("vector")
                kb.op("vector", lambda e, h=h: e.scalar_tensor_tensor(
                    out=Hst[:, 256 * h:256 * (h + 1)],
                    in0=Hst[:, 256 * h:256 * (h + 1)], scalar=gamc(h)[0:64, :],
                    in1=pA[0:64, 0:256], op0=ALU.mult, op1=ALU.add),
                    wait=[mTx[h]])
                mVh[h] = kb.mk("vector")
            hist[cc] = (mTx[7], mVw[7], mVy[7])
            if cc % 4 == 3:
                kb.wt("tensor", ld_wo, mVy[7], mVo.get(6), mVo.get(7))
                tt4 = cc // 4
                mTo = {}
                for m in range(NCH):
                    op_ps = pA[:, 512:1024] if m % 2 == 0 else pA[:, 1024:1536]
                    kb.wt("tensor", mVo.get(m - 2))
                    for kc in range(16):
                        kb.op("tensor", lambda e, m=m, kc=kc, op_ps=op_ps:
                              e.matmul(
                                  out=op_ps,
                                  lhsT=wslc(0, kc, 1024, 128, 128 * m),
                                  rhs=A_bf[:, 8192 + 512 * kc:
                                           8192 + 512 * (kc + 1)],
                                  start=(kc == 0), stop=(kc == 15)))
                    mTo[m] = kb.mk("tensor")
                    kb.op("vector", lambda e, m=m, tt4=tt4, op_ps=op_ps:
                          e.scalar_tensor_tensor(
                              out=xch(m)[:, 512 * tt4:512 * (tt4 + 1)],
                              in0=op_ps, scalar=cols[:, 74 + m:75 + m],
                              in1=xch(m)[:, 512 * tt4:512 * (tt4 + 1)],
                              op0=ALU.add, op1=ALU.add), wait=[mTo[m]])
                    mVo[m] = kb.mk("vector")
                mTo_last = mTo[NCH - 1]
        kb.bar(("tensor", "vector", "scalar"))

    for _eng in ("tensor", "vector", "scalar", "gpsimd"):
        for _w in (ld_x, ld_const, ld_cb, ld_mc):
            kb.plan[_eng].append(lambda e, w=_w: e.wait_ge(kb.sem(w[0]), w[1]))
    STAGE = int(os.environ.get("ATH_MK_STAGE", "99"))
    boundary_stats()
    if STAGE >= 2:
        setup_bias_qkv()
        w_y = ffn_ustage(1)
    if STAGE >= 3:
        ffn_route(1)
    if STAGE >= 4:
        ffn_light(1)
    if STAGE >= 5:
        ffn_heavy(1)
    if STAGE >= 6:
        boundary_stats()
        mamba()

    if DEBUG:
        dum = kb.debug_out("um", (D, T), BF16)
        dg = kb.debug_out("g", (INNER, T), BF16)
        dxs = kb.debug_out("xs", (T, INNER), BF16)
        dp8 = kb.debug_out("p8", (8, T))
        dbfm = kb.debug_out("bfm", (64, T), BF16)
        dcfm = kb.debug_out("cfm", (64, T), BF16)
        kb.dma("gpsimd", "gdbg3", dum, um_dram)
        kb.dma("gpsimd", "gdbg3", dg, g_dram)
        kb.dma("gpsimd", "gdbg3", dxs, xs_dram)
        kb.dma("sync", "gdbg4", dp8, repf[0:8, :], wait=[kb.wv("bar_vector")])
        kb.dma("sync", "gdbg4", dbfm, Bfm)
        kb.dma("sync", "gdbg4", dcfm, Cfm)
        for _e in ("vector", "scalar", "tensor", "gpsimd"):
            for _g in ("gdbg3", "gdbg4"):
                kb.plan[_e].append(lambda e, w=kb.gv(_g): e.wait_ge(
                    kb.sem(w[0]), w[1]))

    if STAGE >= 7:
        boundary_stats()
        _wl3 = kb.wv("bar_tensor")
        load_w("gw1b", WQKV_O, wqkv_d, 8, 1536, wait=[_wl3])
        ld_w1b = load_w("gw1b", WOUTL_O, woutl_d, 4, 1024)
        kb.plan["tensor"].append(lambda e, w=ld_w1b: e.wait_ge(kb.sem(w[0]), w[1]))
        ffn_ustage(3)
        ffn_route(3)
    if STAGE >= 8:
        ffn_light(3)
    if STAGE >= 9:
        ffn_heavy(3)

    if DEBUG:
        dmu = kb.debug_out("mu", (4, 512))
        dvar = kb.debug_out("var", (4, 512))
        dinv = kb.debug_out("inv", (4, 512))
        dxc = kb.debug_out("xc", (128, NCH * T), BF16)
        dul = kb.debug_out("ul", (128, NCH * T), BF16)
        dy = kb.debug_out("y", (T, D), BF16)
        didx = kb.debug_out("idx", (1664, 4))
        kb.dma("sync", "gdbg", dmu, statr(SL_MU))
        kb.dma("sync", "gdbg", dvar, statr(SL_VAR))
        kb.dma("sync", "gdbg", dinv, statr(SL_INV))
        kb.dma("sync", "gdbg", dxc, xc_sb[:])
        kb.dma("sync", "gdbg", dul, A_bf[:, 0:16384])
        kb.dma("gpsimd", "gdbg2", dy, y_dram, wait=w_y)
        kb.dma("gpsimd", "gdbg2", didx, idx_dram)

    kb.bar(("vector", "scalar", "tensor"))
    wfin = [kb.wv("bar_vector"), kb.wv("bar_scalar"), kb.wv("bar_tensor")]
    out_r = out_d.rearrange("(c p) t -> p c t", p=128)
    for i in range(4):
        kb.dma("sync", "gout", out_r[:, 2 * i:2 * i + 2, :],
               x_r[:, 2 * i:2 * i + 2, :], wait=wfin)
    fin = [kb.gv("gout")] + ([kb.gv("gdbg"), kb.gv("gdbg2"), kb.gv("gdbg3"),
                              kb.gv("gdbg4")] if DEBUG else [])
    kb.op("sync", lambda e: e.nop(), wait=fin)
    return kb


def make_consts():
    c = {}
    c["ident"] = np.eye(128, dtype=np.float32)
    c["onesrow"] = np.ones((1, 128), np.float32)
    e4 = np.zeros((128, 16), np.float32)
    for tt in range(4):
        e4[:, 4 * tt + tt] = 1.0
    c["e4"] = e4
    ind2 = np.zeros((128, 2), np.float32)
    ind2[0:64, 0] = 1.0
    ind2[64:128, 1] = 1.0
    c["ind2"] = ind2
    c["ind2t"] = ind2.T.copy()
    iota = (np.arange(16)[None, :] * 128 + np.arange(128)[:, None]).astype(np.float32)
    c["iota_t_in"] = iota
    u1 = np.zeros((2, 128), np.float32)
    u1[0, 0:64] = 1.0
    u1[1, 64:128] = 1.0
    c["maskA_l_in"] = u1
    vA = np.zeros((2, 256), np.float32)
    vA[0, 192:256] = -1e9
    vA[1, 0:64] = -1e9
    c["maskA_r_in"] = vA
    c["maskT_r_in"] = vA.copy()
    c["logk_in"] = np.log(np.array([[512 * 9.0 / 8.0, 1024 * 9.0 / 8.0]], np.float32))
    pc = np.zeros((128, 2), np.float32)
    pc[0:64, 0] = -1e9
    pc[64:128, 1] = -1e9
    c["padcolT_in"] = pc
    fc = np.zeros((128, 4), np.float32)
    fc[:, 0] = 1e-5
    fc[:, 1] = 1.0
    fc[:, 3] = 1e-12
    c["fconst"] = fc
    c["iota2_in"] = (np.arange(128)[:, None] * 16 + np.arange(16)[None, :]
                     ).astype(np.float32)
    c["jmat"] = np.broadcast_to(np.arange(16, dtype=np.float32), (128, 16)).copy()
    c["neg1e9"] = np.full((1, 256), -1e9, np.float32)
    eg = np.zeros((2, 256), np.float32)
    eg[0, 0:64] = -1e9
    eg[1, 192:256] = -1e9
    c["edgeLR"] = eg
    lk = np.log(np.array([576.0, 1152.0], np.float32)).astype(np.float32)
    c["skinit"] = (lk - np.float32(np.log(np.float32(2048.0)))).reshape(1, 2)
    s_i = np.arange(128)[:, None]
    t_i = np.arange(128)[None, :]
    c["cmask_in"] = np.where(s_i <= t_i, 1.0, 0.0).astype(np.float32)
    return c


def kernel(L0_attn_input_tensor, L0_attn_ln_gamma, L0_attn_ln_beta, L0_attn_W_in,
           L0_attn_b_in, L0_attn_A_log, L0_attn_D_skip, L0_attn_dt_bias,
           L0_attn_W_out, L0_attn_b_out, L0_ffn_ln_w, L0_ffn_ln_b, L0_ffn_Wqkv,
           L0_ffn_Wout_l, L0_ffn_rt_q, L0_ffn_rt_kv, L0_ffn_gamma, L0_ffn_Wq,
           L0_ffn_Wkv, L0_ffn_Wout_h, L0_ffn_null_kv, L0_ffn_null_q):
    f32 = lambda a: np.ascontiguousarray(np.asarray(a), dtype=np.float32)
    x = f32(L0_attn_input_tensor)
    colchunk = lambda v: np.ascontiguousarray(f32(v).reshape(-1, 128).T)
    shared = {
        "wqkv": np.ascontiguousarray(f32(L0_ffn_Wqkv).T),
        "woutl": np.ascontiguousarray(f32(L0_ffn_Wout_l).T),
        "wouth": np.ascontiguousarray(f32(L0_ffn_Wout_h).T),
        "wq": np.ascontiguousarray(f32(L0_ffn_Wq).T),
        "wkv": np.ascontiguousarray(
            f32(L0_ffn_Wkv).T.reshape(1024, 8, 2, 64).transpose(0, 2, 1, 3)
            .reshape(1024, 1024)),
        "win": np.ascontiguousarray(np.pad(f32(L0_attn_W_in), ((0, 0), (0, 120)))),
        "wout": f32(L0_attn_W_out),
        "lnw_c": colchunk(L0_ffn_ln_w),
        "lnb_c": colchunk(L0_ffn_ln_b),
        "mgam_c": colchunk(L0_attn_ln_gamma),
        "mbet_c": colchunk(L0_attn_ln_beta),
        "hgam_c": colchunk(L0_ffn_gamma),
        "bin_c": colchunk(np.pad(f32(L0_attn_b_in), (0, 120))),
        "bout_c": colchunk(L0_attn_b_out),
        "alog": f32(L0_attn_A_log).reshape(1, 8),
        "dskip": f32(L0_attn_D_skip).reshape(1, 8),
        "dtbias": f32(L0_attn_dt_bias).reshape(1, 8),
        "m8c_in": np.ascontiguousarray(np.stack(
            [f32(L0_attn_A_log), f32(L0_attn_D_skip), f32(L0_attn_dt_bias),
             np.zeros(8, np.float32)], axis=1)),
        "binr_in": np.ascontiguousarray(
            np.pad(f32(L0_attn_b_in), (0, 120))[2048:4352].reshape(1, 2304)),
        "rt2": np.ascontiguousarray(
            np.stack([f32(L0_ffn_rt_q).ravel(), f32(L0_ffn_rt_kv).ravel()], axis=1)),
        "nullqrep_in": np.ascontiguousarray(
            np.broadcast_to(f32(L0_ffn_null_q), (128, D))),
        "nullvrow_in": f32(L0_ffn_null_kv)[1].reshape(1, 512),
    }
    nk = f32(L0_ffn_null_kv)
    nullcol = np.zeros((D,), np.float32)
    for h in range(8):
        nullcol[128 * h:128 * h + 64] = nk[0, h]
        nullcol[128 * h + 64:128 * h + 128] = nk[1, h]
    shared["nullcol_c"] = colchunk(nullcol)
    shared.update(make_consts())

    kb = build()
    nc = kb.finish()
    in_maps = []
    for b in range(8):
        m = dict(shared)
        m["x0"] = np.ascontiguousarray(x[b].T)
        in_maps.append(m)
    res = run_bass_kernel_spmd(nc, in_maps, core_ids=list(range(8)))
    global _last_results
    _last_results = res.results[0]
    out = np.stack([np.ascontiguousarray(r["out"].T) for r in res.results], axis=0)
    return out


_last_results = None


# version 31
# speedup vs baseline: 1.0834x; 1.0642x over previous
import os
import numpy as np
from contextlib import ExitStack
import concourse.bass as bass
import concourse.mybir as mybir
from concourse.bass_utils import run_bass_kernel_spmd

F32 = mybir.dt.float32
BF16 = mybir.dt.bfloat16
I32 = mybir.dt.int32
AF = mybir.ActivationFunctionType
ALU = mybir.AluOpType
AX = mybir.AxisListType
IOA = bass.IndirectOffsetOnAxis

T = 2048
D = 1024
NCH = D // 128
NTT = T // 512
NTC = T // 128
INNER = 2048
NQ, NKV = 512, 1024
EPS_LN = 1e-5

DEBUG = False


class KB:

    ENGS = ("tensor", "vector", "scalar", "gpsimd", "sync")

    def __init__(self):
        self.nc = bass.Bass("TRN2", target_bir_lowering=False)
        self.ctx = ExitStack()
        self.plan = {e: [] for e in self.ENGS}
        self.sems = {}
        self.semv = {}
        self.dbg = {}

    def sem(self, name):
        if name not in self.sems:
            self.sems[name] = self.ctx.enter_context(self.nc.semaphore(name))
            self.semv[name] = 0
        return self.sems[name]

    def sb(self, name, shape, dt=F32):
        return self.ctx.enter_context(self.nc.sbuf_tensor(name, list(shape), dt))

    def psum(self, name, shape, dt=F32):
        return self.ctx.enter_context(self.nc.psum_tensor(name, list(shape), dt))

    def op(self, eng, fn, wait=(), inc=None, incn=1):
        if inc is not None:
            self.sem(inc)
            self.semv[inc] += incn
            target = self.semv[inc]
        waits = tuple(wait)
        cell = {"extra": None}

        def closure(e, fn=fn, waits=waits, inc=inc, incn=incn, cell=cell):
            for s, v in waits:
                e.wait_ge(self.sems[s], v)
            ins = fn(e)
            if inc is not None:
                assert ins is not None
                ins.then_inc(self.sems[inc], incn)
            elif cell["extra"] is not None:
                assert ins is not None, "bar attached to non-instruction op"
                ins.then_inc(self.sems[cell["extra"][0]], cell["extra"][1])
        self.plan[eng].append(closure)
        if not hasattr(self, "_last"):
            self._last = {}
        self._last[eng] = (cell, inc is not None)
        if inc is not None:
            return (inc, target)
        return None

    def dma(self, eng, group, out, in_, wait=(), **kw):
        return self.op(eng, lambda e: e.dma_start(out=out, in_=in_, **kw),
                       wait=wait, inc="g_" + group, incn=16)

    def dmat(self, group, out, in_, wait=()):
        return self.op("sync", lambda e: e.dma_start_transpose(out, in_),
                       wait=wait, inc="g_" + group, incn=16)

    def gv(self, group):
        return ("g_" + group, self.semv.get("g_" + group, 0))

    def bar(self, producers, waiters=None):
        targets = []
        for eng in producers:
            bn = "bar_" + eng
            self.sem(bn)
            cell, has_inc = self._last.get(eng, (None, True))
            if cell is None or has_inc or cell["extra"] is not None:
                if eng == "vector":
                    def _fv(e):
                        e.drain()
                        return e.memset(self.dummy["vector"][:], 0.0)
                    self.op("vector", _fv, inc=bn)
                elif eng in ("scalar", "gpsimd"):
                    def _fs(e, eng=eng):
                        e.drain()
                        return e.memzero(self.dummy[eng][:])
                    self.op(eng, _fs, inc=bn)
                elif eng == "tensor":
                    self.op("tensor", self.tdummy, inc=bn)
                else:
                    raise AssertionError(f"bar: no op to carry inc on {eng}")
            else:
                self.semv[bn] += 1
                cell["extra"] = (bn, 1)
            targets.append((bn, self.semv[bn]))
        for eng in (waiters if waiters is not None else self.ENGS):
            for bn, v in targets:
                self.plan[eng].append(
                    lambda e, bn=bn, v=v: e.wait_ge(self.sems[bn], v))
        return targets

    def wv(self, semname):
        return (semname, self.semv.get(semname, 0))

    def mk(self, eng):
        return self.bar((eng,), waiters=())[0]

    def wt(self, eng, *ws):
        for w in ws:
            if w is None:
                continue
            self.plan[eng].append(
                lambda e, w=w: e.wait_ge(self.sem(w[0]), w[1]))

    def debug_out(self, name, shape, dt=F32):
        h = self.nc.dram_tensor("dbg_" + name, list(shape), dt, kind="ExternalOutput")
        self.dbg[name] = h.ap()
        return self.dbg[name]

    def finish(self):
        plan = self.plan
        with self.ctx:
            with self.nc.Block() as block:
                @block.tensor
                def _(e):
                    for fn in plan["tensor"]:
                        fn(e)

                @block.vector
                def _(e):
                    for fn in plan["vector"]:
                        fn(e)

                @block.scalar
                def _(e):
                    for fn in plan["scalar"]:
                        fn(e)

                @block.gpsimd
                def _(e):
                    for fn in plan["gpsimd"]:
                        fn(e)

                @block.sync
                def _(e):
                    for fn in plan["sync"]:
                        fn(e)
        return self.nc


def build():
    kb = KB()
    nc = kb.nc
    kb.dummy = {e: kb.sb("dummy_" + e, (1, 8), F32)
                for e in ("vector", "scalar", "gpsimd")}
    kb.dummyb = kb.sb("dummyb_sb", (128, 8), BF16)
    kb.tdummy = lambda e: e.transpose(out=kb.pdummy[0:8, 512:520],
                                      in_=kb.dummyb[0:8, 0:8],
                                      identity=kb.dummyb[0:8, 0:8])
    dram = lambda n, shp, dt=F32, kind="ExternalInput": nc.dram_tensor(
        n, list(shp), dt, kind=kind).ap()

    x0_d = dram("x0", (D, T))
    out_d = dram("out", (D, T), kind="ExternalOutput")
    wqkv_d = dram("wqkv", (D, 1536))
    woutl_d = dram("woutl", (512, D))
    wq_d = dram("wq", (D, 512))
    wkv_d = dram("wkv", (D, 1024))
    wouth_d = dram("wouth", (512, D))
    win_d = dram("win", (D, 4352))
    wout_d = dram("wout", (INNER, D))
    lnw_d = dram("lnw_c", (128, 8))
    lnb_d = dram("lnb_c", (128, 8))
    mgam_d = dram("mgam_c", (128, 8))
    mbet_d = dram("mbet_c", (128, 8))
    hgam_d = dram("hgam_c", (128, 8))
    bin_d = dram("bin_c", (128, 34))
    bout_d = dram("bout_c", (128, 8))
    alog_d = dram("alog", (1, 8))
    dskip_d = dram("dskip", (1, 8))
    dtb_d = dram("dtbias", (1, 8))
    rt2_d = dram("rt2", (D, 2))
    nullcol_d = dram("nullcol_c", (128, 8))
    nullvrow_d = dram("nullvrow_in", (1, 512))
    nullqrep_d = dram("nullqrep_in", (128, D))
    ident_d = dram("ident", (128, 128))
    ones_d = dram("onesrow", (1, 128))
    e4_d = dram("e4", (128, 16))
    ind2_d = dram("ind2", (128, 2))
    ind2t_d = dram("ind2t", (2, 128))
    iota_t_d = dram("iota_t_in", (128, 16))
    maskA_l_d = dram("maskA_l_in", (2, 128))
    maskA_r_d = dram("maskA_r_in", (2, 256))
    maskT_r_d = dram("maskT_r_in", (2, 256))
    logk_d = dram("logk_in", (1, 2))
    padcolT_d = dram("padcolT_in", (128, 2))
    fconst_d = dram("fconst", (128, 4))
    iota2_d = dram("iota2_in", (128, 16))
    skinit_d = dram("skinit", (1, 2))
    jmat_d = dram("jmat", (128, 16))
    neg1e9_d = dram("neg1e9", (1, 256))
    edgeLR_d = dram("edgeLR", (2, 256))
    y_dram = dram("y_scr", (T, D), BF16, kind="Internal")
    rows_dram = dram("rows_scr", (16, T), F32, kind="Internal")
    idx_dram = dram("idx_scr", (1664, 4), F32, kind="Internal")
    hs_dram = dram("hs_scr", (T, D), BF16, kind="Internal")

    x_sb = kb.sb("x_sb", (128, NCH * T), F32)
    xc_sb = kb.sb("xc_sb", (128, NCH * T), BF16)
    repf = kb.sb("repf", (128, T), F32)
    repb = kb.sb("repb", (128, T), BF16)
    stat = kb.sb("stat", (4, 6 * 512), F32)
    SL_MU, SL_VAR, SL_INV, SL_S2, SL_T0, SL_T1 = range(6)
    cols = kb.sb("cols", (128, 96), F32)
    colsb = kb.sb("colsb", (128, 96), BF16)
    onescol_b = kb.sb("onescol_b", (128, 1), BF16)
    onescol_f = kb.sb("onescol_f", (128, 1), F32)
    e4_b = kb.sb("e4_b", (128, 16), BF16)
    identb = kb.sb("identb", (128, 128), BF16)
    identf = kb.sb("identf", (128, 128), F32)
    onesrow_b = kb.sb("onesrow_b", (1, 128), BF16)
    ind2_b = kb.sb("ind2_b", (128, 2), BF16)
    ind2t_b = kb.sb("ind2t_b", (2, 128), BF16)
    ptgt2 = kb.sb("ptgt2", (128, 16), I32)
    maskA_l = kb.sb("maskA_l", (2, 128), BF16)
    maskA_r = kb.sb("maskA_r", (2, 256), BF16)
    maskT_r = kb.sb("maskT_r", (2, 256), BF16)
    logk_row = kb.sb("logk_row", (1, 2), F32)
    padcolT = kb.sb("padcolT", (128, 2), F32)
    fconst = kb.sb("fconst_sb", (128, 4), F32)
    nullq_bh = kb.sb("nullq_bh", (128, D), BF16)
    nullvrow_b = kb.sb("nullvrow_b", (1, 512), BF16)
    nullcol_b = kb.sb("nullcol_b", (128, 8), BF16)
    smallrows = kb.sb("smallrows", (1, 32), F32)
    rt2_sb = kb.sb("rt2_sb", (128, 16), F32)

    w_arena = kb.sb("w_arena", (128, 18432), BF16)
    A_bf = kb.sb("A_bf", (128, 18432), BF16)
    ab = lambda off, n: A_bf[:, off:off + n]
    xs_scr = lambda off, n: xc_sb[:, off:off + n]

    pA = kb.psum("pA", (128, 1536), F32)
    pb0 = kb.psum("pb0", (128, 512), F32)
    pb1 = kb.psum("pb1", (128, 512), F32)
    pb2 = kb.psum("pb2", (128, 512), F32)
    pb3 = kb.psum("pb3", (128, 512), F32)
    pT = kb.psum("pT", (128, 1024), BF16)
    kb.pdummy = pT

    xch = lambda c: x_sb[:, c * T:(c + 1) * T]
    xcch = lambda c: xc_sb[:, c * T:(c + 1) * T]
    statr = lambda s: stat[:, s * 512:(s + 1) * 512]
    stat_tile = lambda s, tt: stat[tt:tt + 1, s * 512:(s + 1) * 512]

    iota2 = kb.sb("iota2", (128, 16), F32)
    ptgt_i = kb.sb("ptgt_i", (128, 16), I32)
    gld_f = kb.sb("gld_f", (128, 48), F32)
    zrow = kb.sb("zrow", (128, 52), F32)
    onesrow_f = kb.sb("onesrow_f", (1, 128), F32)
    skinit_row = kb.sb("skinit_row", (1, 2), F32)
    jmat = kb.sb("jmat_sb", (128, 16), F32)
    zrow_r = zrow[:].rearrange("p (g k) -> p g k", k=4)

    nmlb_dram = dram("nmlb_scr", (2, 4096), BF16, kind="Internal")
    negm_seg = kb.sb("negm_seg", (128, 32), F32)
    s_seg = kb.sb("s_seg", (128, 32), F32)
    nmlT = kb.sb("nmlT", (32, 128), F32)
    nmlT_h = kb.sb("nmlT_h", (32, 128), BF16)
    nmlT_l = kb.sb("nmlT_l", (32, 128), BF16)
    zrow_b = kb.sb("zrow_b", (1, 512), BF16)
    neg1e9_row = kb.sb("neg1e9_row", (1, 256), BF16)
    bv_row = kb.sb("bv_row", (1, 512), BF16)
    u2row = kb.sb("u2row", (1, 128), BF16)
    edgeL_row = kb.sb("edgeL_row", (1, 256), BF16)
    edgeR_row = kb.sb("edgeR_row", (1, 256), BF16)
    bqkv_col = kb.sb("bqkv_col", (128, 12), F32)
    bvT = kb.sb("bvT", (4, 128), F32)
    bq_scaled = kb.sb("bq_scaled", (128, 4), F32)

    x0_r = x0_d.rearrange("(c p) t -> p c t", p=128)
    x_r = x_sb[:].rearrange("p (c t) -> p c t", c=NCH)
    for i in range(4):
        kb.dma("sync", "gx", x_r[:, 2 * i:2 * i + 2, :], x0_r[:, 2 * i:2 * i + 2, :])
    ld_x = kb.gv("gx")

    kb.dma("sync", "gc", identf[:], ident_d)
    kb.dma("sync", "gc", logk_row[:], logk_d)
    kb.dma("sync", "gc", padcolT[:], padcolT_d)
    kb.dma("sync", "gc", fconst[:], fconst_d)
    kb.dma("sync", "gc", cols[:, 0:8], lnw_d)
    kb.dma("sync", "gc", cols[:, 8:16], lnb_d)
    kb.dma("sync", "gc", cols[:, 16:24], mgam_d)
    kb.dma("sync", "gc", cols[:, 24:32], mbet_d)
    kb.dma("sync", "gc", cols[:, 32:40], hgam_d)
    kb.dma("sync", "gc", cols[:, 40:74], bin_d)
    kb.dma("sync", "gc", cols[:, 74:82], bout_d)
    kb.dma("sync", "gc", smallrows[0:1, 0:8], alog_d)
    kb.dma("sync", "gc", smallrows[0:1, 8:16], dskip_d)
    kb.dma("sync", "gc", smallrows[0:1, 16:24], dtb_d)
    kb.dma("sync", "gc", onescol_f[:], ones_d.rearrange("r p -> p r"))
    kb.dma("sync", "gc", onesrow_f[:], ones_d)
    kb.dma("sync", "gc", iota2[:], iota2_d)
    kb.dma("sync", "gc", jmat[:], jmat_d)
    kb.dma("sync", "gc", skinit_row[:], skinit_d)
    kb.dma("sync", "gc", rt2_sb[:].rearrange("p (c r) -> p c r", c=8),
           rt2_d.rearrange("(c p) r -> p c r", p=128))
    ld_const = kb.gv("gc")

    for dst, src in ((identb[:], ident_d), (onescol_b[:], ones_d.rearrange("r p -> p r")),
                     (e4_b[:], e4_d), (onesrow_b[:], ones_d), (ind2_b[:], ind2_d),
                     (ind2t_b[:], ind2t_d), (maskA_l[:], maskA_l_d),
                     (maskA_r[:], maskA_r_d), (maskT_r[:], maskT_r_d),
                     (neg1e9_row[:], neg1e9_d), (u2row[:], maskA_l_d[1:2, :]),
                     (edgeL_row[:], edgeLR_d[0:1, :]), (edgeR_row[:], edgeLR_d[1:2, :]),
                     (nullvrow_b[:], nullvrow_d), (nullcol_b[:], nullcol_d),
                     (kb.dummyb[:], ident_d[:, 0:8])):
        kb.dma("gpsimd", "gcb", dst, src)
    ld_cb = kb.gv("gcb")

    WQKV_O, WOUTL_O = 0, 12288
    WQ_O, WKV_O, WOUTH_O = 0, 4096, 12288
    WINZ_O, WINX_O, WOUT_O = 0, 0, 0

    def load_w(grp, off, src_d, nk, f, wait=()):
        kb.dma("gpsimd", grp,
               w_arena[:, off:off + nk * f].rearrange("p (c f) -> p c f", c=nk),
               src_d.rearrange("(c p) f -> p c f", p=128), wait=wait)
        return kb.gv(grp)

    load_w("gw1", WQKV_O, wqkv_d, 8, 1536)
    ld_w1 = load_w("gw1", WOUTL_O, woutl_d, 4, 1024)
    wslc = lambda off, kc, fdim, ncols=128, col0=0: w_arena[
        :, off + kc * fdim + col0: off + kc * fdim + col0 + ncols]

    kb.dma("gpsimd", "gnq", nullq_bh[:], nullqrep_d)
    ld_cb2 = kb.gv("gnq")
    kb.op("scalar", lambda e: e.activation(out=nullq_bh[:], in_=nullq_bh[:],
                                           func=AF.Identity, scale=0.5),
          wait=[ld_cb2])
    kb.bar(("scalar",))

    kb.op("vector", lambda e: e.memset(zrow[:], 0.0), wait=[ld_const])
    kb.bar(("vector",))

    sqb0 = ab(0, 2048)
    sqb1 = ab(2048, 2048)

    def boundary_stats():
        mC = {}
        for c in range(NCH):
            kb.op("vector", lambda e, c=c: e.tensor_copy(out=xcch(c), in_=xch(c)))
            mC[c] = kb.mk("vector")
        for c in range(NCH):
            kb.wt("tensor", mC[c])
            for tt in range(NTT):
                kb.op("tensor", lambda e, tt=tt, c=c: e.matmul(
                    out=pb0[0:4, :], lhsT=e4_b[:, 4 * tt:4 * tt + 4],
                    rhs=xcch(c)[:, tt * 512:(tt + 1) * 512],
                    start=(tt == 0 and c == 0), stop=(tt == 3 and c == 7)))
        kb.bar(("tensor",))
        kb.op("scalar", lambda e: e.activation(out=statr(SL_MU), in_=pb0[0:4, :],
                                               func=AF.Identity, scale=1.0 / D))
        kb.bar(("scalar",))
        kb.op("vector", lambda e: e.tensor_scalar_mul(out=statr(SL_T0),
                                                      in0=statr(SL_MU), scalar1=-1.0))
        kb.bar(("vector",))
        kb.dma("sync", "gsmu", rows_dram[0:1, :].rearrange("r (p j) -> (r p) j", p=4),
               statr(SL_T0))
        w1 = kb.gv("gsmu")
        kb.dma("gpsimd", "gbmu", repf[:], rows_dram[0:1, :].to_broadcast((128, T)),
               wait=[w1])
        w2 = kb.gv("gbmu")
        mCC = {}
        for c in range(NCH):
            kb.op("vector", lambda e, c=c: e.tensor_tensor(
                out=xcch(c), in0=xch(c), in1=repf[:], op=ALU.add),
                wait=[w2] if c == 0 else ())
            mCC[c] = kb.mk("vector")
        sq = (sqb0, sqb1)
        for c in range(NCH):
            buf = sq[c % 2]
            w_mm = (("sqmm", kb.semv.get("sqmm", 0)),) if c >= 2 else ()
            wv = kb.op("scalar", lambda e, buf=buf, c=c: e.activation(
                out=buf, in_=xcch(c), func=AF.Square),
                wait=list(w_mm) + [mCC[c]], inc="sq")
            for tt in range(NTT):
                kb.op("tensor", lambda e, buf=buf, tt=tt, c=c: e.matmul(
                    out=pb1[0:4, :], lhsT=e4_b[:, 4 * tt:4 * tt + 4],
                    rhs=buf[:, tt * 512:(tt + 1) * 512],
                    start=(c == 0 and tt == 0), stop=(c == 7 and tt == 3)),
                    wait=[wv] if tt == 0 else (),
                    inc="sqmm" if tt == 3 else None)
        kb.bar(("vector",))
        kb.bar(("tensor",))
        kb.op("scalar", lambda e: e.activation(out=statr(SL_VAR), in_=pb1[0:4, :],
                                               func=AF.Identity, scale=1.0 / D))
        kb.op("scalar", lambda e: e.drain())
        kb.op("scalar", lambda e: e.activation(out=statr(SL_T1), in_=statr(SL_VAR),
                                               func=AF.Ln, bias=fconst[0:4, 0:1]))
        kb.op("scalar", lambda e: e.drain())
        kb.op("scalar", lambda e: e.activation(out=statr(SL_INV), in_=statr(SL_T1),
                                               func=AF.Exp, scale=-0.5))
        kb.bar(("scalar",))

    def bcast_row(slot, dram_slot, dest, grp, wait_extra=()):
        kb.dma("sync", "gs" + grp, rows_dram[dram_slot:dram_slot + 1, :].rearrange(
            "r (p j) -> (r p) j", p=4), statr(slot), wait=wait_extra)
        w1 = kb.gv("gs" + grp)
        kb.dma("gpsimd", "gb" + grp, dest,
               rows_dram[dram_slot:dram_slot + 1, :].to_broadcast((128, T)), wait=[w1])
        return kb.gv("gb" + grp)

    yf_dram = dram("yf_scr", (D, T), BF16, kind="Internal")
    ytm0 = ab(16384, 1024)
    ytm1 = ab(17408, 1024)
    inv_tm = kb.sb("inv_tm", (128, 16), F32)
    ulch = lambda c: ab(c * T, T)

    def ffn_ustage(call):
        tg = f"f{call}"
        kb.op("vector", lambda e: e.tensor_tensor(out=statr(SL_T0), in0=statr(SL_INV),
                                                  in1=statr(SL_INV), op=ALU.mult))
        kb.op("vector", lambda e: e.drain())
        kb.op("vector", lambda e: e.tensor_tensor(out=statr(SL_T0), in0=statr(SL_T0),
                                                  in1=statr(SL_VAR), op=ALU.mult))
        kb.bar(("vector",))
        kb.op("scalar", lambda e: e.activation(out=statr(SL_T1), in_=statr(SL_T0),
                                               func=AF.Ln, bias=fconst[0:4, 0:1]))
        kb.op("scalar", lambda e: e.drain())
        kb.op("scalar", lambda e: e.activation(out=statr(SL_T1), in_=statr(SL_T1),
                                               func=AF.Exp, scale=-0.5))
        kb.op("scalar", lambda e: e.drain())
        kb.bar(("scalar",))
        kb.op("vector", lambda e: e.tensor_tensor(out=statr(SL_S2), in0=statr(SL_INV),
                                                  in1=statr(SL_T1), op=ALU.mult))
        kb.bar(("vector",))
        w_rep = bcast_row(SL_S2, 2, repb[:], "sl")
        for j in range(4):
            kb.op("tensor", lambda e, j=j: e.transpose(
                out=pb2[0:128, 0:16].rearrange("p (a b) -> p a b", b=4)[:, :, j],
                in_=stat[0:4, SL_INV * 512 + 128 * j:SL_INV * 512 + 128 * (j + 1)],
                identity=identf[0:4, 0:4]))
        kb.bar(("tensor",))
        kb.op("scalar", lambda e: e.copy(out=inv_tm[:], in_=pb2[0:128, 0:16]),
              inc="invtm")
        w_invtm = kb.wv("invtm")
        for c in range(NCH):
            kb.op("vector", lambda e, c=c: e.scalar_tensor_tensor(
                out=ulch(c), in0=xcch(c), scalar=cols[:, c:c + 1],
                in1=repb[:], op0=ALU.mult, op1=ALU.mult),
                wait=[w_rep] if c == 0 else ())
        kb.bar(("vector",))
        xc_r = xc_sb[:].rearrange("p (c t) -> p c t", c=NCH)
        yf_r = yf_dram.rearrange("(c p) t -> p c t", p=128)
        for i in range(2):
            kb.dma("sync", "gyf", yf_r[:, 4 * i:4 * i + 4, :], xc_r[:, 4 * i:4 * i + 4, :])
        w_yf = kb.gv("gyf")
        ybufs = (ytm0, ytm1)
        for tc in range(NTC):
            par = tc % 2
            buf = ybufs[par]
            wt = [w_yf] if tc < 2 else [kb.gv("gyt%d" % par)]
            kb.dmat("gytt%d" % par, buf, yf_dram[:, tc * 128:(tc + 1) * 128],
                    wait=wt)
            w_t = kb.gv("gytt%d" % par)
            kb.op("vector", lambda e, buf=buf, tc=tc: e.tensor_scalar_mul(
                out=buf, in0=buf, scalar1=inv_tm[:, tc:tc + 1]),
                wait=[w_t, w_invtm] if tc < 2 else [w_t], inc="yts")
            mYs = kb.mk("vector")
            kb.plan["sync"].append(lambda e, w=mYs: e.wait_ge(
                kb.sem(w[0]), w[1]))
            kb.dma("sync", "gyt%d" % par, y_dram[tc * 128:(tc + 1) * 128, :], buf)
        return [kb.gv("gyt0"), kb.gv("gyt1")]

    srt_q = kb.sb("srt_q", (128, 16), F32)
    srt_kv = kb.sb("srt_kv", (128, 16), F32)
    sk_sc = kb.sb("sk_sc", (128, 16 * 8), F32)
    skr = lambda i: sk_sc[:, 16 * i:16 * (i + 1)]
    sk_cols = kb.sb("sk_cols", (128, 8), F32)
    sk_small = kb.sb("sk_small", (2, 16), F32)
    sk_row = kb.sb("sk_row", (1, 128), F32)
    sk_row2 = kb.sb("sk_row2", (1, 128), F32)
    msr_col = kb.sb("msr_col", (2, 1), F32)
    scatv = kb.sb("scatv", (128, 64), F32)
    gidx = kb.sb("gidx", (128, 12), I32)
    gmeta = kb.sb("gmeta", (128, 36), F32)

    route_w = {}

    def route_ms(ridx, s_tm):
        o = 7 * ridx
        c0 = 0 if ridx == 0 else 7
        kb.op("vector", lambda e, c0=c0: e.tensor_reduce(
            out=sk_cols[:, c0:c0 + 1], in_=s_tm[:], axis=AX.X, op=ALU.max))
        mv = kb.mk("vector")
        kb.op("tensor", lambda e, c0=c0: e.transpose(
            out=pb3[0:1, 0:128], in_=sk_cols[:, c0:c0 + 1], identity=identf[:]),
            wait=[mv])
        mt = kb.mk("tensor")
        kb.op("vector", lambda e, o=o: e.tensor_reduce(
            out=sk_small[0:1, o + 6:o + 7], in_=pb3[0:1, 0:128], axis=AX.X,
            op=ALU.max), wait=[mt])
        kb.op("vector", lambda e: e.drain())
        kb.op("vector", lambda e, o=o, ridx=ridx: e.tensor_copy(
            out=sk_small[0:1, o:o + 1],
            in_=skinit_row[0:1, ridx:ridx + 1]))
        kb.op("vector", lambda e: e.drain())

    def route_iters():
        pT2 = None
        pVc = None
        pS1 = None
        pS2 = None
        for it in range(49):
            for o in (0, 7):
                kb.op("vector", lambda e, o=o: e.tensor_scalar_mul(
                    out=sk_small[0:1, o + 1:o + 2], in0=sk_small[0:1, o:o + 1],
                    scalar1=-1.0), wait=[pS2] if pS2 and o == 0 else ())
            kb.op("vector", lambda e: e.drain())
            for o in (0, 7):
                kb.op("vector", lambda e, o=o: e.tensor_tensor(
                    out=sk_small[0:1, o + 3:o + 4],
                    in0=sk_small[0:1, o + 6:o + 7],
                    in1=sk_small[0:1, o + 1:o + 2], op=ALU.min))
            kb.op("vector", lambda e: e.drain())
            for o in (0, 7):
                kb.op("vector", lambda e, o=o: e.tensor_scalar_mul(
                    out=sk_small[0:1, o + 2:o + 3], in0=sk_small[0:1, o + 3:o + 4],
                    scalar1=-1.0))
            mV1 = kb.mk("vector")
            for r, o in ((0, 0), (1, 7)):
                kb.op("tensor", lambda e, r=r, o=o: e.matmul(
                    out=pb3[0:128, 4 * r:4 * r + 2], lhsT=onesrow_f[0:1, :],
                    rhs=sk_small[0:1, o + 1:o + 3], start=True, stop=True),
                    wait=([mV1] if r == 0 else []) +
                    ([pVc] if pVc and r == 0 else []))
            mT1 = kb.mk("tensor")
            for r in (0, 1):
                kb.op("vector", lambda e, r=r: e.tensor_copy(
                    out=sk_cols[:, 1 + 3 * r:3 + 3 * r],
                    in_=pb3[0:128, 4 * r:4 * r + 2]),
                    wait=([mT1] if r == 0 else []) +
                    ([pS1] if pS1 and r == 0 else []))
            kb.op("vector", lambda e: e.drain())
            pVc = kb.mk("vector")
            for r, s_tm in ((0, srt_q), (1, srt_kv)):
                kb.op("vector", lambda e, r=r, s_tm=s_tm: e.tensor_scalar(
                    out=skr(3 + r), in0=s_tm[:],
                    scalar1=sk_cols[:, 1 + 3 * r:2 + 3 * r], scalar2=None,
                    op0=ALU.min))
            mV2 = kb.mk("vector")
            for r in (0, 1):
                kb.op("scalar", lambda e, r=r: e.activation(
                    out=skr(3 + r), in_=skr(3 + r), func=AF.Exp,
                    bias=sk_cols[:, 2 + 3 * r:3 + 3 * r],
                    accum_out=sk_cols[:, 3 + 3 * r:4 + 3 * r]),
                    wait=([mV2] if r == 0 else []) +
                    ([pT2] if pT2 and r == 0 else []))
                kb.op("scalar", lambda e: e.drain())
            mS1 = kb.mk("scalar")
            pS1 = mS1
            for r in (0, 1):
                kb.op("tensor", lambda e, r=r: e.matmul(
                    out=pb3[0:1, 130 + r:131 + r],
                    lhsT=sk_cols[:, 3 + 3 * r:4 + 3 * r], rhs=onescol_f[:],
                    start=True, stop=True),
                    wait=([mS1] if r == 0 else []) +
                    ([pS2] if pS2 and r == 0 else []))
            pT2 = kb.mk("tensor")
            for r, o in ((0, 0), (1, 7)):
                kb.op("scalar", lambda e, r=r, o=o: e.activation(
                    out=sk_small[0:1, o + 4:o + 5],
                    in_=pb3[0:1, 130 + r:131 + r], func=AF.Ln),
                    wait=[pT2] if r == 0 else ())
            kb.op("scalar", lambda e: e.drain())
            pS2 = kb.mk("scalar")
            for r, o in ((0, 0), (1, 7)):
                kb.op("vector", lambda e, r=r, o=o: e.tensor_tensor(
                    out=sk_small[0:1, o + 5:o + 6],
                    in0=logk_row[0:1, r:r + 1],
                    in1=sk_small[0:1, o + 2:o + 3], op=ALU.add),
                    wait=[pS2] if r == 0 else ())
            kb.op("vector", lambda e: e.drain())
            for o in (0, 7):
                kb.op("vector", lambda e, o=o: e.tensor_tensor(
                    out=sk_small[0:1, o:o + 1], in0=sk_small[0:1, o + 5:o + 6],
                    in1=sk_small[0:1, o + 4:o + 5], op=ALU.subtract))
            kb.op("vector", lambda e: e.drain())
        kb.bar(("vector", "scalar", "tensor"))

    def route_one(call, ridx, s_tm, kk, base, trash):
        tg = "r%d_%d" % (call, ridx)
        U = 3; EE = 3; ONES = 4; PFX = 5; CC = 6; TSC = 7; PT = 6; D_ = 0
        kb.op("tensor", lambda e, ridx=ridx: e.matmul(
            out=pb3[0:128, 2:3], lhsT=onesrow_f[0:1, :],
            rhs=sk_small[0:1, 7 * ridx:7 * ridx + 1], start=True, stop=True))
        kb.bar(("tensor",))
        kb.op("vector", lambda e: e.tensor_copy(out=sk_cols[:, 4:5],
                                                in_=pb3[0:128, 2:3]))
        kb.op("vector", lambda e: e.drain())
        kb.op("vector", lambda e: e.tensor_scalar(
            out=skr(U), in0=s_tm[:], scalar1=sk_cols[:, 4:5], scalar2=None,
            op0=ALU.add))
        kb.op("vector", lambda e: e.drain())
        kb.op("vector", lambda e: e.tensor_single_scalar(
            out=skr(ONES), in_=skr(U), scalar=0.0, op=ALU.is_ge))
        kb.op("vector", lambda e: e.tensor_scalar_min(
            out=skr(CC), in0=skr(U), scalar1=0.0))
        kb.bar(("vector",))
        kb.op("scalar", lambda e: e.activation(
            out=skr(CC), in_=skr(CC), func=AF.Exp))
        kb.bar(("scalar",))
        kb.op("vector", lambda e: e.tensor_scalar(
            out=skr(TSC), in0=skr(CC), scalar1=-1.0, scalar2=1.0,
            op0=ALU.mult, op1=ALU.add))
        kb.op("vector", lambda e: e.drain())
        kb.op("vector", lambda e: e.tensor_tensor(
            out=skr(TSC), in0=skr(TSC), in1=skr(CC), op=ALU.add))
        kb.op("vector", lambda e: e.tensor_tensor_scan(
            out=skr(PFX), data0=skr(ONES), data1=skr(ONES), initial=0.0,
            op0=ALU.add, op1=ALU.bypass))
        kb.bar(("vector",))
        kb.op("tensor", lambda e: e.transpose(
            out=pb3[0:1, 128:256], in_=skr(PFX)[:, 15:16], identity=identf[:]))
        kb.bar(("tensor",))
        kb.op("vector", lambda e: e.tensor_copy(out=sk_row2[0:1, :],
                                                in_=pb3[0:1, 128:256]))
        kb.op("vector", lambda e: e.drain())
        kb.op("vector", lambda e: e.tensor_tensor_scan(
            out=sk_row[0:1, :], data0=sk_row2[0:1, :], data1=sk_row2[0:1, :],
            initial=0.0, op0=ALU.add, op1=ALU.bypass))
        kb.bar(("vector",))
        kb.op("tensor", lambda e: e.transpose(
            out=pb3[0:128, 3:4], in_=sk_row[0:1, :], identity=identf[0:1, 0:1]))
        kb.bar(("tensor",))
        kb.op("vector", lambda e: e.tensor_tensor(
            out=sk_cols[:, 5:6], in0=pb3[0:128, 3:4], in1=skr(PFX)[:, 15:16],
            op=ALU.subtract))
        kb.op("vector", lambda e: e.drain())
        kb.op("vector", lambda e: e.tensor_scalar(
            out=skr(PFX), in0=skr(PFX), scalar1=sk_cols[:, 5:6], scalar2=None,
            op0=ALU.add))
        kb.op("vector", lambda e: e.drain())
        kb.op("vector", lambda e: e.tensor_single_scalar(
            out=skr(D_), in_=skr(PFX), scalar=float(kk), op=ALU.is_le))
        kb.op("vector", lambda e: e.drain())
        kb.op("vector", lambda e: e.tensor_tensor(
            out=skr(ONES), in0=skr(ONES), in1=skr(D_), op=ALU.mult))
        kb.op("vector", lambda e: e.drain())
        kb.op("vector", lambda e: e.tensor_scalar(
            out=skr(D_), in0=skr(PFX), scalar1=1.0 - float(base) + float(trash),
            scalar2=None, op0=ALU.subtract))
        kb.op("vector", lambda e: e.drain())
        kb.op("vector", lambda e: e.tensor_tensor(
            out=skr(D_), in0=skr(D_), in1=jmat[:], op=ALU.subtract))
        kb.op("vector", lambda e: e.drain())
        kb.op("vector", lambda e: e.tensor_tensor(
            out=skr(D_), in0=skr(D_), in1=skr(ONES), op=ALU.mult))
        kb.op("vector", lambda e: e.drain())
        kb.op("vector", lambda e: e.tensor_scalar(
            out=skr(PT), in0=skr(D_), scalar1=float(trash), scalar2=None,
            op0=ALU.add))
        kb.op("vector", lambda e: e.drain())
        kb.op("vector", lambda e: e.tensor_tensor(
            out=skr(PT), in0=skr(PT), in1=jmat[:], op=ALU.add))
        kb.op("vector", lambda e: e.drain())
        return TSC, PT

    def route_scatter(call, ridx, TSC, PT, wait_prev=()):
        tg = "sc%d_%d" % (call, ridx)
        ptgt = ptgt_i if ridx == 0 else ptgt2
        svb = scatv[:] if ridx == 0 else sk_sc[:, 0:64]
        kb.op("vector", lambda e: e.tensor_scalar(
            out=skr(PT), in0=skr(PT), scalar1=0.0, scalar2=1663.0,
            op0=ALU.max, op1=ALU.min), wait=wait_prev)
        kb.op("vector", lambda e: e.drain())
        kb.op("vector", lambda e, ptgt=ptgt: e.tensor_copy(
            out=ptgt[:], in_=skr(PT)))
        sv = svb.rearrange("p (j k) -> p j k", k=4)
        kb.op("vector", lambda e, sv=sv: e.tensor_copy(out=sv[:, :, 0],
                                                       in_=iota2[:]))
        kb.op("vector", lambda e, sv=sv: e.tensor_copy(out=sv[:, :, 1],
                                                       in_=skr(TSC)))
        kb.op("vector", lambda e, sv=sv: e.tensor_copy(out=sv[:, :, 2],
                                                       in_=gld_f[:, 0:16]))
        kb.op("vector", lambda e, sv=sv: e.tensor_copy(out=sv[:, :, 3],
                                                       in_=gld_f[:, 0:16]))
        mScv = kb.mk("vector")
        kb.plan["gpsimd"].append(lambda e, w=mScv: e.wait_ge(
            kb.sem(w[0]), w[1]))
        ws_all = []
        for j in range(16):
            gname = "gsc%d" % (j % 8)
            w = [kb.gv(gname)] if kb.semv.get("g_" + gname, 0) else []
            kb.op("gpsimd", lambda e, j=j, ptgt=ptgt, svb=svb:
                  e.indirect_dma_start(
                      out=idx_dram, out_offset=IOA(ptgt[:, j:j + 1], 0),
                      in_=svb[:, 4 * j:4 * j + 4], in_offset=None),
                  wait=w, inc="g_" + gname, incn=16)
        for g in range(8):
            ws_all.append(kb.gv("gsc%d" % g))
        return ws_all

    def ffn_route(call):
        tg = "rt%d" % call
        kb.plan["tensor"].append(
            lambda e, w=kb.wv("invtm"): e.wait_ge(kb.sem(w[0]), w[1]))
        for tt in range(NTT):
            bp = 0 if tt % 2 == 0 else 32
            for c in range(NCH):
                kb.op("tensor", lambda e, tt=tt, c=c, bp=bp: e.matmul(
                    out=pb2[bp:bp + 2, 0:512],
                    lhsT=rt2_sb[:, 2 * c:2 * c + 2],
                    rhs=xch(c)[:, tt * 512:(tt + 1) * 512],
                    start=(c == 0), stop=(c == 7)))
            kb.bar(("tensor",))
            slot = (SL_T0, SL_T1, SL_S2, SL_VAR)[tt]
            kb.op("scalar", lambda e, slot=slot, bp=bp: e.copy(
                out=stat[0:2, slot * 512:slot * 512 + 512],
                in_=pb2[bp:bp + 2, 0:512]))
            kb.bar(("scalar",))
        kb.bar(("scalar",))
        for tt in range(NTT):
            slot = (SL_T0, SL_T1, SL_S2, SL_VAR)[tt]
            for r in range(2):
                kb.dma("sync", "gsr" + tg,
                       rows_dram[3 + r:4 + r, tt * 512:(tt + 1) * 512],
                       stat[r:r + 1, slot * 512:slot * 512 + 512])
        w_sr = kb.gv("gsr" + tg)
        for c in range(NCH):
            kb.op("tensor", lambda e, c=c: e.matmul(
                out=pb3[0:1, 256:258], lhsT=onescol_f[:],
                rhs=rt2_sb[:, 2 * c:2 * c + 2],
                start=(c == 0), stop=(c == 7)))
        kb.bar(("tensor",))
        kb.op("vector", lambda e: e.tensor_copy(out=sk_small[0:1, 14:16],
                                                in_=pb3[0:1, 256:258]))
        kb.op("vector", lambda e: e.drain())
        kb.op("vector", lambda e: e.tensor_tensor(
            out=statr(SL_T0), in0=statr(SL_MU), in1=statr(SL_INV), op=ALU.mult),
            wait=[w_sr])
        kb.op("vector", lambda e: e.drain())
        kb.op("vector", lambda e: e.tensor_scalar_mul(
            out=statr(SL_T0), in0=statr(SL_T0), scalar1=-1.0))
        kb.bar(("vector",))
        kb.dma("sync", "gsh" + tg,
               rows_dram[9:10, :].rearrange("r (p j) -> (r p) j", p=4),
               statr(SL_T0))
        w_sh = kb.gv("gsh" + tg)
        def ldr(row, dst, w):
            kb.dma("gpsimd", "gld" + tg, dst,
                   rows_dram[row:row + 1, :].rearrange("r (p j) -> p (r j)", p=128),
                   wait=w)
        kb.dma("sync", "gsi" + tg,
               rows_dram[1:2, :].rearrange("r (p j) -> (r p) j", p=4),
               statr(SL_INV))
        w_si = kb.gv("gsi" + tg)
        ldr(3, srt_q[:], [w_sr])
        ldr(4, srt_kv[:], [w_sr])
        ldr(0, skr(0), [])
        ldr(1, skr(1), [w_si])
        ldr(9, skr(2), [w_sh])
        w_ld = kb.gv("gld" + tg)
        kb.plan["vector"].append(
            lambda e, w=w_ld: e.wait_ge(kb.sem(w[0]), w[1]))
        for ridx, s_tm in ((0, srt_q), (1, srt_kv)):
            kb.bar(("vector",))
            kb.op("tensor", lambda e, ridx=ridx: e.matmul(
                out=pb3[0:128, 4:5], lhsT=onesrow_f[0:1, :],
                rhs=sk_small[0:1, 14 + ridx:15 + ridx], start=True, stop=True))
            kb.bar(("tensor",))
            kb.op("vector", lambda e: e.tensor_copy(out=sk_cols[:, 6:7],
                                                    in_=pb3[0:128, 4:5]))
            kb.op("vector", lambda e: e.drain())
            kb.op("vector", lambda e: e.tensor_scalar(
                out=skr(3), in0=skr(0), scalar1=sk_cols[:, 6:7], scalar2=None,
                op0=ALU.mult))
            kb.op("vector", lambda e: e.drain())
            kb.op("vector", lambda e, s_tm=s_tm: e.tensor_tensor(
                out=s_tm[:], in0=s_tm[:], in1=skr(3), op=ALU.add))
            kb.op("vector", lambda e: e.drain())
            kb.op("vector", lambda e, s_tm=s_tm: e.tensor_tensor(
                out=s_tm[:], in0=s_tm[:], in1=skr(1), op=ALU.mult))
            kb.op("vector", lambda e: e.drain())
        kb.op("vector", lambda e: e.tensor_copy(out=gld_f[:, 0:16],
                                                in_=skr(2)))
        kb.op("vector", lambda e: e.drain())
        kb.dma("sync", "gzi" + tg,
               idx_dram.rearrange("(p a) k -> p (a k)", p=128),
               zrow[:, 0:52])
        w_zi = kb.gv("gzi" + tg)
        kb.plan["gpsimd"].append(
            lambda e, w=w_zi: e.wait_ge(kb.sem(w[0]), w[1]))
        route_ms(0, srt_q)
        route_ms(1, srt_kv)
        route_iters()
        TSCq, PTq = route_one(call, 0, srt_q, 512, 0, 512)
        wsq = route_scatter(call, 0, TSCq, PTq, [w_zi])
        TSCk, PTk = route_one(call, 1, srt_kv, 1024, 528, 1552)
        if DEBUG and call == 1:
            dbg_pt = kb.debug_out("kv_pt", (128, 16))
            dbg_pf = kb.debug_out("kv_pfx", (128, 16))
            dbg_on = kb.debug_out("kv_ones", (128, 16))
            dbg_s = kb.debug_out("kv_s", (128, 16))
            kb.bar(("vector",))
            kb.dma("sync", "gdbgr", dbg_pt, skr(6))
            kb.dma("sync", "gdbgr", dbg_pf, skr(5))
            kb.dma("sync", "gdbgr", dbg_on, skr(4))
            kb.dma("sync", "gdbgr", dbg_s, srt_kv[:])
            dbg_rf = kb.debug_out("kv_rf", (128, 16))
            kb.dma("sync", "gdbgr", dbg_rf, skr(3))
        wsk = route_scatter(call, 1, TSCk, PTk, [])
        route_w[call] = (wsq, wsk)


    def setup_bias_qkv():
        kb.op("vector", lambda e: e.tensor_copy(out=colsb[:, 0:82],
                                                in_=cols[:, 0:82]),
              wait=[ld_const])
        kb.bar(("vector",))
        for m in range(12):
            for c in range(NCH):
                kb.op("tensor", lambda e, m=m, c=c: e.matmul(
                    out=pb0[0:128, m:m + 1],
                    lhsT=wslc(WQKV_O, c, 1536, 128, 128 * m),
                    rhs=colsb[:, 8 + c:9 + c],
                    start=(c == 0), stop=(c == 7)), wait=[ld_w1] if m == 0 and c == 0 else ())
        kb.bar(("tensor",))
        kb.op("scalar", lambda e: e.copy(out=bqkv_col[:], in_=pb0[0:128, 0:12]))
        kb.bar(("scalar",))
        kb.op("vector", lambda e: e.tensor_scalar_mul(
            out=bq_scaled[:], in0=bqkv_col[:, 0:4], scalar1=0.125))
        kb.bar(("vector",))
        kb.op("tensor", lambda e: e.transpose(
            out=pb1[0:4, 0:128], in_=bqkv_col[:, 8:12], identity=identf[:]))
        kb.bar(("tensor",))
        kb.op("scalar", lambda e: e.copy(out=bvT[:], in_=pb1[0:4, 0:128]))
        kb.bar(("scalar",))
        kb.dma("sync", "gbv", rows_dram[10:11, 0:512].rearrange(
            "r (m p) -> (r m) p", m=4), bvT[:])
        wbv = kb.gv("gbv")
        kb.dma("gpsimd", "gbv2", bv_row[:], rows_dram[10:11, 0:512], wait=[wbv])
        wbv2 = kb.gv("gbv2")
        kb.plan["tensor"].append(lambda e, w=wbv2: e.wait_ge(kb.sem(w[0]), w[1]))

    def ffn_light(call):
        tg = "lt%d" % call
        QSEG, KSEG, VSEG, OFM = 0, 4096, 9216, 11776
        qseg = xc_sb[0:66, QSEG:QSEG + 4096]
        kseg = xc_sb[0:66, KSEG:KSEG + 5120]
        vseg = xc_sb[:, VSEG:VSEG + 2560]
        vseg_hi = xc_sb[0:64, 13824:16384]
        ofm = xc_sb[:, OFM:OFM + 2048]
        PT_O, ADIS = 16384, 16896
        obank = [pA[:, 0:512], pA[:, 512:1024], pb2[:], pb3[:]]
        for _e in ("vector", "scalar", "tensor"):
            for _g in ("gyt0", "gyt1"):
                kb.plan[_e].append(
                    lambda e, w=kb.gv(_g): e.wait_ge(kb.sem(w[0]), w[1]))
        kb.op("vector", lambda e: e.memset(xc_sb[64:66, KSEG:KSEG + 5120], 1.0))
        kb.op("vector", lambda e: e.memset(zrow_b[:], 0.0))
        kb.bar(("vector",))
        mSeg = None
        mOutV = {}
        for seg in range(4):
            t0 = 512 * seg
            kv0 = t0 - 64
            r0, r1 = max(0, kv0), min(T, t0 + 576)
            if seg in (0, 3):
                kb.wt("vector", mSeg)
                if seg == 0:
                    kb.op("vector", lambda e: e.memset(
                        kseg[0:64, 0:5120].rearrange(
                            "p (h n) -> p h n", h=8)[:, :, 0:64], 0.0))
                    kb.op("vector", lambda e: e.memset(vseg[:, 0:512], 0.0))
                    kb.op("vector", lambda e: e.memset(vseg_hi[:, 0:512], 0.0))
                else:
                    kb.op("vector", lambda e: e.memset(
                        kseg[0:64, 0:5120].rearrange(
                            "p (h n) -> p h n", h=8)[:, :, 576:640], 0.0))
                    kb.op("vector", lambda e: e.memset(vseg[:, 2048:2560], 0.0))
                    kb.op("vector", lambda e: e.memset(vseg_hi[:, 2048:2560], 0.0))
                mPad = kb.mk("vector")
            else:
                mPad = None
            gi = 0
            mPm = {}
            mPe = {}
            def pbank(g):
                return pb0 if g % 2 == 0 else pb1
            for m in range(4):
                ps = pbank(gi)
                kb.wt("tensor", mPe.get(gi - 2),
                      mSeg if gi < 2 else None,
                      mOutV.get(0 if gi % 2 == 0 else 1) if gi < 2 else None,
                      mPad if gi == 0 else None)
                for c in range(NCH):
                    kb.op("tensor", lambda e, m=m, c=c, t0=t0, ps=ps: e.matmul(
                        out=ps[:, 0:512],
                        lhsT=wslc(WQKV_O, c, 1536, 128, 128 * m),
                        rhs=ulch(c)[:, t0:t0 + 512],
                        start=(c == 0), stop=(c == 7)),
                        wait=[ld_w1] if (call == 1 and seg == 0 and m == 0
                                         and c == 0) else ())
                mPm[gi] = kb.mk("tensor")
                for half in range(2):
                    kb.op("scalar", lambda e, m=m, half=half, ps=ps: e.activation(
                        out=qseg[0:64, 512 * (2 * m + half):
                                 512 * (2 * m + half) + 512],
                        in_=ps[64 * half:64 * half + 64, 0:512],
                        func=AF.Identity, scale=0.125,
                        bias=bq_scaled[64 * half:64 * half + 64, m:m + 1]),
                        wait=[mPm[gi]] if half == 0 else ())
                mPe[gi] = kb.mk("scalar")
                gi += 1
            nsub = [(r0, min(r0 + 512, r1))]
            if r0 + 512 < r1:
                nsub.append((r0 + 512, r1))
            for m in range(4):
                for (a0, b0) in nsub:
                    ps = pbank(gi)
                    kb.wt("tensor", mPe.get(gi - 2))
                    for c in range(NCH):
                        kb.op("tensor", lambda e, m=m, c=c, a0=a0, b0=b0, ps=ps:
                              e.matmul(
                                  out=ps[:, 0:b0 - a0],
                                  lhsT=wslc(WQKV_O, c, 1536, 128, 512 + 128 * m),
                                  rhs=ulch(c)[:, a0:b0],
                                  start=(c == 0), stop=(c == 7)))
                    mPm[gi] = kb.mk("tensor")
                    for half in range(2):
                        kb.op("scalar", lambda e, m=m, half=half, a0=a0, b0=b0,
                              kv0=kv0, ps=ps: e.activation(
                            out=kseg[0:64, 640 * (2 * m + half) + (a0 - kv0):
                                     640 * (2 * m + half) + (b0 - kv0)],
                            in_=ps[64 * half:64 * half + 64, 0:b0 - a0],
                            func=AF.Identity,
                            bias=bqkv_col[64 * half:64 * half + 64, 4 + m:5 + m]),
                            wait=[mPm[gi]] if half == 0 else ())
                    mPe[gi] = kb.mk("scalar")
                    gi += 1
            for kcl in range(5):
                a0 = max(r0, kv0 + 128 * kcl)
                b0 = min(r1, kv0 + 128 * (kcl + 1))
                row_off = a0 - (kv0 + 128 * kcl)
                ps = pbank(gi)
                kb.wt("tensor", mPe.get(gi - 2))
                for c in range(NCH):
                    kb.op("tensor", lambda e, c=c, a0=a0, b0=b0, row_off=row_off,
                          ps=ps: e.matmul(
                        out=ps[row_off:row_off + (b0 - a0), 0:512],
                        lhsT=ulch(c)[:, a0:b0],
                        rhs=wslc(WQKV_O, c, 1536, 512, 1024),
                        start=(c == 0), stop=False))
                kb.op("tensor", lambda e, a0=a0, b0=b0, row_off=row_off, ps=ps:
                      e.matmul(
                          out=ps[row_off:row_off + (b0 - a0), 0:512],
                          lhsT=onesrow_b[0:1, 0:b0 - a0],
                          rhs=bv_row[0:1, 0:512], start=False, stop=True))
                mPm[gi] = kb.mk("tensor")
                kb.op("scalar", lambda e, kcl=kcl, a0=a0, b0=b0, row_off=row_off,
                      ps=ps: e.copy(
                    out=vseg[row_off:row_off + (b0 - a0),
                             512 * kcl:512 * kcl + 512],
                    in_=ps[row_off:row_off + (b0 - a0), 0:512]),
                    wait=[mPm[gi]])
                hr0, hr1 = max(64, row_off), min(128, row_off + (b0 - a0))
                if hr1 > hr0:
                    kb.op("scalar", lambda e, kcl=kcl, hr0=hr0, hr1=hr1, ps=ps:
                          e.copy(
                              out=vseg_hi[hr0 - 64:hr1 - 64,
                                          512 * kcl:512 * kcl + 512],
                              in_=ps[hr0:hr1, 0:512]))
                mPe[gi] = kb.mk("scalar")
                gi += 1
            mProjT = mPm[gi - 1]
            mProjS = mPe[gi - 1]
            mProjS2 = mPe[gi - 2]
            aslots = (pA[:, 0:256], pA[:, 512:768], pb0[:, 0:256], pb1[:, 0:256])
            mAe = {}
            for g in range(8):
                kb.wt("tensor", mAe.get(g - 1),
                      mProjS if g == 0 else None, mProjS2 if g == 0 else None)
                for gi2 in range(4):
                    ti = 4 * g + gi2
                    h, qtl = ti // 4, ti % 4
                    slot = aslots[gi2]
                    kb.op("tensor", lambda e, h=h, qtl=qtl, slot=slot: e.matmul(
                        out=slot, lhsT=qseg[0:64, 512 * h + 128 * qtl:
                                            512 * h + 128 * qtl + 128],
                        rhs=kseg[0:64, 640 * h + 128 * qtl:
                                 640 * h + 128 * qtl + 256],
                        start=True, stop=False))
                    ex = []
                    if seg == 0 and qtl == 0:
                        ex = [(0, 0)]
                    if seg == 3 and qtl == 3:
                        ex = [(1, 192)]
                    kb.op("tensor", lambda e, slot=slot, st=(not ex): e.matmul(
                        out=slot, lhsT=maskA_l[:], rhs=maskA_r[:],
                        start=False, stop=st))
                    for (ur, c0) in ex:
                        kb.op("tensor", lambda e, slot=slot, ur=ur, c0=c0:
                              e.matmul(
                                  out=slot,
                                  lhsT=(maskA_l[0:1, :] if ur == 0
                                        else u2row[0:1, :]),
                                  rhs=(edgeL_row[0:1, :] if c0 == 0
                                       else edgeR_row[0:1, :]),
                                  start=False, stop=True))
                mAm = kb.mk("tensor")
                for gi2 in range(4):
                    ti = 4 * g + gi2
                    kb.op("vector", lambda e, gi2=gi2, ti=ti, aslots=aslots:
                          e.tensor_reduce(
                              out=negm_seg[:, ti:ti + 1], in_=aslots[gi2],
                              axis=AX.X, op=ALU.max, negate=True),
                          wait=[mAm] if gi2 == 0 else ())
                mAv = kb.mk("vector")
                for gi2 in range(4):
                    ti = 4 * g + gi2
                    kb.op("scalar", lambda e, gi2=gi2, ti=ti, aslots=aslots:
                          e.activation(
                              out=A_bf[:, ADIS:ADIS + 256], in_=aslots[gi2],
                              func=AF.Exp, bias=negm_seg[:, ti:ti + 1],
                              accum_out=s_seg[:, ti:ti + 1]),
                          wait=[mAv] if gi2 == 0 else ())
                    kb.op("scalar", lambda e: e.drain())
                mAe[g] = kb.mk("scalar")
            kb.op("scalar", lambda e: e.activation(out=s_seg[:], in_=s_seg[:],
                                                   func=AF.Ln))
            mN1 = kb.mk("scalar")
            kb.op("vector", lambda e: e.tensor_tensor(
                out=negm_seg[:], in0=negm_seg[:], in1=s_seg[:],
                op=ALU.subtract), wait=[mN1])
            mN2 = kb.mk("vector")
            kb.op("tensor", lambda e: e.transpose(
                out=pb0[0:32, 0:128], in_=negm_seg[:, 0:32], identity=identf[:]),
                wait=[mN2])
            mN3 = kb.mk("tensor")
            kb.op("scalar", lambda e: e.copy(out=nmlT[:], in_=pb0[0:32, 0:128]),
                  wait=[mN3])
            mN4 = kb.mk("scalar")
            kb.op("vector", lambda e: e.tensor_copy(out=nmlT_h[:], in_=nmlT[:]),
                  wait=[mN4])
            kb.op("vector", lambda e: e.drain())
            kb.op("vector", lambda e: e.tensor_tensor(
                out=nmlT_l[:], in0=nmlT[:], in1=nmlT_h[:], op=ALU.subtract))
            mN5 = kb.mk("vector")
            kb.dma("sync", "gnm" + tg, nmlb_dram[0:1, :].rearrange(
                "r (p j) -> (r p) j", p=32), nmlT_h[:], wait=[mN5])
            kb.dma("sync", "gnm" + tg, nmlb_dram[1:2, :].rearrange(
                "r (p j) -> (r p) j", p=32), nmlT_l[:])
            wnm = kb.gv("gnm" + tg)
            kb.dma("sync", "gnl" + tg, qseg[64:66, 0:4096], nmlb_dram[:, :],
                   wait=[wnm])
            wnl = kb.gv("gnl" + tg)
            kb.wt("tensor", mAe[7], wnl)
            for h2 in range(4):
                kb.op("tensor", lambda e, h2=h2: e.matmul(
                    out=obank[h2], lhsT=onesrow_b[0:1, 0:128],
                    rhs=zrow_b[0:1, 0:512], start=True, stop=False))
            mTe = {}
            mPVm = {}
            tiles = [(h, kcl) for h in range(8) for kcl in range(5)]

            def emit_pv(i):
                h, kcl = tiles[i]
                q0 = max(0, 128 * kcl - 128)
                q1 = min(512, 128 * kcl + 128)
                nq = q1 - q0
                ptb = A_bf[:, PT_O + 256 * (i % 2):PT_O + 256 * (i % 2) + 256]
                ptbh = A_bf[:, 17920 + 256 * (i % 2):
                            17920 + 256 * (i % 2) + 256]
                po = q0 - (128 * kcl - 128)
                ob = obank[h // 2]
                orow = 64 * (h % 2)
                kb.wt("tensor", mTe[i])
                npv = 0
                for (jr0, pc0, pc1) in ((0, 0, 192), (64, 64, 256)):
                    if seg == 0 and kcl == 0 and jr0 == 0:
                        continue
                    if seg == 3 and kcl == 4 and jr0 == 64:
                        continue
                    c0 = max(pc0, po) - po
                    c1 = min(pc1, po + nq) - po
                    if c1 <= c0:
                        continue
                    vs = vseg if jr0 == 0 else vseg_hi
                    pt_ = ptb if jr0 == 0 else ptbh
                    kb.op("tensor", lambda e, h=h, kcl=kcl, c0=c0, c1=c1,
                          q0=q0, ob=ob, orow=orow, vs=vs, pt_=pt_: e.matmul(
                        out=ob[orow:orow + 64, q0 + c0:q0 + c1],
                        lhsT=vs[0:64, 512 * kcl + 64 * h:
                                512 * kcl + 64 * h + 64],
                        rhs=pt_[0:64, c0:c1],
                        start=False, stop=False))
                    npv += 1
                mPVm[i] = kb.mk("tensor") if npv else mTe[i]

            for i, (h, kcl) in enumerate(tiles):
                q0 = max(0, 128 * kcl - 128)
                q1 = min(512, 128 * kcl + 128)
                nq = q1 - q0
                tp = (pb0 if i % 2 == 0 else pb1)
                kb.wt("tensor", mTe.get(i - 2))
                kb.op("tensor", lambda e, h=h, kcl=kcl, q0=q0, nq=nq, tp=tp:
                      e.matmul(
                          out=tp[:, 0:nq],
                          lhsT=kseg[0:66, 640 * h + 128 * kcl:
                                    640 * h + 128 * kcl + 128],
                          rhs=qseg[0:66, 512 * h + q0:512 * h + q0 + nq],
                          start=True, stop=True))
                mTm = kb.mk("tensor")
                ptb = A_bf[:, PT_O + 256 * (i % 2):PT_O + 256 * (i % 2) + 256]
                ptbh = A_bf[:, 17920 + 256 * (i % 2):
                            17920 + 256 * (i % 2) + 256]
                kb.op("scalar", lambda e, tp=tp, nq=nq, ptb=ptb: e.activation(
                    out=ptb[0:64, 0:nq], in_=tp[0:64, 0:nq], func=AF.Exp),
                    wait=[w for w in (mTm, mPVm.get(i - 2)) if w is not None])
                kb.op("scalar", lambda e, tp=tp, nq=nq, ptbh=ptbh: e.activation(
                    out=ptbh[0:64, 0:nq], in_=tp[64:128, 0:nq], func=AF.Exp))
                mTe[i] = kb.mk("scalar")
                if i > 0:
                    emit_pv(i - 1)
            emit_pv(len(tiles) - 1)
            for h2 in range(4):
                kb.op("tensor", lambda e, h2=h2: e.matmul(
                    out=obank[h2], lhsT=onesrow_b[0:1, 0:128],
                    rhs=zrow_b[0:1, 0:512], start=False, stop=True))
            mClose = kb.mk("tensor")
            for h2 in range(4):
                kb.op("scalar", lambda e, h2=h2: e.copy(
                    out=ofm[:, 512 * h2:512 * h2 + 512], in_=obank[h2]),
                    wait=[mClose] if h2 == 0 else ())
            mOe = kb.mk("scalar")
            mOm = {}
            for m in range(NCH):
                ps = pb0 if m % 2 == 0 else pb1
                kb.wt("tensor", mOe if m == 0 else None, mOutV.get(m % 2))
                for c in range(4):
                    kb.op("tensor", lambda e, m=m, c=c, ps=ps: e.matmul(
                        out=ps[:, 0:512],
                        lhsT=wslc(WOUTL_O, c, 1024, 128, 128 * m),
                        rhs=ofm[:, 512 * c:512 * c + 512],
                        start=(c == 0), stop=(c == 3)))
                mOm[m] = kb.mk("tensor")
                kb.op("vector", lambda e, m=m, t0=t0, ps=ps:
                      e.scalar_tensor_tensor(
                          out=xch(m)[:, t0:t0 + 512], in0=ps[:, 0:512],
                          scalar=0.5, in1=xch(m)[:, t0:t0 + 512],
                          op0=ALU.mult, op1=ALU.add), wait=[mOm[m]])
                mOutV[m % 2] = kb.mk("vector")
            mSeg = mOm[NCH - 1]
            kb.bar(("tensor", "vector", "scalar"))
        kb.bar(("tensor", "vector", "scalar"))

    xq_dram = dram("xq_scr", (512, D), BF16, kind="Internal")
    xkv_dram = dram("xkv_scr", (1024, D), BF16, kind="Internal")
    rms_c = kb.sb("rms_c", (128, 12), F32)
    lntkv = kb.sb("lntkv", (128, 8), F32)
    sq5 = kb.sb("sq5", (128, 4), F32)
    negm_h = negm_seg
    s_h = s_seg

    def ffn_heavy(call):
        tg = "hv%d" % call
        QH, KH, VH, PT0, PT1 = 0, 4096, 12352, 16448, 16960
        wsq, wsk = route_w[call]
        for (row0, ntile, base) in ((0, 4, 0), (528, 8, 4)):
            for g in range(ntile):
                kb.dma("sync", "gli" + tg,
                       gld_f[:, 4 * (base + g):4 * (base + g) + 4],
                       idx_dram[row0 + 128 * g: row0 + 128 * (g + 1), :],
                       wait=(wsq + wsk) if (row0 == 0 and g == 0) else ())
        w_li = kb.gv("gli" + tg)
        kb.plan["vector"].append(
            lambda e, w=w_li: e.wait_ge(kb.sem(w[0]), w[1]))
        for g in range(12):
            kb.op("vector", lambda e, g=g: e.tensor_scalar(
                out=gld_f[:, 4 * g:4 * g + 1], in0=gld_f[:, 4 * g:4 * g + 1],
                scalar1=0.0, scalar2=2047.0, op0=ALU.max, op1=ALU.min))
        kb.op("vector", lambda e: e.drain())
        for g in range(12):
            kb.op("vector", lambda e, g=g: e.tensor_copy(
                out=gidx[:, g:g + 1], in_=gld_f[:, 4 * g:4 * g + 1]))
            kb.op("vector", lambda e, g=g: e.tensor_copy(
                out=gmeta[:, 2 * g:2 * g + 2], in_=gld_f[:, 4 * g + 1:4 * g + 3]))
        kb.op("vector", lambda e: e.drain())
        mGc = kb.mk("vector")
        kb.plan["gpsimd"].append(lambda e, w=mGc: e.wait_ge(
            kb.sem(w[0]), w[1]))
        wlt = kb.wv("bar_tensor")
        load_w("gw2" + tg, WQ_O, wq_d, 8, 512, wait=[wlt])
        load_w("gw2" + tg, WKV_O, wkv_d, 8, 1024)
        ld_w2 = load_w("gw2" + tg, WOUTH_O, wouth_d, 4, 1024)
        kb.plan["tensor"].append(lambda e, w=ld_w2: e.wait_ge(kb.sem(w[0]), w[1]))
        XTMP0, XTMP1, XQF, XKVF, OH = 0, 1024, 2048, 6144, 14336
        qh = A_bf[0:66, QH:QH + 4096]
        kh = A_bf[0:66, KH:KH + 8256]
        vh = A_bf[:, VH:VH + 4096]
        for g in range(16):
            kb.dma("sync", "ghp" + tg, hs_dram[128 * g:128 * (g + 1), :],
                   nullq_bh[:])
        w_hp = kb.gv("ghp" + tg)
        for g in range(12):
            tmp = xc_sb[:, (XTMP0 if g % 2 == 0 else XTMP1):
                        (XTMP0 if g % 2 == 0 else XTMP1) + 1024]
            wpr = [kb.gv("ghg%d" % (g % 2) + tg)] if g >= 2 else []
            kb.op("gpsimd", lambda e, g=g, tmp=tmp: e.indirect_dma_start(
                out=tmp, out_offset=None, in_=y_dram,
                in_offset=IOA(gidx[:, g:g + 1], 0)),
                wait=wpr, inc="g_gig%d" % (g % 2) + tg, incn=16)
            wg = kb.gv("gig%d" % (g % 2) + tg)
            kb.op("vector", lambda e, g=g, tmp=tmp: e.tensor_scalar(
                out=tmp, in0=tmp, scalar1=gmeta[:, 2 * g + 1:2 * g + 2],
                scalar2=None, op0=ALU.add), wait=[wg])
            mGv1 = kb.mk("vector")
            kb.op("scalar", lambda e, g=g, tmp=tmp: e.activation(
                out=xc_sb[:, OH:OH + 1024], in_=tmp, func=AF.Square,
                accum_out=rms_c[:, g:g + 1]), wait=[mGv1])
            kb.op("scalar", lambda e: e.drain())
            kb.op("scalar", lambda e, g=g: e.activation(
                out=rms_c[:, g:g + 1], in_=rms_c[:, g:g + 1], func=AF.Ln))
            kb.op("scalar", lambda e: e.drain())
            kb.op("scalar", lambda e, g=g: e.activation(
                out=rms_c[:, g:g + 1], in_=rms_c[:, g:g + 1], func=AF.Exp,
                scale=0.5))
            mGs = kb.mk("scalar")
            kb.op("vector", lambda e, g=g: e.tensor_scalar_max(
                out=rms_c[:, g:g + 1], in0=rms_c[:, g:g + 1], scalar1=1e-12),
                wait=[mGs])
            kb.op("vector", lambda e: e.drain())
            kb.op("vector", lambda e, g=g: e.reciprocal(
                out=rms_c[:, g:g + 1], in_=rms_c[:, g:g + 1]))
            kb.op("vector", lambda e: e.drain())
            kb.op("vector", lambda e, g=g, tmp=tmp: e.tensor_scalar(
                out=tmp, in0=tmp, scalar1=rms_c[:, g:g + 1], scalar2=32.0,
                op0=ALU.mult, op1=ALU.mult))
            mGv2 = kb.mk("vector")
            kb.plan["sync"].append(lambda e, w=mGv2: e.wait_ge(
                kb.sem(w[0]), w[1]))
            if g < 4:
                kb.dma("sync", "ghg%d" % (g % 2) + tg,
                       xq_dram[128 * g:128 * (g + 1), :], tmp)
            else:
                kb.dma("sync", "ghg%d" % (g % 2) + tg,
                       xkv_dram[128 * (g - 4):128 * (g - 3), :], tmp)
        w_g0 = kb.gv("ghg0" + tg)
        w_g1 = kb.gv("ghg1" + tg)
        for c in range(NCH):
            kb.dmat("ghq" + tg, xc_sb[:, XQF + 512 * c:XQF + 512 * (c + 1)],
                    xq_dram[:, 128 * c:128 * (c + 1)],
                    wait=[w_g0, w_g1] if c == 0 else ())
            kb.dmat("ghk" + tg, xc_sb[:, XKVF + 1024 * c:XKVF + 1024 * (c + 1)],
                    xkv_dram[:, 128 * c:128 * (c + 1)])
        w_tq = kb.gv("ghq" + tg)
        w_tk = kb.gv("ghk" + tg)
        kb.plan["vector"].append(lambda e, w=w_tq: e.wait_ge(kb.sem(w[0]), w[1]))
        kb.plan["vector"].append(lambda e, w=w_tk: e.wait_ge(kb.sem(w[0]), w[1]))
        for c in range(NCH):
            kb.op("vector", lambda e, c=c: e.tensor_scalar_mul(
                out=xc_sb[:, XQF + 512 * c:XQF + 512 * (c + 1)],
                in0=xc_sb[:, XQF + 512 * c:XQF + 512 * (c + 1)],
                scalar1=cols[:, 32 + c:33 + c]))
            kb.op("vector", lambda e, c=c: e.tensor_scalar_mul(
                out=xc_sb[:, XKVF + 1024 * c:XKVF + 1024 * (c + 1)],
                in0=xc_sb[:, XKVF + 1024 * c:XKVF + 1024 * (c + 1)],
                scalar1=cols[:, 32 + c:33 + c]))
        kb.bar(("vector",))
        kb.op("vector", lambda e: e.memset(A_bf[64:66, KH:KH + 8256], 1.0))
        kb.bar(("vector",))
        hgi = 0
        mHm = {}
        mHe = {}
        hpb = lambda g: (pb0 if g % 2 == 0 else pb1)
        for m in range(4):
            ps = hpb(hgi)
            kb.wt("tensor", mHe.get(hgi - 2))
            for c in range(NCH):
                kb.op("tensor", lambda e, m=m, c=c, ps=ps: e.matmul(
                    out=ps[:, 0:512],
                    lhsT=wslc(WQ_O, c, 512, 128, 128 * m),
                    rhs=xc_sb[:, XQF + 512 * c:XQF + 512 * (c + 1)],
                    start=(c == 0), stop=(c == 7)))
            mHm[hgi] = kb.mk("tensor")
            for half in range(2):
                kb.op("scalar", lambda e, m=m, half=half, ps=ps: e.activation(
                    out=A_bf[0:64, QH + 512 * (2 * m + half):
                             QH + 512 * (2 * m + half) + 512],
                    in_=ps[64 * half:64 * half + 64, 0:512],
                    func=AF.Identity, scale=0.125),
                    wait=[mHm[hgi]] if half == 0 else ())
            mHe[hgi] = kb.mk("scalar")
            hgi += 1
        for km in range(4):
            for nt in range(2):
                ps = hpb(hgi)
                kb.wt("tensor", mHe.get(hgi - 2))
                for c in range(NCH):
                    base = WKV_O + 1024 * c + 128 * km
                    kb.op("tensor", lambda e, km=km, nt=nt, c=c, base=base,
                          ps=ps: e.matmul(
                        out=ps[:, 0:512],
                        lhsT=w_arena[:, base:base + 128],
                        rhs=xc_sb[:, XKVF + 1024 * c + 512 * nt:
                                  XKVF + 1024 * c + 512 * (nt + 1)],
                        start=(c == 0), stop=(c == 7)))
                mHm[hgi] = kb.mk("tensor")
                for half in range(2):
                    kb.op("scalar", lambda e, km=km, nt=nt, half=half, ps=ps:
                          e.activation(
                        out=A_bf[0:64, KH + 1032 * (2 * km + half) + 512 * nt:
                                 KH + 1032 * (2 * km + half) + 512 * (nt + 1)],
                        in_=ps[64 * half:64 * half + 64, 0:512],
                        func=AF.Identity),
                        wait=[mHm[hgi]] if half == 0 else ())
                mHe[hgi] = kb.mk("scalar")
                hgi += 1
        for h in range(8):
            kb.op("vector", lambda e, h=h: e.tensor_copy(
                out=A_bf[0:64, KH + 1032 * h + 1024:KH + 1032 * h + 1025],
                in_=nullcol_b[0:64, h:h + 1]))
        kb.bar(("vector",))
        for kc in range(8):
            ps = hpb(hgi)
            kb.wt("tensor", mHe.get(hgi - 2))
            for c in range(NCH):
                base = WKV_O + 1024 * c + 512
                kb.op("tensor", lambda e, kc=kc, c=c, base=base, ps=ps: e.matmul(
                    out=ps[:, 0:512],
                    lhsT=xc_sb[:, XKVF + 1024 * c + 128 * kc:
                               XKVF + 1024 * c + 128 * (kc + 1)],
                    rhs=w_arena[:, base:base + 512],
                    start=(c == 0), stop=(c == 7)))
            mHm[hgi] = kb.mk("tensor")
            kb.op("scalar", lambda e, kc=kc, ps=ps: e.copy(
                out=A_bf[:, VH + 512 * kc:VH + 512 * (kc + 1)],
                in_=ps[:, 0:512]), wait=[mHm[hgi]])
            mHe[hgi] = kb.mk("scalar")
            hgi += 1
        kb.bar(("tensor", "vector", "scalar"))
        mAs = None
        for h in range(8):
            for qt in range(4):
                ti = 4 * h + qt
                kb.wt("tensor", mAs, mHe.get(hgi - 1) if ti == 0 else None,
                      mHe.get(hgi - 2) if ti == 0 else None)
                for ks, n in ((0, 512), (512, 512), (1024, 1)):
                    kb.op("tensor", lambda e, h=h, qt=qt, ks=ks, n=n: e.matmul(
                        out=pA[:, ks:ks + n],
                        lhsT=A_bf[0:64, QH + 512 * h + 128 * qt:
                                  QH + 512 * h + 128 * qt + 128],
                        rhs=A_bf[0:64, KH + 1032 * h + ks:KH + 1032 * h + ks + n],
                        start=True, stop=True))
                mAt = kb.mk("tensor")
                kb.op("vector", lambda e, ti=ti: e.tensor_reduce(
                    out=negm_h[:, ti:ti + 1], in_=pA[:, 0:1025], axis=AX.X,
                    op=ALU.max, negate=True), wait=[mAt])
                mAv = kb.mk("vector")
                kb.op("scalar", lambda e, ti=ti: e.activation(
                    out=xc_sb[:, OH:OH + 1025], in_=pA[:, 0:1025],
                    func=AF.Exp, bias=negm_h[:, ti:ti + 1],
                    accum_out=s_h[:, ti:ti + 1]), wait=[mAv])
                kb.op("scalar", lambda e: e.drain())
                mAs = kb.mk("scalar")
        kb.op("scalar", lambda e: e.activation(out=s_h[:], in_=s_h[:], func=AF.Ln))
        kb.bar(("scalar",))
        kb.op("vector", lambda e: e.tensor_tensor(
            out=negm_h[:], in0=negm_h[:], in1=s_h[:], op=ALU.subtract))
        kb.bar(("vector",))
        kb.op("tensor", lambda e: e.transpose(
            out=pb0[0:32, 0:128], in_=negm_h[:, 0:32], identity=identf[:]))
        kb.bar(("tensor",))
        kb.op("scalar", lambda e: e.copy(out=nmlT[:], in_=pb0[0:32, 0:128]))
        kb.bar(("scalar",))
        kb.op("vector", lambda e: e.tensor_copy(out=nmlT_h[:], in_=nmlT[:]))
        kb.op("vector", lambda e: e.drain())
        kb.op("vector", lambda e: e.tensor_tensor(
            out=nmlT_l[:], in0=nmlT[:], in1=nmlT_h[:], op=ALU.subtract))
        kb.bar(("vector",))
        kb.dma("sync", "gnh" + tg, nmlb_dram[0:1, :].rearrange(
            "r (p j) -> (r p) j", p=32), nmlT_h[:])
        kb.dma("sync", "gnh" + tg, nmlb_dram[1:2, :].rearrange(
            "r (p j) -> (r p) j", p=32), nmlT_l[:])
        wnm = kb.gv("gnh" + tg)
        kb.dma("sync", "gnk" + tg, A_bf[64:66, QH:QH + 4096], nmlb_dram[:, :],
               wait=[wnm])
        wnl = kb.gv("gnk" + tg)
        kb.plan["tensor"].append(lambda e, w=wnl: e.wait_ge(kb.sem(w[0]), w[1]))
        for kc in range(8):
            kb.op("scalar", lambda e, kc=kc: e.activation(
                out=lntkv[:, kc:kc + 1], in_=gmeta[:, 2 * (4 + kc):2 * (4 + kc) + 1],
                func=AF.Ln))
        kb.bar(("scalar",))
        obank2 = (pb0, pb1, pb2, pb3)
        tilesH = [(h, kc) for h in range(8) for kc in range(9)]
        mHTe = {}
        mHPV = {}

        def h_emit_pv(i):
            h, kc = tilesH[i]
            ob = obank2[h // 2]
            orow = 64 * (h % 2)
            ptb = A_bf[:, (PT0 if i % 2 == 0 else PT1):
                       (PT0 if i % 2 == 0 else PT1) + 512]
            kb.wt("tensor", mHTe[i])
            if kc < 8:
                kb.op("tensor", lambda e, h=h, kc=kc, ob=ob, orow=orow,
                      ptb=ptb: e.matmul(
                    out=ob[orow:orow + 64, 0:512],
                    lhsT=A_bf[:, VH + 512 * kc + 64 * h:
                              VH + 512 * kc + 64 * h + 64],
                    rhs=ptb[0:128, 0:512],
                    start=(kc == 0), stop=False))
            else:
                kb.op("tensor", lambda e, h=h, ob=ob, orow=orow, ptb=ptb:
                      e.matmul(
                    out=ob[orow:orow + 64, 0:512],
                    lhsT=nullvrow_b[0:1, 64 * h:64 * h + 64],
                    rhs=ptb[0:1, 0:512],
                    start=False, stop=True))
            mHPV[i] = kb.mk("tensor")

        for i, (h, kc) in enumerate(tilesH):
            n = 1 if kc == 8 else 128
            tp = pA[:, 0:512] if i % 2 == 0 else pA[:, 512:1024]
            kb.wt("tensor", mHTe.get(i - 2))
            kb.op("tensor", lambda e, h=h, kc=kc, n=n, tp=tp: e.matmul(
                out=tp[0:n, 0:512],
                lhsT=A_bf[0:66, KH + 1032 * h + 128 * kc:
                          KH + 1032 * h + 128 * kc + n],
                rhs=A_bf[0:66, QH + 512 * h:QH + 512 * h + 512],
                start=True, stop=True))
            mHTm = kb.mk("tensor")
            ptb = A_bf[:, (PT0 if i % 2 == 0 else PT1):
                       (PT0 if i % 2 == 0 else PT1) + 512]
            kb.op("scalar", lambda e, kc=kc, n=n, tp=tp, ptb=ptb: e.activation(
                out=ptb[0:n, 0:512], in_=tp[0:n, 0:512], func=AF.Exp,
                bias=(lntkv[0:n, kc:kc + 1] if kc < 8 else fconst[0:1, 2:3])),
                wait=[w for w in (mHTm, mHPV.get(i - 2)) if w is not None])
            mHTe[i] = kb.mk("scalar")
            if i > 0:
                h_emit_pv(i - 1)
        h_emit_pv(len(tilesH) - 1)
        mHlast = mHPV[len(tilesH) - 1]
        kb.bar(("tensor", "vector", "scalar"))
        for h2 in range(4):
            kb.op("scalar", lambda e, h2=h2: e.copy(
                out=xc_sb[:, OH + 512 * h2:OH + 512 * (h2 + 1)],
                in_=obank2[h2][:, 0:512]),
                wait=[mHlast] if h2 == 0 else ())
        kb.bar(("scalar",))
        kb.op("vector", lambda e: e.tensor_scalar(
            out=sq5[:], in0=gmeta[:, 0:8].rearrange("p (g k) -> p g k", k=2)[:, :, 0],
            scalar1=0.5, scalar2=None, op0=ALU.mult))
        kb.bar(("vector",))
        for tc in range(4):
            for nt in range(2):
                for c2 in range(4):
                    kb.op("tensor", lambda e, tc=tc, nt=nt, c2=c2: e.matmul(
                        out=pA[:, 0:512],
                        lhsT=xc_sb[:, OH + 512 * c2 + 128 * tc:
                                   OH + 512 * c2 + 128 * tc + 128],
                        rhs=wslc(WOUTH_O, c2, 1024, 512, 512 * nt),
                        start=(c2 == 0), stop=(c2 == 3)))
                kb.bar(("tensor",))
                kb.op("scalar", lambda e, tc=tc, nt=nt: e.activation(
                    out=A_bf[:, 6144 + 1024 * tc + 512 * nt:
                             6144 + 1024 * tc + 512 * (nt + 1)],
                    in_=pA[:, 0:512], func=AF.Identity,
                    scale=sq5[:, tc:tc + 1]))
                kb.bar(("scalar",))
        kb.plan["gpsimd"].append(lambda e, w=w_hp: e.wait_ge(kb.sem(w[0]), w[1]))
        for tc in range(4):
            kb.op("gpsimd", lambda e, tc=tc: e.indirect_dma_start(
                out=hs_dram, out_offset=IOA(gidx[:, tc:tc + 1], 0),
                in_=A_bf[:, 6144 + 1024 * tc:6144 + 1024 * (tc + 1)],
                in_offset=None), inc="g_ghs" + tg, incn=16)
        w_hs = kb.gv("ghs" + tg)
        for c in range(NCH):
            buf = xc_sb[:, 2048 + 2048 * (c % 2):2048 + 2048 * (c % 2) + 2048]
            wt = [w_hs] if c < 2 else [kb.gv("ghb%d" % (c % 2) + tg)]
            kb.dmat("ghb%d" % (c % 2) + tg, buf, hs_dram[:, 128 * c:128 * (c + 1)],
                    wait=wt)
            wldb = kb.gv("ghb%d" % (c % 2) + tg)
            kb.op("vector", lambda e, c=c, buf=buf: e.tensor_tensor(
                out=xch(c), in0=xch(c), in1=buf, op=ALU.add), wait=[wldb],
                inc="hsadd")
            kb.plan["sync"].append(
                lambda e, w=kb.wv("hsadd"): e.wait_ge(kb.sem(w[0]), w[1]))
        kb.bar(("vector",))

    um_dram = dram("um_scr", (D, T), BF16, kind="Internal")
    g_dram = dram("g_scr", (INNER, T), BF16, kind="Internal")
    xs_dram = dram("xs_scr", (T, INNER), BF16, kind="Internal")
    m8c_d = dram("m8c_in", (8, 4))
    binr_d = dram("binr_in", (1, 2304))
    cmask_d = dram("cmask_in", (128, 128))
    Bfm = xc_sb[0:64, 640:2688]
    Cfm = xc_sb[0:64, 2688:4736]
    Btm = xc_sb[0:128, 4736:5760]
    Hst = xc_sb[0:64, 5760:7808]
    Dident = xc_sb[0:128, 7808:8832]
    binr_b = xc_sb[0:1, 8832:11136]
    bmx_row = xc_sb[0:1, 11136:13440]
    lt01b = xc_sb[0:128, 13440:13568]
    mr0 = lambda c0=0, n=T: repf[32:40, c0:c0 + n]
    mr1 = lambda c0=0, n=T: repf[64:72, c0:c0 + n]
    P8 = lambda c0=0, n=T: repf[0:8, c0:c0 + n]
    E8 = lambda c0=0, n=T: repb[0:8, c0:c0 + n]
    mcol = srt_q[:, 0:12]
    bias_mz = srt_kv[:, 0:16]
    bias_mx = gmeta[:, 0:18]
    dsk8 = cols[:, 86:94]
    kb.dma("sync", "gmc", cols[0:8, 82:86], m8c_d)
    kb.dma("sync", "gmc", cols[32:40, 82:86], m8c_d)
    ld_mc = kb.gv("gmc")

    def mamba():
        if DEBUG:
            for _e in ("vector", "scalar", "tensor"):
                kb.plan[_e].append(lambda e, w=kb.gv("gdbgr"): e.wait_ge(
                    kb.sem(w[0]), w[1]))
        w_rep2 = bcast_row(SL_INV, 2, repb[:], "mi",
                           wait_extra=[kb.wv("bar_vector")])
        for c in range(NCH):
            kb.op("vector", lambda e, c=c: e.tensor_tensor(
                out=A_bf[:, 0:2048], in0=xcch(c), in1=repb[:], op=ALU.mult),
                wait=[w_rep2] if c == 0 else ())
            kb.op("vector", lambda e: e.drain())
            kb.op("vector", lambda e, c=c: e.tensor_tensor(
                out=xch(c), in0=xch(c), in1=A_bf[:, 0:2048], op=ALU.add))
            kb.op("vector", lambda e: e.drain())
        kb.bar(("vector",))
        kb.op("vector", lambda e: e.tensor_tensor(out=statr(SL_T0), in0=statr(SL_INV),
                                                  in1=statr(SL_INV), op=ALU.mult))
        kb.op("vector", lambda e: e.drain())
        kb.op("vector", lambda e: e.tensor_tensor(out=statr(SL_T0), in0=statr(SL_T0),
                                                  in1=statr(SL_VAR), op=ALU.mult))
        kb.bar(("vector",))
        kb.op("scalar", lambda e: e.activation(out=statr(SL_T1), in_=statr(SL_T0),
                                               func=AF.Ln, bias=fconst[0:4, 0:1]))
        kb.op("scalar", lambda e: e.drain())
        kb.op("scalar", lambda e: e.activation(out=statr(SL_T1), in_=statr(SL_T1),
                                               func=AF.Exp, scale=-0.5))
        kb.bar(("scalar",))
        kb.op("vector", lambda e: e.tensor_tensor(out=statr(SL_S2), in0=statr(SL_INV),
                                                  in1=statr(SL_T1), op=ALU.mult))
        kb.bar(("vector",))
        w_rep3 = bcast_row(SL_S2, 3, repb[:], "ms",
                           wait_extra=[kb.wv("bar_vector")])
        for c in range(NCH):
            kb.op("vector", lambda e, c=c: e.scalar_tensor_tensor(
                out=xcch(c), in0=xcch(c), scalar=cols[:, 16 + c:17 + c],
                in1=repb[:], op0=ALU.mult, op1=ALU.mult),
                wait=[w_rep3] if c == 0 else ())
        kb.bar(("vector",))
        xc_r2 = xc_sb[:].rearrange("p (c t) -> p c t", c=NCH)
        um_r = um_dram.rearrange("(c p) t -> p c t", p=128)
        for i in range(2):
            kb.dma("sync", "gum", um_r[:, 4 * i:4 * i + 4, :],
                   xc_r2[:, 4 * i:4 * i + 4, :], wait=[kb.wv("bar_vector")])
        w_um = kb.gv("gum")
        kb.dma("gpsimd", "gmcb", binr_b, binr_d, wait=[w_um])
        kb.dma("gpsimd", "gmcb", lt01b, cmask_d)
        ld_mcb = kb.gv("gmcb")
        wmt = kb.wv("bar_tensor")
        win_r = win_d.rearrange("(c p) f -> p c f", p=128)
        kb.dma("gpsimd", "gwz",
               w_arena[:, 0:16384].rearrange("p (c f) -> p c f", c=8),
               win_r[:, :, 0:2048], wait=[wmt])
        ld_wz = kb.gv("gwz")
        kb.plan["tensor"].append(lambda e, w=ld_wz: e.wait_ge(kb.sem(w[0]), w[1]))
        for m in range(16):
            for c in range(NCH):
                kb.op("tensor", lambda e, m=m, c=c: e.matmul(
                    out=pb0[0:128, m:m + 1], lhsT=wslc(0, c, 2048, 128, 128 * m),
                    rhs=colsb[:, 24 + c:25 + c], start=(c == 0), stop=(c == 7)))
        kb.bar(("tensor",))
        kb.op("vector", lambda e: e.tensor_tensor(
            out=bias_mz, in0=pb0[0:128, 0:16], in1=cols[:, 40:56], op=ALU.add))
        kb.bar(("vector",))
        mZe = {}
        wZd = {}
        for tt in range(NTT):
            ub = A_bf[:, (0 if tt % 2 == 0 else 4096):(0 if tt % 2 == 0 else 4096) + 4096]
            wpr = ([kb.gv("gub%d" % (tt % 2))] if tt >= 2 else [w_um]) + \
                [kb.wv("bar_tensor"), kb.wv("bar_scalar")]
            kb.dma("gpsimd", "gub%d" % (tt % 2),
                   ub.rearrange("p (c t) -> p c t", c=8),
                   um_r[:, :, 512 * tt:512 * (tt + 1)], wait=wpr)
            wub = kb.gv("gub%d" % (tt % 2))
            kb.plan["tensor"].append(lambda e, w=wub: e.wait_ge(kb.sem(w[0]), w[1]))
            for m in range(16):
                zgi = 16 * tt + m
                zp = pb1 if zgi % 2 == 0 else pb2
                zb = A_bf[:, 8192 + 512 * (zgi % 2):8192 + 512 * (zgi % 2) + 512]
                kb.wt("tensor", mZe.get(zgi - 2))
                for c in range(NCH):
                    kb.op("tensor", lambda e, m=m, c=c, ub=ub, zp=zp: e.matmul(
                        out=zp[:, 0:512], lhsT=wslc(0, c, 2048, 128, 128 * m),
                        rhs=ub[:, 512 * c:512 * (c + 1)],
                        start=(c == 0), stop=(c == 7)))
                mZm = kb.mk("tensor")
                kb.op("scalar", lambda e, m=m, zp=zp, zb=zb: e.activation(
                    out=zb, in_=zp[:, 0:512],
                    func=AF.Silu, bias=bias_mz[:, m:m + 1]),
                    wait=[w for w in (mZm, wZd.get(zgi - 2)) if w is not None])
                mZe[zgi] = kb.mk("scalar")
                kb.plan["sync"].append(lambda e, w=mZe[zgi]: e.wait_ge(
                    kb.sem(w[0]), w[1]))
                kb.dma("sync", "ggst%d" % (zgi % 2),
                       g_dram[128 * m:128 * (m + 1), 512 * tt:512 * (tt + 1)],
                       zb)
                wZd[zgi] = kb.gv("ggst%d" % (zgi % 2))
        kb.bar(("tensor", "vector", "scalar"))
        w_g = (kb.gv("ggst0"), kb.gv("ggst1"))
        if os.environ.get("ATH_MK_MM") == "1":
            kb.bar(("tensor", "vector", "scalar"))
            return
        wmt2 = kb.wv("bar_tensor")
        kb.dma("gpsimd", "gwx",
               w_arena[:, 0:18432].rearrange("p (c f) -> p c f", c=8),
               win_r[:, :, 2048:4352], wait=[wmt2])
        ld_wx = kb.gv("gwx")
        kb.plan["tensor"].append(lambda e, w=ld_wx: e.wait_ge(kb.sem(w[0]), w[1]))
        kb.plan["tensor"].append(lambda e, w=ld_mcb: e.wait_ge(kb.sem(w[0]), w[1]))
        kb.wt("tensor", mZe.get(62), mZe.get(63))
        for m in (16, 17):
            for c in range(NCH):
                kb.op("tensor", lambda e, m=m, c=c: e.matmul(
                    out=pb0[0:128, m - 16:m - 15],
                    lhsT=wslc(0, c, 2304, 128, 128 * m),
                    rhs=colsb[:, 24 + c:25 + c], start=(c == 0), stop=(c == 7)))
        kb.bar(("tensor",))
        kb.op("vector", lambda e: e.tensor_tensor(
            out=bias_mx[:, 16:18], in0=pb0[0:128, 0:2], in1=cols[:, 72:74],
            op=ALU.add))
        kb.bar(("vector",))
        for n in range(4):
            for c in range(NCH):
                kb.op("tensor", lambda e, n=n, c=c: e.matmul(
                    out=pb2[0:1, 0:512], lhsT=colsb[:, 24 + c:25 + c],
                    rhs=wslc(0, c, 2304, 512, 512 * n), start=(c == 0), stop=False))
            kb.op("tensor", lambda e, n=n: e.matmul(
                out=pb2[0:1, 0:512], lhsT=onesrow_b[0:1, 0:1],
                rhs=binr_b[0:1, 512 * n:512 * (n + 1)], start=False, stop=True))
            kb.bar(("tensor",))
            kb.op("scalar", lambda e, n=n: e.copy(
                out=bmx_row[0:1, 512 * n:512 * (n + 1)], in_=pb2[0:1, 0:512]),
                wait=[w_um] if n == 0 else ())
            kb.bar(("scalar",))
        kb.op("scalar", lambda e: e.activation(out=mcol[32:40, 0:1],
                                               in_=cols[32:40, 82:83], func=AF.Exp))
        kb.bar(("scalar",))
        kb.op("vector", lambda e: e.tensor_scalar_mul(out=mcol[32:40, 1:2],
                                                      in0=mcol[32:40, 0:1],
                                                      scalar1=-1.0))
        kb.bar(("vector",))
        mXe = {}
        wXd = {-2: wZd.get(62), -1: wZd.get(63)}
        mXbc = {}
        for tt in range(NTT):
            ub = A_bf[:, (0 if tt % 2 == 0 else 4096):(0 if tt % 2 == 0 else 4096) + 4096]
            wpr = ([kb.gv("gux%d" % (tt % 2))] if tt >= 2 else [w_um]) + \
                [kb.wv("bar_tensor"), kb.wv("bar_scalar")]
            kb.dma("gpsimd", "gux%d" % (tt % 2),
                   ub.rearrange("p (c t) -> p c t", c=8),
                   um_r[:, :, 512 * tt:512 * (tt + 1)], wait=wpr)
            wub = kb.gv("gux%d" % (tt % 2))
            kb.plan["tensor"].append(lambda e, w=wub: e.wait_ge(kb.sem(w[0]), w[1]))
            for tc4 in range(4):
                for n in range(4):
                    xgi = 16 * tt + 4 * tc4 + n
                    xp = pb0 if xgi % 2 == 0 else pb3
                    xb = A_bf[:, 8192 + 512 * (xgi % 2):
                              8192 + 512 * (xgi % 2) + 512]
                    kb.wt("tensor", mXe.get(xgi - 2))
                    for c in range(NCH):
                        kb.op("tensor", lambda e, tc4=tc4, n=n, c=c, ub=ub,
                              xp=xp: e.matmul(
                            out=xp[:, 0:512],
                            lhsT=ub[:, 512 * c + 128 * tc4:
                                    512 * c + 128 * tc4 + 128],
                            rhs=wslc(0, c, 2304, 512, 512 * n),
                            start=(c == 0), stop=False))
                    kb.op("tensor", lambda e, n=n, xp=xp: e.matmul(
                        out=xp[:, 0:512], lhsT=onesrow_b[0:1, :],
                        rhs=bmx_row[0:1, 512 * n:512 * (n + 1)],
                        start=False, stop=True))
                    mXm = kb.mk("tensor")
                    kb.op("scalar", lambda e, xp=xp, xb=xb: e.copy(
                        out=xb, in_=xp[:, 0:512]),
                        wait=[w for w in (mXm, wXd.get(xgi - 2))
                              if w is not None])
                    mXe[xgi] = kb.mk("scalar")
                    kb.plan["sync"].append(lambda e, w=mXe[xgi]: e.wait_ge(
                        kb.sem(w[0]), w[1]))
                    kb.dma("sync", "gxst%d" % (xgi % 2),
                           xs_dram[512 * tt + 128 * tc4:
                                   512 * tt + 128 * (tc4 + 1),
                                   512 * n:512 * (n + 1)], xb)
                    wXd[xgi] = kb.gv("gxst%d" % (xgi % 2))
            for mm2 in (16, 17):
                kb.wt("tensor", mXbc.get((tt, mm2 - 1)),
                      mXbc.get((tt - 1, 17)) if mm2 == 16 else None)
                for c in range(NCH):
                    kb.op("tensor", lambda e, mm2=mm2, c=c, ub=ub: e.matmul(
                        out=pb1[:, 0:512], lhsT=wslc(0, c, 2304, 128, 128 * mm2),
                        rhs=ub[:, 512 * c:512 * (c + 1)],
                        start=(c == 0), stop=(c == 7)))
                mXt = kb.mk("tensor")
                if mm2 == 16:
                    kb.op("scalar", lambda e, tt=tt: e.activation(
                        out=Bfm[:, 512 * tt:512 * (tt + 1)], in_=pb1[0:64, 0:512],
                        func=AF.Identity, bias=bias_mx[0:64, 16:17]),
                        wait=[mXt])
                    kb.op("scalar", lambda e, tt=tt: e.activation(
                        out=Cfm[:, 512 * tt:512 * (tt + 1)], in_=pb1[64:128, 0:512],
                        func=AF.Identity, bias=bias_mx[64:128, 16:17]))
                else:
                    kb.op("scalar", lambda e, tt=tt: e.copy(
                        out=mr0(512 * tt, 512), in_=pb1[0:8, 0:512]), wait=[mXt])
                mXbc[(tt, mm2)] = kb.mk("scalar")
            kb.bar(("tensor", "vector", "scalar"))
        for tc in range(16):
            kb.op("tensor", lambda e, tc=tc: e.transpose(
                out=pT[:, 0:64], in_=Bfm[:, 128 * tc:128 * (tc + 1)],
                identity=identb[0:64, 0:64]))
            kb.bar(("tensor",))
            kb.op("scalar", lambda e, tc=tc: e.copy(
                out=Btm[:, 64 * tc:64 * (tc + 1)], in_=pT[:, 0:64]))
            kb.bar(("scalar",))
        if os.environ.get("ATH_MK_MM") == "2":
            kb.bar(("tensor", "vector", "scalar"))
            return
        kb.op("vector", lambda e: e.tensor_tensor(
            out=mcol[0:8, 2:3], in0=bias_mx[0:8, 17:18], in1=cols[0:8, 84:85],
            op=ALU.add))
        kb.bar(("vector",))
        kb.op("scalar", lambda e: e.copy(out=pb3[0:8, 256:257],
                                         in_=mcol[0:8, 2:3]))
        kb.op("scalar", lambda e: e.drain())
        kb.op("scalar", lambda e: e.copy(out=mcol[32:40, 2:3],
                                         in_=pb3[0:8, 256:257]))
        kb.bar(("scalar",))
        kb.op("vector", lambda e: e.tensor_scalar(
            out=mr0(), in0=mr0(), scalar1=mcol[32:40, 2:3], scalar2=None,
            op0=ALU.add))
        kb.bar(("vector",))
        for i4 in range(4):
            PP = pb3[0:8, 0:512]
            kb.op("scalar", lambda e, i4=i4: e.activation(
                out=PP, in_=mr0(512 * i4, 512), func=AF.Abs))
            kb.op("scalar", lambda e: e.drain())
            kb.op("scalar", lambda e: e.activation(out=PP, in_=PP, func=AF.Exp,
                                                   scale=-1.0))
            kb.op("scalar", lambda e: e.drain())
            kb.op("scalar", lambda e: e.activation(out=PP, in_=PP, func=AF.Ln,
                                                   bias=fconst[0:8, 1:2]))
            kb.bar(("scalar",))
            kb.op("vector", lambda e, i4=i4: e.tensor_scalar_max(
                out=mr0(512 * i4, 512), in0=mr0(512 * i4, 512), scalar1=0.0))
            kb.op("vector", lambda e: e.drain())
            kb.op("vector", lambda e, i4=i4: e.tensor_tensor(
                out=mr0(512 * i4, 512), in0=mr0(512 * i4, 512), in1=PP,
                op=ALU.add))
            kb.bar(("vector",))
        for cc2 in range(NTC):
            kb.op("scalar", lambda e, cc2=cc2: e.activation(
                out=pb3[0:8, 0:128], in_=mr0(128 * cc2, 128), func=AF.Ln))
            kb.bar(("scalar",))
            kb.op("vector", lambda e, cc2=cc2: e.tensor_scalar_mul(
                out=pb3[0:8, 128:256], in0=mr0(128 * cc2, 128),
                scalar1=mcol[32:40, 1:2]))
            kb.op("vector", lambda e: e.drain())
            kb.op("vector", lambda e, cc2=cc2: e.tensor_tensor_scan(
                out=P8(128 * cc2, 128), data0=pb3[0:8, 128:256],
                data1=nmlT[0:8, :], initial=0.0, op0=ALU.add, op1=ALU.bypass))
            kb.op("vector", lambda e: e.drain())
            kb.op("vector", lambda e, cc2=cc2: e.tensor_tensor(
                out=nmlT[0:8, :], in0=pb3[0:8, 0:128], in1=P8(128 * cc2, 128),
                op=ALU.subtract))
            kb.bar(("vector",))
            kb.op("tensor", lambda e, cc2=cc2: e.transpose(
                out=pA[0:128, 8 * cc2:8 * cc2 + 8], in_=nmlT[0:8, :],
                identity=identf[0:8, 0:8]))
            kb.bar(("tensor",))
        kb.op("scalar", lambda e: e.copy(out=sk_sc[:, 0:128],
                                         in_=pA[0:128, 0:128]))
        kb.bar(("scalar",))
        kb.op("tensor", lambda e: e.matmul(
            out=pb0[0:128, 0:8], lhsT=onesrow_f[0:1, :],
            rhs=smallrows[0:1, 8:16], start=True, stop=True))
        kb.bar(("tensor",))
        kb.op("vector", lambda e: e.tensor_copy(out=dsk8, in_=pb0[0:128, 0:8]))
        kb.bar(("vector",))
        for h in range(8):
            kb.op("vector", lambda e, h=h: e.tensor_scalar_mul(
                out=Dident[:, 128 * h:128 * (h + 1)], in0=identb[:],
                scalar1=dsk8[:, h:h + 1]), wait=[w_um] if h == 0 else ())
        kb.op("vector", lambda e: e.drain())
        kb.op("vector", lambda e: e.memset(Hst, 0.0))
        kb.bar(("vector",))
        if os.environ.get("ATH_MK_MM") == "3":
            kb.bar(("tensor", "vector", "scalar"))
            return
        wmt3 = kb.wv("bar_tensor")
        kb.dma("gpsimd", "gwo",
               w_arena[:, 0:16384].rearrange("p (c f) -> p c f", c=16),
               wout_d.rearrange("(c p) f -> p c f", p=128), wait=[wmt3])
        ld_wo = kb.gv("gwo")
        M2O, EO, CMEO, WXSO, EREPO, GMO = 16384, 16640, 16896, 17152, 17664, 17920
        m2b = lambda h: A_bf[:, M2O + 128 * (h % 2):M2O + 128 * (h % 2) + 128]
        eb = lambda h: A_bf[:, EO + 128 * (h % 2):EO + 128 * (h % 2) + 128]
        cmeb = lambda h: A_bf[0:64, CMEO + 128 * (h % 2):CMEO + 128 * (h % 2) + 128]
        wxb = lambda h: A_bf[:, WXSO + 256 * (h % 2):WXSO + 256 * (h % 2) + 256]
        erb = lambda h: A_bf[0:64, EREPO + 128 * (h % 2):EREPO + 128 * (h % 2) + 128]
        skrow = lambda h: (sk_row if h % 2 == 0 else sk_row2)
        dslot = lambda h: (0 if h % 2 == 0 else 384)
        wargc = lambda h: mcol[:, (9 if h % 2 == 0 else 3):(10 if h % 2 == 0 else 4)]
        wcolc = lambda h: mcol[:, (10 if h % 2 == 0 else 4):(11 if h % 2 == 0 else 5)]
        gamc = lambda h: mcol[:, (11 if h % 2 == 0 else 5):(12 if h % 2 == 0 else 6)]
        mS1 = {}; mS2 = {}; mVa = {}; mVm = {}; mVw = {}; mVy = {}
        mVh = {}; mTy = {}; mTx = {}
        mVh_prev = {}; mTo_last = None; mVo = {}; mGm = None
        hist = {}
        w_xst0, w_xst1 = kb.gv("gxst0"), kb.gv("gxst1")
        wbt0, wbv0 = kb.wv("bar_tensor"), kb.wv("bar_vector")
        SCN = int(os.environ.get("ATH_MK_SC", "16"))
        SBAR = int(os.environ.get("ATH_MK_SBAR", "2"))
        for cc in range(SCN):
            par = cc % 2
            h2 = hist.get(cc - 2, (None, None, None))
            wx = ([kb.gv("gxs%d" % par), h2[0], h2[1]]
                  if cc >= 2 else [w_xst0, w_xst1, wbt0, wbv0])
            kb.dma("gpsimd", "gxs%d" % par, A_bf[:, 2048 * par:2048 * (par + 1)],
                   xs_dram[128 * cc:128 * (cc + 1), :],
                   wait=[w for w in wx if w is not None])
            wxs_ = kb.gv("gxs%d" % par)
            wg2 = ([kb.gv("ggb%d" % par), h2[2]]
                   if cc >= 2 else [*w_g, wbt0, wbv0])
            kb.dma("gpsimd", "ggb%d" % par,
                   A_bf[:, 4096 + 2048 * par:4096 + 2048 * (par + 1)].rearrange(
                       "p (c t) -> p c t", c=16),
                   g_dram[:, 128 * cc:128 * (cc + 1)].rearrange(
                       "(c p) t -> p c t", p=128),
                   wait=[w for w in wg2 if w is not None])
            wgb = kb.gv("ggb%d" % par)
            xsb = A_bf[:, 2048 * par:2048 * (par + 1)]
            gb = A_bf[:, 4096 + 2048 * par:4096 + 2048 * (par + 1)]
            kb.wt("tensor", mGm)
            kb.op("tensor", lambda e, cc=cc: e.matmul(
                out=pb0[:, 0:128], lhsT=Bfm[:, 128 * cc:128 * (cc + 1)],
                rhs=Cfm[:, 128 * cc:128 * (cc + 1)], start=True, stop=True))
            mGt = kb.mk("tensor")
            kb.op("vector", lambda e: e.tensor_tensor(
                out=A_bf[:, GMO:GMO + 128], in0=pb0[:, 0:128], in1=lt01b,
                op=ALU.mult), wait=[mGt])
            mGm = kb.mk("vector")
            kb.wt("tensor", mS1.get(6), mS1.get(7))
            kb.op("tensor", lambda e, cc=cc: e.matmul(
                out=pb3[0:1, 384:512], lhsT=identf[0:8, 0:1],
                rhs=repf[0:8, 128 * cc:128 * (cc + 1)], start=True, stop=True))
            mT0 = kb.mk("tensor")
            kb.op("scalar", lambda e: e.copy(out=sk_row[0:1, :],
                                             in_=pb3[0:1, 384:512]),
                  wait=[mT0])
            mS0 = kb.mk("scalar")
            mS1.clear(); mS2.clear(); mVa.clear(); mVm.clear(); mVw.clear()
            mVh_prev = dict(mVh); mVy.clear(); mVh.clear(); mTy.clear()
            mTx.clear()
            mS1[-1] = mS0
            if SBAR >= 2:
                kb.bar(("tensor", "vector", "scalar"))
            for h in range(8):
                if SBAR >= 1:
                    kb.bar(("tensor", "vector", "scalar"))
                bcol = sk_sc[:, 8 * cc + h:8 * cc + h + 1]
                kb.wt("tensor", mS1.get(h - 2), mVa.get(h - 1), mS2.get(h - 1),
                      mS1.get(h - 1))
                kb.op("tensor", lambda e, h=h: e.matmul(
                    out=pb1[:, dslot(h):dslot(h) + 128], lhsT=onesrow_f[0:1, :],
                    rhs=skrow(h)[0:1, :], start=True, stop=True))
                kb.op("tensor", lambda e, h=h: e.matmul(
                    out=pb1[0:64, 128:256], lhsT=onesrow_f[0:1, 0:64],
                    rhs=skrow(h)[0:1, :], start=True, stop=True))
                kb.op("tensor", lambda e, h=h: e.matmul(
                    out=pb1[:, 256:257], lhsT=onesrow_f[0:1, :],
                    rhs=skrow(h)[0:1, 127:128], start=True, stop=True))
                if h < 7:
                    kb.op("tensor", lambda e, h=h, cc=cc: e.matmul(
                        out=pb3[0:1, 384:512], lhsT=identf[0:8, h + 1:h + 2],
                        rhs=repf[0:8, 128 * cc:128 * (cc + 1)],
                        start=True, stop=True))
                mT1 = kb.mk("tensor")
                kb.op("vector", lambda e, h=h, bcol=bcol: e.tensor_scalar(
                    out=pb1[:, dslot(h):dslot(h) + 128],
                    in0=pb1[:, dslot(h):dslot(h) + 128], scalar1=bcol,
                    scalar2=20.0, op0=ALU.add, op1=ALU.min),
                    wait=[mT1])
                mTmin = kb.mk("vector")
                if h < 7:
                    kb.op("scalar", lambda e, h=h: e.copy(
                        out=skrow(h + 1)[0:1, :], in_=pb3[0:1, 384:512]),
                        wait=[mT1])
                kb.op("scalar", lambda e, h=h: e.activation(
                    out=eb(h), in_=pb1[:, dslot(h):dslot(h) + 128], func=AF.Exp),
                    wait=[mTmin])
                kb.op("scalar", lambda e, h=h: e.activation(
                    out=erb(h), in_=pb1[0:64, 128:256], func=AF.Exp))
                mS1[h] = kb.mk("scalar")
                kb.op("vector", lambda e, h=h, bcol=bcol: e.tensor_tensor(
                    out=wargc(h), in0=pb1[:, 256:257], in1=bcol, op=ALU.add),
                    wait=[mT1])
                mVa[h] = kb.mk("vector")
                kb.op("scalar", lambda e, h=h: e.activation(
                    out=wcolc(h), in_=wargc(h), func=AF.Exp), wait=[mVa[h]])
                kb.op("scalar", lambda e, h=h: e.activation(
                    out=gamc(h), in_=pb1[:, 256:257], func=AF.Exp))
                mS2[h] = kb.mk("scalar")
                kb.op("vector", lambda e, h=h, cc=cc: e.tensor_tensor(
                    out=cmeb(h), in0=Cfm[:, 128 * cc:128 * (cc + 1)],
                    in1=erb(h), op=ALU.mult), wait=[mS1[h]])
                kb.op("vector", lambda e, h=h: e.tensor_tensor(
                    out=m2b(h), in0=A_bf[:, GMO:GMO + 128], in1=eb(h),
                    op=ALU.mult))
                kb.op("vector", lambda e: e.drain())
                kb.op("vector", lambda e, h=h: e.tensor_tensor(
                    out=m2b(h), in0=m2b(h), in1=Dident[:, 128 * h:128 * (h + 1)],
                    op=ALU.add))
                mVm[h] = kb.mk("vector")
                kb.op("vector", lambda e, h=h, xsb=xsb: e.tensor_scalar_mul(
                    out=wxb(h), in0=xsb[:, 256 * h:256 * (h + 1)],
                    scalar1=wcolc(h)),
                    wait=[w for w in ([mS2[h], mTx.get(h - 2)] +
                                      ([wxs_] if h == 0 else []))
                          if w is not None])
                mVw[h] = kb.mk("vector")
                kb.wt("tensor", mVm[h], mVy.get(h - 1), mVh_prev.get(h))
                if h == 0:
                    kb.wt("tensor", wxs_)
                for dh2 in range(2):
                    kb.op("tensor", lambda e, h=h, dh2=dh2, xsb=xsb: e.matmul(
                        out=(pb2 if dh2 == 0 else pb3)[:, 0:128],
                        lhsT=xsb[:, 256 * h + 128 * dh2:256 * h + 128 * (dh2 + 1)],
                        rhs=m2b(h), start=True, stop=False))
                    kb.op("tensor", lambda e, h=h, dh2=dh2: e.matmul(
                        out=(pb2 if dh2 == 0 else pb3)[:, 0:128],
                        lhsT=Hst[:, 256 * h + 128 * dh2:256 * h + 128 * (dh2 + 1)],
                        rhs=cmeb(h), start=False, stop=True))
                mTy[h] = kb.mk("tensor")
                kb.wt("tensor", mVw[h], mVh.get(h - 1))
                kb.op("tensor", lambda e, h=h, cc=cc: e.matmul(
                    out=pA[0:64, 0:256], lhsT=Btm[:, 64 * cc:64 * (cc + 1)],
                    rhs=wxb(h), start=True, stop=True))
                mTx[h] = kb.mk("tensor")
                ygw = [mTy[h]] + \
                    ([wgb, w_xst0, w_xst1] if h == 0 else []) + \
                    ([mTo_last] if mTo_last is not None and h == 0 else [])
                for dh2 in range(2):
                    ygo = 8192 + 512 * (2 * h + dh2) + 128 * (cc % 4)
                    kb.op("vector", lambda e, h=h, dh2=dh2, ygo=ygo, gb=gb:
                          e.tensor_tensor(
                              out=A_bf[:, ygo:ygo + 128],
                              in0=(pb2 if dh2 == 0 else pb3)[:, 0:128],
                              in1=gb[:, 128 * (2 * h + dh2):
                                     128 * (2 * h + dh2 + 1)],
                              op=ALU.mult),
                          wait=ygw if dh2 == 0 else ())
                mVy[h] = kb.mk("vector")
                kb.op("vector", lambda e, h=h: e.scalar_tensor_tensor(
                    out=Hst[:, 256 * h:256 * (h + 1)],
                    in0=Hst[:, 256 * h:256 * (h + 1)], scalar=gamc(h)[0:64, :],
                    in1=pA[0:64, 0:256], op0=ALU.mult, op1=ALU.add),
                    wait=[mTx[h]])
                mVh[h] = kb.mk("vector")
            hist[cc] = (mTx[7], mVw[7], mVy[7])
            if cc % 4 == 3:
                kb.wt("tensor", ld_wo, mVy[7], mVo.get(6), mVo.get(7))
                tt4 = cc // 4
                mTo = {}
                for m in range(NCH):
                    op_ps = pA[:, 512:1024] if m % 2 == 0 else pA[:, 1024:1536]
                    kb.wt("tensor", mVo.get(m - 2))
                    for kc in range(16):
                        kb.op("tensor", lambda e, m=m, kc=kc, op_ps=op_ps:
                              e.matmul(
                                  out=op_ps,
                                  lhsT=wslc(0, kc, 1024, 128, 128 * m),
                                  rhs=A_bf[:, 8192 + 512 * kc:
                                           8192 + 512 * (kc + 1)],
                                  start=(kc == 0), stop=(kc == 15)))
                    mTo[m] = kb.mk("tensor")
                    kb.op("vector", lambda e, m=m, tt4=tt4, op_ps=op_ps:
                          e.scalar_tensor_tensor(
                              out=xch(m)[:, 512 * tt4:512 * (tt4 + 1)],
                              in0=op_ps, scalar=cols[:, 74 + m:75 + m],
                              in1=xch(m)[:, 512 * tt4:512 * (tt4 + 1)],
                              op0=ALU.add, op1=ALU.add), wait=[mTo[m]])
                    mVo[m] = kb.mk("vector")
                mTo_last = mTo[NCH - 1]
        kb.bar(("tensor", "vector", "scalar"))

    for _eng in ("tensor", "vector", "scalar", "gpsimd"):
        for _w in (ld_x, ld_const, ld_cb, ld_mc):
            kb.plan[_eng].append(lambda e, w=_w: e.wait_ge(kb.sem(w[0]), w[1]))
    STAGE = int(os.environ.get("ATH_MK_STAGE", "99"))
    boundary_stats()
    if STAGE >= 2:
        setup_bias_qkv()
        w_y = ffn_ustage(1)
    if STAGE >= 3:
        ffn_route(1)
    if STAGE >= 4:
        ffn_light(1)
    if STAGE >= 5:
        ffn_heavy(1)
    if STAGE >= 6:
        boundary_stats()
        mamba()

    if DEBUG:
        dum = kb.debug_out("um", (D, T), BF16)
        dg = kb.debug_out("g", (INNER, T), BF16)
        dxs = kb.debug_out("xs", (T, INNER), BF16)
        dp8 = kb.debug_out("p8", (8, T))
        dbfm = kb.debug_out("bfm", (64, T), BF16)
        dcfm = kb.debug_out("cfm", (64, T), BF16)
        kb.dma("gpsimd", "gdbg3", dum, um_dram)
        kb.dma("gpsimd", "gdbg3", dg, g_dram)
        kb.dma("gpsimd", "gdbg3", dxs, xs_dram)
        kb.dma("sync", "gdbg4", dp8, repf[0:8, :], wait=[kb.wv("bar_vector")])
        kb.dma("sync", "gdbg4", dbfm, Bfm)
        kb.dma("sync", "gdbg4", dcfm, Cfm)
        for _e in ("vector", "scalar", "tensor", "gpsimd"):
            for _g in ("gdbg3", "gdbg4"):
                kb.plan[_e].append(lambda e, w=kb.gv(_g): e.wait_ge(
                    kb.sem(w[0]), w[1]))

    if STAGE >= 7:
        boundary_stats()
        _wl3 = kb.wv("bar_tensor")
        load_w("gw1b", WQKV_O, wqkv_d, 8, 1536, wait=[_wl3])
        ld_w1b = load_w("gw1b", WOUTL_O, woutl_d, 4, 1024)
        kb.plan["tensor"].append(lambda e, w=ld_w1b: e.wait_ge(kb.sem(w[0]), w[1]))
        ffn_ustage(3)
        ffn_route(3)
    if STAGE >= 8:
        ffn_light(3)
    if STAGE >= 9:
        ffn_heavy(3)

    if DEBUG:
        dmu = kb.debug_out("mu", (4, 512))
        dvar = kb.debug_out("var", (4, 512))
        dinv = kb.debug_out("inv", (4, 512))
        dxc = kb.debug_out("xc", (128, NCH * T), BF16)
        dul = kb.debug_out("ul", (128, NCH * T), BF16)
        dy = kb.debug_out("y", (T, D), BF16)
        didx = kb.debug_out("idx", (1664, 4))
        kb.dma("sync", "gdbg", dmu, statr(SL_MU))
        kb.dma("sync", "gdbg", dvar, statr(SL_VAR))
        kb.dma("sync", "gdbg", dinv, statr(SL_INV))
        kb.dma("sync", "gdbg", dxc, xc_sb[:])
        kb.dma("sync", "gdbg", dul, A_bf[:, 0:16384])
        kb.dma("gpsimd", "gdbg2", dy, y_dram, wait=w_y)
        kb.dma("gpsimd", "gdbg2", didx, idx_dram)

    kb.bar(("vector", "scalar", "tensor"))
    wfin = [kb.wv("bar_vector"), kb.wv("bar_scalar"), kb.wv("bar_tensor")]
    out_r = out_d.rearrange("(c p) t -> p c t", p=128)
    for i in range(4):
        kb.dma("sync", "gout", out_r[:, 2 * i:2 * i + 2, :],
               x_r[:, 2 * i:2 * i + 2, :], wait=wfin)
    fin = [kb.gv("gout")] + ([kb.gv("gdbg"), kb.gv("gdbg2"), kb.gv("gdbg3"),
                              kb.gv("gdbg4")] if DEBUG else [])
    kb.op("sync", lambda e: e.nop(), wait=fin)
    return kb


def make_consts():
    c = {}
    c["ident"] = np.eye(128, dtype=np.float32)
    c["onesrow"] = np.ones((1, 128), np.float32)
    e4 = np.zeros((128, 16), np.float32)
    for tt in range(4):
        e4[:, 4 * tt + tt] = 1.0
    c["e4"] = e4
    ind2 = np.zeros((128, 2), np.float32)
    ind2[0:64, 0] = 1.0
    ind2[64:128, 1] = 1.0
    c["ind2"] = ind2
    c["ind2t"] = ind2.T.copy()
    iota = (np.arange(16)[None, :] * 128 + np.arange(128)[:, None]).astype(np.float32)
    c["iota_t_in"] = iota
    u1 = np.zeros((2, 128), np.float32)
    u1[0, 0:64] = 1.0
    u1[1, 64:128] = 1.0
    c["maskA_l_in"] = u1
    vA = np.zeros((2, 256), np.float32)
    vA[0, 192:256] = -1e9
    vA[1, 0:64] = -1e9
    c["maskA_r_in"] = vA
    c["maskT_r_in"] = vA.copy()
    c["logk_in"] = np.log(np.array([[512 * 9.0 / 8.0, 1024 * 9.0 / 8.0]], np.float32))
    pc = np.zeros((128, 2), np.float32)
    pc[0:64, 0] = -1e9
    pc[64:128, 1] = -1e9
    c["padcolT_in"] = pc
    fc = np.zeros((128, 4), np.float32)
    fc[:, 0] = 1e-5
    fc[:, 1] = 1.0
    fc[:, 3] = 1e-12
    c["fconst"] = fc
    c["iota2_in"] = (np.arange(128)[:, None] * 16 + np.arange(16)[None, :]
                     ).astype(np.float32)
    c["jmat"] = np.broadcast_to(np.arange(16, dtype=np.float32), (128, 16)).copy()
    c["neg1e9"] = np.full((1, 256), -1e9, np.float32)
    eg = np.zeros((2, 256), np.float32)
    eg[0, 0:64] = -1e9
    eg[1, 192:256] = -1e9
    c["edgeLR"] = eg
    lk = np.log(np.array([576.0, 1152.0], np.float32)).astype(np.float32)
    c["skinit"] = (lk - np.float32(np.log(np.float32(2048.0)))).reshape(1, 2)
    s_i = np.arange(128)[:, None]
    t_i = np.arange(128)[None, :]
    c["cmask_in"] = np.where(s_i <= t_i, 1.0, 0.0).astype(np.float32)
    return c


def kernel(L0_attn_input_tensor, L0_attn_ln_gamma, L0_attn_ln_beta, L0_attn_W_in,
           L0_attn_b_in, L0_attn_A_log, L0_attn_D_skip, L0_attn_dt_bias,
           L0_attn_W_out, L0_attn_b_out, L0_ffn_ln_w, L0_ffn_ln_b, L0_ffn_Wqkv,
           L0_ffn_Wout_l, L0_ffn_rt_q, L0_ffn_rt_kv, L0_ffn_gamma, L0_ffn_Wq,
           L0_ffn_Wkv, L0_ffn_Wout_h, L0_ffn_null_kv, L0_ffn_null_q):
    f32 = lambda a: np.ascontiguousarray(np.asarray(a), dtype=np.float32)
    x = f32(L0_attn_input_tensor)
    colchunk = lambda v: np.ascontiguousarray(f32(v).reshape(-1, 128).T)
    shared = {
        "wqkv": np.ascontiguousarray(f32(L0_ffn_Wqkv).T),
        "woutl": np.ascontiguousarray(f32(L0_ffn_Wout_l).T),
        "wouth": np.ascontiguousarray(f32(L0_ffn_Wout_h).T),
        "wq": np.ascontiguousarray(f32(L0_ffn_Wq).T),
        "wkv": np.ascontiguousarray(
            f32(L0_ffn_Wkv).T.reshape(1024, 8, 2, 64).transpose(0, 2, 1, 3)
            .reshape(1024, 1024)),
        "win": np.ascontiguousarray(np.pad(f32(L0_attn_W_in), ((0, 0), (0, 120)))),
        "wout": f32(L0_attn_W_out),
        "lnw_c": colchunk(L0_ffn_ln_w),
        "lnb_c": colchunk(L0_ffn_ln_b),
        "mgam_c": colchunk(L0_attn_ln_gamma),
        "mbet_c": colchunk(L0_attn_ln_beta),
        "hgam_c": colchunk(L0_ffn_gamma),
        "bin_c": colchunk(np.pad(f32(L0_attn_b_in), (0, 120))),
        "bout_c": colchunk(L0_attn_b_out),
        "alog": f32(L0_attn_A_log).reshape(1, 8),
        "dskip": f32(L0_attn_D_skip).reshape(1, 8),
        "dtbias": f32(L0_attn_dt_bias).reshape(1, 8),
        "m8c_in": np.ascontiguousarray(np.stack(
            [f32(L0_attn_A_log), f32(L0_attn_D_skip), f32(L0_attn_dt_bias),
             np.zeros(8, np.float32)], axis=1)),
        "binr_in": np.ascontiguousarray(
            np.pad(f32(L0_attn_b_in), (0, 120))[2048:4352].reshape(1, 2304)),
        "rt2": np.ascontiguousarray(
            np.stack([f32(L0_ffn_rt_q).ravel(), f32(L0_ffn_rt_kv).ravel()], axis=1)),
        "nullqrep_in": np.ascontiguousarray(
            np.broadcast_to(f32(L0_ffn_null_q), (128, D))),
        "nullvrow_in": f32(L0_ffn_null_kv)[1].reshape(1, 512),
    }
    nk = f32(L0_ffn_null_kv)
    nullcol = np.zeros((D,), np.float32)
    for h in range(8):
        nullcol[128 * h:128 * h + 64] = nk[0, h]
        nullcol[128 * h + 64:128 * h + 128] = nk[1, h]
    shared["nullcol_c"] = colchunk(nullcol)
    shared.update(make_consts())

    kb = build()
    nc = kb.finish()
    in_maps = []
    for b in range(8):
        m = dict(shared)
        m["x0"] = np.ascontiguousarray(x[b].T)
        in_maps.append(m)
    res = run_bass_kernel_spmd(nc, in_maps, core_ids=list(range(8)))
    global _last_results
    _last_results = res.results[0]
    out = np.stack([np.ascontiguousarray(r["out"].T) for r in res.results], axis=0)
    return out


_last_results = None
